# version 03
# baseline (speedup 1.0000x reference)
import numpy as np
from contextlib import ExitStack

import concourse.bass as bass
import concourse.mybir as mybir
from concourse.bass_utils import run_bass_kernel_spmd

F32 = mybir.dt.float32
BF16 = mybir.dt.bfloat16
AF = mybir.ActivationFunctionType
OP = mybir.AluOpType
AX_X = mybir.AxisListType.X

CORES = 8
D = 1024
S = 16384
TOK = S // CORES
TC = 512
NCHUNK = TOK // TC
FF = 4096
FI = FF // 128
TB = 512
NQ = 4
EPS = 1e-5


class P:

    ENGS = ("sync", "gpsimd", "vector", "scalar", "tensor")

    def __init__(self, nc, es):
        self.nc = nc
        self.ops = {e: [] for e in self.ENGS}
        self.sems = {}
        self.cnt = {}
        for name in ("sg", "ss", "sv", "sc", "st", "scc", "so"):
            self.sems[name] = es.enter_context(nc.semaphore(name))
            self.cnt[name] = 0

    def add(self, eng, fn, waits=(), inc=None, n=1):
        ws = tuple(w for w in waits if w is not None and w[1] > 0)
        self.ops[eng].append((fn, ws, inc, n))
        if inc is not None:
            self.cnt[inc] += n
            return (inc, self.cnt[inc])
        return None

    def tok(self, name):
        return (name, self.cnt[name])

    def _run(self, name, eng):
        seen = {}
        for fn, waits, inc, n in self.ops[name]:
            for (s, v) in waits:
                if seen.get(s, 0) < v:
                    eng.wait_ge(self.sems[s], v)
                    seen[s] = v
            instr = fn(eng)
            if inc is not None:
                instr.then_inc(self.sems[inc], n)

    def emit(self, block):
        runner = self._run

        @block.sync
        def _(e):
            runner("sync", e)

        @block.gpsimd
        def _(e):
            runner("gpsimd", e)

        @block.vector
        def _(e):
            runner("vector", e)

        @block.scalar
        def _(e):
            runner("scalar", e)

        @block.tensor
        def _(e):
            runner("tensor", e)


def build_nc():
    nc = bass.Bass("TRN2", target_bir_lowering=False)

    xT = nc.dram_tensor("xT", [D, TOK], F32, kind="ExternalInput").ap()
    w_in = nc.dram_tensor("w_in", [D, D], F32, kind="ExternalInput").ap()
    w_gx = nc.dram_tensor("w_gx", [D, D], F32, kind="ExternalInput").ap()
    w_ga = nc.dram_tensor("w_ga", [D, D], F32, kind="ExternalInput").ap()
    w_out = nc.dram_tensor("w_out", [D, D], F32, kind="ExternalInput").ap()
    smalls_h = nc.dram_tensor("smalls_h", [128, 112], F32, kind="ExternalInput").ap()
    w1T = nc.dram_tensor("w1T", [D, FF], F32, kind="ExternalInput").ap()
    w2T = nc.dram_tensor("w2T", [FF, D], F32, kind="ExternalInput").ap()
    sel8_h = nc.dram_tensor("sel8_h", [8, 128], F32, kind="ExternalInput").ap()
    idn_h = nc.dram_tensor("idn_h", [128, 128], F32, kind="ExternalInput").ap()
    out_d = nc.dram_tensor("out", [TOK, D], F32, kind="ExternalOutput").ap()

    ag1_in = nc.dram_tensor("ag1_in", [2, D], F32)
    ag1_out = nc.dram_tensor("ag1_out", [2 * CORES, D], F32)
    ag2_in = nc.dram_tensor("ag2_in", [D + 8, TOK], BF16)
    ag2_out = nc.dram_tensor("ag2_out", [(D + 8) * CORES, TOK], BF16)
    rs_in = [nc.dram_tensor(f"rs_in_{q}", [CORES * TC, D], BF16) for q in range(NQ)]
    rs_out = [nc.dram_tensor(f"rs_out_{q}", [TC, D], BF16) for q in range(NQ)]

    es = ExitStack()

    AW1 = es.enter_context(nc.sbuf_tensor("AW1", [128, 32768], BF16))
    AW2 = es.enter_context(nc.sbuf_tensor("AW2", [128, 32768], BF16))
    AX = es.enter_context(nc.sbuf_tensor("AX", [128, 8192], BF16))
    AB = es.enter_context(nc.sbuf_tensor("AB", [128, 14336], BF16))
    AFr = es.enter_context(nc.sbuf_tensor("AF", [128, 8192], F32))
    AFb = AFr.ap().bitcast(BF16)
    ABf = AB.ap().bitcast(F32)

    SML = es.enter_context(nc.sbuf_tensor("SML", [128, 112], F32))
    sm = {}
    sm["b_in"] = SML[:, 0:8]
    sm["bgx_hf"] = SML[:, 8:16]
    sm["bga_hf"] = SML[:, 16:24]
    sm["b_out"] = SML[:, 24:32]
    sm["aptmp"] = SML[:, 32:40]
    sm["selc"] = SML[:, 40:48]
    sm["gwtf"] = SML[:, 48:112]
    for nm, shp, dt in [
        ("msp_h", [128, 8], F32),
        ("hcar", [128, 8], F32), ("apcar", [128, 8], F32),
        ("cin", [128, 8], F32), ("tmpc", [128, 8], F32), ("dmy", [128, 8], F32),
        ("aesum", [128, 128], F32), ("cpre", [128, 64], F32),
        ("sel8", [8, 128], BF16), ("idn", [128, 128], F32),
        ("onesc", [128, 1], BF16), ("onesf", [1, 128], F32),
        ("glt4", [128, 32], F32), ("mk4", [128, 32], F32),
        ("ms4", [128, 32], F32), ("comb4", [128, 32], F32),
        ("st12", [128, 24], F32), ("mv2", [128, 4], F32), ("rstd2", [128, 2], F32),
        ("epsc", [128, 1], F32), ("onec", [128, 1], F32), ("spt", [128, 8], F32),
    ]:
        sm[nm] = es.enter_context(nc.sbuf_tensor(nm, shp, dt))
    sm["asum"] = sm["aesum"][:, 0:64]
    sm["esum"] = sm["aesum"][:, 64:128]
    sm["c24"] = sm["mk4"]
    sm["m14"] = sm["st12"][:, 0:4]
    sm["m24"] = sm["st12"][:, 4:8]
    sm["dd4"] = sm["st12"][:, 8:12]
    sm["th4"] = sm["st12"][:, 12:16]
    sm["g14"] = sm["st12"][:, 16:20]
    sm["g24"] = sm["st12"][:, 20:24]

    PA0 = es.enter_context(nc.psum_tensor("PA0", [128, 512], F32))
    PA1 = es.enter_context(nc.psum_tensor("PA1", [128, 512], F32))
    PA2 = es.enter_context(nc.psum_tensor("PA2", [128, 512], F32))
    PSP = es.enter_context(nc.psum_tensor("PSP", [128, 512], F32))
    PY0 = es.enter_context(nc.psum_tensor("PY0", [128, 1024], F32))
    PY1 = es.enter_context(nc.psum_tensor("PY1", [128, 1024], F32))

    p = P(nc, es)
    T0 = ("sv", 0)

    def aw1_w(base, ki, ii):
        c = base + ki * 1024 + ii * 128
        return AW1[:, c:c + 128]

    def aw1_w1(ki, fi):
        c = ki * 4096 + fi * 128
        return AW1[:, c:c + 128]

    def hloc_sl(ii, c):
        a = ii * TOK + c * TC
        return AW2[:, a:a + TC]

    def apb_sl(ii, c):
        a = 16384 + ii * TOK + c * TC
        return AW2[:, a:a + TC]

    def w2_rhs(fk, half):
        a = fk * 1024 + half * 512
        return AW2[:, a:a + 512]

    def x0b_sl(ki):
        return AX[:, ki * TC:(ki + 1) * TC]

    def xinb_sl(ii):
        return AX[:, 4096 + ii * TC:4096 + (ii + 1) * TC]

    def xinf_sl(ii):
        return AFr[:, ii * TC:(ii + 1) * TC]

    def afull_sl(ii):
        return AFr[:, 4096 + ii * TC:4096 + (ii + 1) * TC]

    def slot(k):
        return ABf[:, 6144 + k * 512:6144 + (k + 1) * 512]

    def ufull_sl(ii):
        return AB[:, ii * TC:(ii + 1) * TC]

    def xfull_sl(ii):
        return AB[:, 8192 + ii * 512:8192 + (ii + 1) * 512]

    def hfix_sl(ii):
        return AB[:, ii * TC:(ii + 1) * TC]

    def x1c_sl(ii):
        return AB[:, 4096 + ii * TC:4096 + (ii + 1) * TC]

    def zb_slot(k):
        return AB[:, 8192 + k * 512:8192 + (k + 1) * 512]

    def zsq_slot(k):
        return AB[:, 9216 + k * 512:9216 + (k + 1) * 512]

    def z_sl(ii):
        return AFr[:, ii * TC:(ii + 1) * TC]

    def rows(k):
        return AFr[0:1, 4096 + k * 512:4096 + (k + 1) * 512]

    def x1tmp():
        return AFr[:, 6144:6656]

    def glog_sl():
        return ABf[0:8, 5120:5632]

    def combT_sl(c):
        return AB[0:8, 12288 + c * TC:12288 + (c + 1) * TC]

    def xg_sl(ki):
        return AX[:, ki * TB:(ki + 1) * TB]

    def xw_sl(ki):
        return AX[:, 4096 + ki * TB:4096 + (ki + 1) * TB]

    def h1s_sl(fk, off, ln):
        a = fk * TB + off
        return AFb[:, a:a + ln]

    def x1T_sl(par):
        return AB[:, par * 1024:(par + 1) * 1024]

    def moe_sl(par):
        return AB[:, 2048 + par * 1024:2048 + (par + 1) * 1024]

    def rsst_sl(par):
        return AB[:, 4096 + par * 1024:4096 + (par + 1) * 1024]

    def wb_sl(par):
        return AB[:, 6144 + par * 512:6144 + (par + 1) * 512]

    def cm8_sl(par):
        return AB[0:8, 7168 + par * 512:7168 + (par + 1) * 512]

    def z2_sl(par):
        return ABf[:, 4096 + par * 1024:4096 + (par + 1) * 1024]

    def spacer():
        p.add("vector", lambda e: e.tensor_copy(sm["dmy"][:, :], sm["tmpc"][:, :]))

    def vgap():
        spacer()
        return p.add("vector", lambda e: e.tensor_copy(sm["dmy"][:, :], sm["tmpc"][:, :]),
                     inc="sv")

    t_w = {}
    for nm, src, base in [("in", w_in, 0), ("gx", w_gx, 8192),
                          ("ga", w_ga, 16384), ("out", w_out, 24576)]:
        src3 = src.rearrange("(ki p) c -> p ki c", p=128)
        dst3 = AW1[:, base:base + 8192].rearrange("p (ki c) -> p ki c", ki=8)
        t_w[nm] = p.add("gpsimd",
                        lambda e, o=dst3, i=src3: e.dma_start(out=o, in_=i),
                        inc="sg", n=16)


    t_small = p.add("sync", lambda e: e.dma_start(out=SML[:, :], in_=smalls_h),
                    inc="ss", n=16)
    t_idn = p.add("sync", lambda e: e.dma_start(out=sm["idn"][:, :], in_=idn_h),
                  waits=[t_small], inc="ss", n=16)
    t_sel8 = p.add("gpsimd", lambda e: e.dma_start(out=sm["sel8"][:, :], in_=sel8_h),
                   waits=[t_idn], inc="sg", n=16)
    t_gwtf = t_small
    small_loads = [t_small] * 6
    t_sml = t_small

    p.add("vector", lambda e: e.memset(sm["hcar"][:, :], 0.0))
    p.add("vector", lambda e: e.memset(sm["apcar"][:, :], 1.0))
    p.add("vector", lambda e: e.memset(sm["cin"][:, :], 0.0))
    p.add("vector", lambda e: e.memset(sm["tmpc"][:, :], 0.0))
    p.add("vector", lambda e: e.memset(sm["onesc"][:, :], 1.0))
    p.add("vector", lambda e: e.memset(sm["epsc"][:, :], EPS))
    p.add("vector", lambda e: e.memset(sm["onec"][:, :], 1.0))
    t_ones = p.add("vector", lambda e: e.memset(sm["onesf"][:, :], 1.0), inc="sv")
    def sgap(n=8):
        for _ in range(n):
            p.add("scalar", lambda e: e.activation(sm["dmy"][:, :], sm["tmpc"][:, :],
                                                   AF.Copy))
    ta1 = p.add("scalar", lambda e: e.activation(
        sm["spt"][:, :], sm["aptmp"][:, :], AF.Abs),
        waits=[small_loads[4]], inc="sc")
    t_r = p.add("scalar", lambda e: e.activation(
        sm["msp_h"][:, :], sm["aptmp"][:, :], AF.Relu), inc="sc")
    sgap()
    t_e = p.add("scalar", lambda e: e.activation(
        sm["aptmp"][:, :], sm["spt"][:, :], AF.Exp, scale=-1.0), inc="sc")
    sgap()
    t_l = p.add("scalar", lambda e: e.activation(
        sm["spt"][:, :], sm["aptmp"][:, :], AF.Ln, bias=sm["onec"][:, 0:1]),
        waits=[t_ones], inc="sc")
    sgap()
    tad = p.add("vector", lambda e: e.tensor_add(
        sm["msp_h"][:, :], sm["msp_h"][:, :], sm["spt"][:, :]),
        waits=[t_l, t_r], inc="sv")
    spacer()
    spacer()
    t_msp = p.add("vector", lambda e: e.tensor_scalar_mul(
        sm["msp_h"][:, :], sm["msp_h"][:, :], -4.0), waits=[tad], inc="sv")
    t_bgxh = p.add("vector", lambda e: e.tensor_scalar_mul(
        sm["bgx_hf"][:, :], sm["bgx_hf"][:, :], 0.5), waits=[small_loads[1]], inc="sv")
    t_bgah = p.add("vector", lambda e: e.tensor_scalar_mul(
        sm["bga_hf"][:, :], sm["bga_hf"][:, :], 0.5), waits=[small_loads[2]], inc="sv")

    rotors = [PA0, PA1, PA2, PSP]
    rot_evac = [T0, T0, T0, T0]
    rot_i = 0

    def next_rot():
        nonlocal rot_i
        r = rot_i % 4
        rot_i += 1
        return r

    t_x0 = [None] * NCHUNK
    t_xin_mm_last = T0
    slot_free = [T0, T0]
    t_u_done = {}
    t_x_done = {}
    t_sct = {}

    for c in range(NCHUNK):
        wx = [t_xin_mm_last] if c > 0 else []
        src3 = xT.rearrange("(ki p) t -> p ki t", p=128)[:, :, c * TC:(c + 1) * TC]
        dst3 = AX[:, 0:4096].rearrange("p (ki t) -> p ki t", ki=8)
        t_x0[c] = p.add("gpsimd", lambda e, o=dst3, i=src3: e.dma_start(out=o, in_=i),
                        waits=wx, inc="sg", n=16)

        t_evac_xin = []
        for ii in range(8):
            r = next_rot()
            tok = None
            for ki in range(8):
                tok = p.add("tensor", lambda e, ps=rotors[r], l=aw1_w(0, ki, ii),
                            rr=x0b_sl(ki), s=(ki == 0), t=(ki == 7):
                            e.matmul(ps[:, :], l, rr, start=s, stop=t),
                            waits=([t_x0[c], t_w["in"], rot_evac[r]] if ki == 0 else []),
                            inc=("st" if ki == 7 else None))
            p.add("vector", lambda e, ps=rotors[r], o=xinf_sl(ii), b=sm["b_in"][:, ii:ii + 1]:
                  e.tensor_scalar_add(o, ps[:, :], b), waits=[tok, t_sml])
            e2 = p.add("vector", lambda e, ps=rotors[r], o=xinb_sl(ii), b=sm["b_in"][:, ii:ii + 1]:
                       e.tensor_scalar_add(o, ps[:, :], b), inc="sv")
            rot_evac[r] = e2
            t_evac_xin.append(e2)
        t_evac_xin_g = vgap()
        t_xin_mm_last = p.tok("st")

        for ii in range(8):
            sA = slot(0)
            sB = slot(1)
            rA = next_rot()
            tokA = None
            for ki in range(8):
                tokA = p.add("tensor", lambda e, ps=rotors[rA], l=aw1_w(16384, ki, ii),
                             rr=xinb_sl(ki), s=(ki == 0), t=(ki == 7):
                             e.matmul(ps[:, :], l, rr, start=s, stop=t),
                             waits=([t_evac_xin_g, t_w["ga"], rot_evac[rA]] if ki == 0 else []),
                             inc=("st" if ki == 7 else None))
            rB = next_rot()
            tokB = None
            for ki in range(8):
                tokB = p.add("tensor", lambda e, ps=rotors[rB], l=aw1_w(8192, ki, ii),
                             rr=xinb_sl(ki), s=(ki == 0), t=(ki == 7):
                             e.matmul(ps[:, :], l, rr, start=s, stop=t),
                             waits=([t_w["gx"], rot_evac[rB]] if ki == 0 else []),
                             inc=("st" if ki == 7 else None))
            ta = p.add("scalar", lambda e, ps=rotors[rA], o=sB, b=sm["bga_hf"][:, ii:ii + 1]:
                       e.activation(o, ps[:, :], AF.Tanh, bias=b, scale=0.5),
                       waits=[tokA, t_bgah, slot_free[1]], inc="sc")
            rot_evac[rA] = ta
            tb = p.add("scalar", lambda e, ps=rotors[rB], o=sA, b=sm["bgx_hf"][:, ii:ii + 1]:
                       e.activation(o, ps[:, :], AF.Tanh, bias=b, scale=0.5),
                       waits=[tokB, t_bgxh, slot_free[0]], inc="sc")
            rot_evac[rB] = tb
            tl = p.add("vector", lambda e, s=sB, m=sm["msp_h"][:, ii:ii + 1]:
                       e.tensor_scalar(s, s, m, m, OP.mult, OP.add),
                       waits=[ta, t_msp], inc="sv")
            tu = p.add("vector", lambda e, s=sA, o=ufull_sl(ii), xi=xinf_sl(ii):
                       e.scalar_tensor_tensor(o, s, 1.0, xi, OP.add, OP.mult),
                       waits=[tb], inc="sv")
            t_u_done[(c, ii)] = tu
            p.add("scalar", lambda e, s=sB, o=afull_sl(ii):
                  e.activation(o, s, AF.Exp), waits=[tu], inc="sc")
            tq = p.add("scalar", lambda e, s=sB, o=sA:
                       e.activation(o, s, AF.Exp, scale=2.0), waits=[tu], inc="sc")
            slot_free[1] = tq
            p.add("vector", lambda e, s=sA: e.tensor_scalar(
                s, s, -1.0, 1.0, OP.mult, OP.add), waits=[tq])
            p.add("vector", lambda e, s=sA: e.tensor_scalar_max(s, s, 0.0))
            tx3 = p.add("vector", lambda e, s=sA, o=xfull_sl(ii): e.tensor_copy(o, s),
                        inc="sv")
            slot_free[0] = tx3
            t_x_done[(c, ii)] = tx3

        for ii in range(8):
            t_sct[(c, ii)] = p.add("scalar", lambda e, s=xfull_sl(ii):
                                   e.activation(s, s, AF.Sqrt, scale=0.25),
                                   waits=[t_x_done[(c, ii)]], inc="sc")

        for i0 in range(0, 8, 2):
            prs = [i0, i0 + 1]
            bsl = {ii: AFr[:, (ii % 2) * 512:(ii % 2) * 512 + 512] for ii in prs}
            hsl = {ii: AFr[:, 1024 + (ii % 2) * 512:1024 + (ii % 2) * 512 + 512] for ii in prs}
            asl = {ii: AFr[:, 2048 + (ii % 2) * 512:2048 + (ii % 2) * 512 + 512] for ii in prs}
            for ii in prs:
                p.add("vector", lambda e, o=bsl[ii], u=ufull_sl(ii), s=xfull_sl(ii):
                      e.tensor_mul(o, u, s),
                      waits=[t_sct[(c, ii)], t_u_done[(c, ii)]])
            for ii in prs:
                p.add("vector", lambda e, o=hsl[ii], a=afull_sl(ii), b=bsl[ii],
                      i0_=sm["hcar"][:, ii:ii + 1]:
                      e.tensor_tensor_scan(o, a, b, i0_, OP.mult, OP.add))
            for ii in prs:
                p.add("vector", lambda e, o=asl[ii], a=afull_sl(ii),
                      i0_=sm["apcar"][:, ii:ii + 1]:
                      e.tensor_tensor_scan(o, a, a, i0_, OP.mult, OP.bypass))
            for ii in prs:
                p.add("vector", lambda e, o=hloc_sl(ii, c), i=hsl[ii]: e.tensor_copy(o, i))
            for ii in prs:
                p.add("vector", lambda e, o=apb_sl(ii, c), i=asl[ii]: e.tensor_copy(o, i))
            for ii in prs:
                p.add("vector", lambda e, o=sm["hcar"][:, ii:ii + 1], i=hsl[ii][:, 511:512]:
                      e.tensor_copy(o, i))
            for ii in prs:
                p.add("vector", lambda e, o=sm["apcar"][:, ii:ii + 1], i=asl[ii][:, 511:512]:
                      e.tensor_copy(o, i), inc=("sv" if ii == i0 + 1 else None))

    t_lastscan = vgap()
    ag1i3 = ag1_in.ap().rearrange("r (p ii) -> r p ii", ii=8)
    tg1 = p.add("gpsimd", lambda e: e.dma_start(out=ag1i3[0], in_=sm["apcar"][:, :]),
                waits=[t_lastscan], inc="sg", n=16)
    tg2 = p.add("gpsimd", lambda e: e.dma_start(out=ag1i3[1], in_=sm["hcar"][:, :]),
                inc="sg", n=16)
    t_ag1 = p.add("gpsimd", lambda e: e.collective_compute(
        "AllGather", OP.bypass, replica_groups=[list(range(CORES))],
        ins=[ag1_in.ap().opt()], outs=[ag1_out.ap().opt()]),
        waits=[tg1, tg2], inc="scc")
    ag1o3 = ag1_out.ap().rearrange("(r two) (p ii) -> two p r ii", two=2, ii=8)
    tga = p.add("gpsimd", lambda e: e.dma_start(
        out=sm["asum"].rearrange("p (r ii) -> p r ii", ii=8), in_=ag1o3[0]),
        waits=[t_ag1], inc="sg", n=16)
    tge = p.add("gpsimd", lambda e: e.dma_start(
        out=sm["esum"].rearrange("p (r ii) -> p r ii", ii=8), in_=ag1o3[1]),
        waits=[tga], inc="sg", n=16)
    p.add("vector", lambda e: e.tensor_copy(sm["cpre"][:, 0:8], sm["esum"][:, 0:8]),
          waits=[tge])
    for k in range(1, 7):
        spacer()
        p.add("vector", lambda e, k=k: e.tensor_mul(
            sm["tmpc"][:, :], sm["asum"][:, k * 8:(k + 1) * 8],
            sm["cpre"][:, (k - 1) * 8:k * 8]))
        spacer()
        p.add("vector", lambda e, k=k: e.tensor_add(
            sm["cpre"][:, k * 8:(k + 1) * 8], sm["tmpc"][:, :],
            sm["esum"][:, k * 8:(k + 1) * 8]))
    for k in range(7):
        spacer()
        spacer()
        p.add("vector", lambda e, k=k: e.scalar_tensor_tensor(
            sm["cin"][:, :], sm["cpre"][:, k * 8:(k + 1) * 8],
            sm["selc"][:, k:k + 1], sm["cin"][:, :], OP.mult, OP.add))
    spacer()
    spacer()
    t_cin = p.add("vector", lambda e: e.tensor_copy(sm["dmy"][:, :], sm["cin"][:, :]),
                  inc="sv")

    rot3 = [PA2, PSP]
    rot3_evac = [rot_evac[2], rot_evac[3]]
    pa0_free = rot_evac[0]
    pa1_free = rot_evac[1]
    t_glog_mm = T0
    t_x1dma_prev = ("sg", 0)
    t_zev = T0
    t_wout_mm = T0
    t_hf = []
    tct = T0

    for c in range(NCHUNK):
        src3 = xT.rearrange("(ki p) t -> p ki t", p=128)[:, :, c * TC:(c + 1) * TC]
        dst3 = AX[:, 0:4096].rearrange("p (ki t) -> p ki t", ki=8)
        t_x0r = p.add("gpsimd", lambda e, o=dst3, i=src3: e.dma_start(out=o, in_=i),
                      waits=[t_lastscan] + ([t_zev] if c > 0 else []), inc="sg", n=16)
        t_hf = []
        for ii in range(8):
            t_hf.append(p.add("vector", lambda e, o=hfix_sl(ii), a=apb_sl(ii, c),
                              s=sm["cin"][:, ii:ii + 1], h=hloc_sl(ii, c):
                              e.scalar_tensor_tensor(o, a, s, h, OP.mult, OP.add),
                              waits=[t_cin] + ([t_wout_mm] if c > 0 else []), inc="sv"))
        t_hf_g = vgap()
        t_zs = []
        for ii in range(8):
            r = ii % 2
            tok = None
            for ki in range(8):
                tok = p.add("tensor", lambda e, ps=rot3[r], l=aw1_w(24576, ki, ii),
                            rr=hfix_sl(ki), s=(ki == 0), t=(ki == 7):
                            e.matmul(ps[:, :], l, rr, start=s, stop=t),
                            waits=([t_hf_g, t_w["out"], rot3_evac[r]] if ki == 0 else []),
                            inc=("st" if ki == 7 else None))
            tz = p.add("vector", lambda e, ps=rot3[r], o=z_sl(ii),
                       b=sm["b_out"][:, ii:ii + 1], x0=x0b_sl(ii):
                       e.scalar_tensor_tensor(o, ps[:, :], b, x0, OP.add, OP.add),
                       waits=[tok, t_x0r], inc="sv")
            rot3_evac[r] = tz
            t_zs.append(tz)
        t_wout_mm = p.tok("st")
        t_zev = t_zs[-1]
        t_slot_mm = [T0, T0]
        tok_sum = tok_sq = None
        for ii in range(8):
            k = ii % 2
            p.add("vector", lambda e, o=zb_slot(k), z=z_sl(ii): e.tensor_copy(o, z),
                  waits=[t_zs[ii], t_slot_mm[k]])
            p.add("vector", lambda e, o=zsq_slot(k), z=z_sl(ii): e.tensor_mul(o, z, z))
            tzb = tzq = vgap()
            tok_sum = p.add("tensor", lambda e, rr=zb_slot(k), s=(ii == 0), t=(ii == 7):
                            e.matmul(PA0[0:1, :], sm["onesc"][:, :], rr, start=s, stop=t),
                            waits=[tzb, t_ones] + ([pa0_free] if ii == 0 else []),
                            inc="st")
            tok_sq = p.add("tensor", lambda e, rr=zsq_slot(k), s=(ii == 0), t=(ii == 7):
                           e.matmul(PA1[0:1, :], sm["onesc"][:, :], rr, start=s, stop=t),
                           waits=[tzq] + ([pa1_free] if ii == 0 else []),
                           inc="st")
            t_slot_mm[k] = tok_sq
        trm = p.add("vector", lambda e: e.tensor_scalar_mul(rows(0), PA0[0:1, :], 1.0 / D),
                    waits=[tok_sum], inc="sv")
        p.add("vector", lambda e: e.tensor_scalar_mul(rows(1), PA1[0:1, :], 1.0 / D),
              waits=[tok_sq])
        spacer()
        p.add("vector", lambda e: e.tensor_mul(rows(2), rows(0), rows(0)))
        spacer()
        p.add("vector", lambda e: e.tensor_sub(rows(1), rows(1), rows(2)))
        spacer()
        spacer()
        tvar = p.add("vector", lambda e: e.tensor_copy(sm["dmy"][:, :], sm["tmpc"][:, :]),
                     inc="sv")
        tsq = p.add("scalar", lambda e: e.activation(rows(2), rows(1), AF.Sqrt, bias=sm["epsc"][0:1, 0:1]),
                    waits=[tvar], inc="sc")
        trc = p.add("vector", lambda e: e.reciprocal(rows(3), rows(2)), waits=[tsq],
                    inc="sv")
        spacer()
        spacer()
        p.add("vector", lambda e: e.tensor_mul(rows(2), rows(0), rows(3)))
        tmr = vgap()
        tbr = p.add("tensor", lambda e: e.matmul(PA1[:, :], sm["onesf"][:, :], rows(3),
                                                 start=True, stop=True),
                    waits=[tmr, tvar, t_ones], inc="st")
        tbm = p.add("tensor", lambda e: e.matmul(PA0[:, :], sm["onesf"][:, :], rows(2),
                                                 start=True, stop=True),
                    waits=[tmr, trm], inc="st")
        tx1last = None
        for ii in range(8):
            p.add("vector", lambda e, z=z_sl(ii): e.tensor_mul(x1tmp(), z, PA1[:, :]),
                  waits=([tbr, tbm, t_glog_mm, t_x1dma_prev] if ii == 0 else []))
            p.add("vector", lambda e, z=z_sl(ii): e.tensor_sub(z, x1tmp(), PA0[:, :]))
            p.add("vector", lambda e, o=x1c_sl(ii), z=z_sl(ii): e.tensor_copy(o, z))
            tx1last = None
        tx1last = vgap()
        pa0_free = tx1last
        pa1_free = tx1last
        t_x1dma_prev = p.add("gpsimd", lambda e, c=c: e.dma_start(
            out=ag2_in.ap()[0:D, c * TC:(c + 1) * TC].rearrange("(ki p) t -> p ki t", p=128),
            in_=AB[:, 4096:8192].rearrange("p (ki t) -> p ki t", ki=8)),
            waits=[tx1last], inc="sg", n=16)
        tok = None
        for ki in range(8):
            tok = p.add("tensor", lambda e, l=sm["gwtf"][:, ki * 8:(ki + 1) * 8],
                        rr=z_sl(ki), s=(ki == 0), t=(ki == 7):
                        e.matmul(PSP[0:8, :], l, rr, start=s, stop=t),
                        waits=([tx1last, t_gwtf, rot3_evac[1]] if ki == 0 else []),
                        inc=("st" if ki == 7 else None))
        t_glog_mm = tok
        tgl = p.add("vector", lambda e: e.tensor_copy(glog_sl(), PSP[0:8, :]),
                    waits=[tok], inc="sv")
        rot3_evac[1] = tgl
        tprev = rot3_evac[0]
        for j in range(4):
            ttr = p.add("tensor", lambda e, j=j: e.transpose(
                PA2[:, 0:8], glog_sl()[0:8, j * 128:(j + 1) * 128],
                sm["idn"][0:8, 0:8]),
                waits=[tgl, t_idn, tprev], inc="st")
            tprev = p.add("vector", lambda e, j=j: e.tensor_copy(
                sm["glt4"][:, j * 8:(j + 1) * 8], PA2[:, 0:8]),
                waits=[ttr], inc="sv")
        rot3_evac[0] = tprev
        for j in range(4):
            p.add("vector", lambda e, j=j: e.reduce_max(
                sm["m14"][:, j:j + 1], sm["glt4"][:, j * 8:(j + 1) * 8], axis=AX_X))
        spacer()
        spacer()
        for j in range(4):
            p.add("vector", lambda e, j=j: e.tensor_scalar(
                sm["mk4"][:, j * 8:(j + 1) * 8], sm["glt4"][:, j * 8:(j + 1) * 8],
                sm["m14"][:, j:j + 1], 1.0, OP.is_equal, OP.mult))
        spacer()
        for j in range(4):
            p.add("vector", lambda e, j=j: e.scalar_tensor_tensor(
                sm["ms4"][:, j * 8:(j + 1) * 8], sm["mk4"][:, j * 8:(j + 1) * 8],
                -1e30, sm["glt4"][:, j * 8:(j + 1) * 8], OP.mult, OP.add))
        spacer()
        for j in range(4):
            p.add("vector", lambda e, j=j: e.reduce_max(
                sm["m24"][:, j:j + 1], sm["ms4"][:, j * 8:(j + 1) * 8], axis=AX_X))
        spacer()
        spacer()
        p.add("vector", lambda e: e.tensor_sub(sm["dd4"], sm["m24"],
                                               sm["m14"]))
        spacer()
        spacer()
        tdd = p.add("vector", lambda e: e.tensor_copy(sm["dmy"][:, :], sm["tmpc"][:, :]),
                    inc="sv")
        tth = p.add("scalar", lambda e: e.activation(sm["th4"], sm["dd4"],
                                                     AF.Tanh, scale=0.5),
                    waits=[tdd], inc="sc")
        p.add("vector", lambda e: e.tensor_scalar(sm["g24"], sm["th4"],
                                                  0.5, 0.5, OP.mult, OP.add),
              waits=[tth])
        p.add("vector", lambda e: e.tensor_scalar(sm["g14"], sm["th4"],
                                                  -0.5, 0.5, OP.mult, OP.add))
        spacer()
        spacer()
        for j in range(4):
            p.add("vector", lambda e, j=j: e.tensor_scalar(
                sm["comb4"][:, j * 8:(j + 1) * 8], sm["glt4"][:, j * 8:(j + 1) * 8],
                sm["m14"][:, j:j + 1], sm["g14"][:, j:j + 1], OP.is_equal, OP.mult))
        spacer()
        for j in range(4):
            p.add("vector", lambda e, j=j: e.tensor_scalar(
                sm["c24"][:, j * 8:(j + 1) * 8], sm["ms4"][:, j * 8:(j + 1) * 8],
                sm["m24"][:, j:j + 1], sm["g24"][:, j:j + 1], OP.is_equal, OP.mult))
        spacer()
        tcb = p.add("vector", lambda e: e.tensor_add(sm["comb4"][:, :], sm["comb4"][:, :],
                                                     sm["c24"][:, :]), inc="sv")
        tprev = rot3_evac[1]
        for j in range(4):
            ttb = p.add("tensor", lambda e, j=j: e.transpose(
                PSP[0:8, 0:128], sm["comb4"][:, j * 8:(j + 1) * 8], sm["idn"][:, :]),
                waits=[tcb, tprev], inc="st")
            tprev = p.add("vector", lambda e, c=c, j=j: e.tensor_copy(
                combT_sl(c)[:, j * 128:(j + 1) * 128], PSP[0:8, 0:128]),
                waits=[ttb], inc="sv")
        tct = tprev
        rot3_evac[1] = tct

    t_tensorA_done = p.tok("st")
    t_comb_dma = p.add("gpsimd", lambda e: e.dma_start(
        out=ag2_in.ap()[D:D + 8, :], in_=AB[0:8, 12288:14336]),
        waits=[tct], inc="sg", n=16)
    t_ag2 = p.add("gpsimd", lambda e: e.collective_compute(
        "AllGather", OP.bypass, replica_groups=[list(range(CORES))],
        ins=[ag2_in.ap().opt()], outs=[ag2_out.ap().opt()]),
        waits=[t_comb_dma, t_x1dma_prev], inc="scc")

    tw1 = p.add("gpsimd", lambda e: e.dma_start(
        out=AW1.ap().rearrange("p (ki f) -> p ki f", ki=8),
        in_=w1T.rearrange("(ki p) f -> p ki f", p=128)),
        waits=[t_tensorA_done], inc="sg", n=16)
    tw2 = p.add("gpsimd", lambda e: e.dma_start(
        out=AW2.ap().rearrange("p (fk d) -> p fk d", fk=32),
        in_=w2T.rearrange("(fk p) d -> p fk d", p=128)),
        waits=[t_hf[-1]], inc="sg", n=16)

    h1_rot = [PA0, PA1, PSP]
    h1_evac = [pa0_free, pa1_free, rot3_evac[1]]
    h1_mm = [T0, T0, T0]
    pyb = [PY0, PY1]
    py_evac = [T0, T0]
    wb_mm = T0
    wb_copy = rot3_evac[0]
    t_xw = {}
    last_rs_dma = [("sg", 0), ("sg", 0)]
    t_out_count = 0
    hro = 0
    pyo = 0
    last_ln_dma = [("so", 0), ("so", 0)]
    last_ln_read = [T0, T0]

    for lc in range(NQ):
        for r in range(CORES):
            ci = lc * CORES + r
            par = ci % 2
            wxg = [t_ag2, t_zev] + ([t_xw[ci - 1]] if ci > 0 else [])
            txg = p.add("sync", lambda e, r=r, lc=lc: e.dma_start(
                out=AX[:, 0:4096].rearrange("p (ki t) -> p ki t", ki=8),
                in_=ag2_out.ap()[r * (D + 8):r * (D + 8) + D,
                                 lc * TB:(lc + 1) * TB].rearrange("(ki p) t -> p ki t", p=128)),
                waits=wxg, inc="ss", n=16)
            tc8 = p.add("sync", lambda e, r=r, lc=lc, par=par: e.dma_start(
                out=cm8_sl(par),
                in_=ag2_out.ap()[r * (D + 8) + D:r * (D + 8) + D + 8,
                                 lc * TB:(lc + 1) * TB]),
                waits=[t_ag2, wb_mm], inc="ss", n=16)
            wb_mm = p.add("tensor", lambda e, par=par: e.matmul(
                PA2[:, :], sm["sel8"][:, :], cm8_sl(par), start=True, stop=True),
                waits=[tc8, t_sel8, wb_copy], inc="st")
            wb_copy = p.add("vector", lambda e, par=par: e.tensor_copy(wb_sl(par), PA2[:, :]),
                            waits=[wb_mm], inc="sv")
            for ki in range(8):
                p.add("vector", lambda e, ki=ki, par=par: e.tensor_mul(
                    xw_sl(ki), xg_sl(ki), wb_sl(par)),
                    waits=(([txg, wb_copy] +
                            ([h1_mm[(hro + 2) % 3]] if ci > 0 else [t_tensorA_done]))
                           if ki == 0 else []))
            txw = vgap()
            t_xw[ci] = txw
            for fi in range(FI):
                rr_ = hro % 3
                hro += 1
                tok = None
                for ki in range(8):
                    tok = p.add("tensor", lambda e, ps=h1_rot[rr_], l=aw1_w1(ki, fi),
                                rhs=xw_sl(ki), s=(ki == 0), t=(ki == 7):
                                e.matmul(ps[:, :], l, rhs, start=s, stop=t),
                                waits=([txw, tw1, h1_evac[rr_]] if ki == 0 else []),
                                inc=("st" if ki == 7 else None))
                h1_mm[rr_] = tok
                if fi % 2 == 0:
                    t_ev = p.add("vector", lambda e, ps=h1_rot[rr_], o=h1s_sl(fi, 0, TB):
                                 e.tensor_scalar(o, ps[:, :], 0.0, None, OP.max),
                                 waits=[tok], inc="sv")
                else:
                    t_ev = p.add("scalar", lambda e, ps=h1_rot[rr_], o=h1s_sl(fi, 0, TB):
                                 e.activation(o, ps[:, :], AF.Relu),
                                 waits=[tok], inc="sc")
                h1_evac[rr_] = t_ev
            h1s_done_v = vgap()
            h1s_done_c = p.add("scalar", lambda e: e.activation(
                sm["dmy"][:, :], sm["tmpc"][:, :], AF.Copy), inc="sc")
            for tt in range(4):
                pb = pyo % 2
                pyo += 1
                tok = None
                for half in range(2):
                    for fk in range(FI):
                        is_first = (half == 0 and fk == 0)
                        is_last = (half == 1 and fk == FI - 1)
                        tok = p.add("tensor", lambda e, ps=pyb[pb], h=half,
                                    l=h1s_sl(fk, tt * 128, 128),
                                    rhs=w2_rhs(fk, half), s=(fk == 0), t=(fk == FI - 1):
                                    e.matmul(ps[:, h * 512:(h + 1) * 512], l, rhs,
                                             start=s, stop=t),
                                    waits=([h1s_done_v, h1s_done_c, tw2, py_evac[pb]]
                                           if is_first else []),
                                    inc=("st" if is_last else None))
                spar = tt % 2
                if tt % 2 == 0:
                    tev = p.add("vector", lambda e, ps=pyb[pb], o=rsst_sl(spar):
                                e.tensor_copy(o, ps[:, :]),
                                waits=[tok, last_rs_dma[spar]], inc="sv")
                else:
                    tev = p.add("scalar", lambda e, ps=pyb[pb], o=rsst_sl(spar):
                                e.activation(o, ps[:, :], AF.Copy),
                                waits=[tok, last_rs_dma[spar]], inc="sc")
                py_evac[pb] = tev
                last_rs_dma[spar] = p.add(
                    "gpsimd", lambda e, q=lc, r=r, tt=tt, spar=spar:
                    e.dma_start(out=rs_in[q].ap()[r * TC + tt * 128:
                                                  r * TC + (tt + 1) * 128, :],
                                in_=rsst_sl(spar)),
                    waits=[tev], inc="sg", n=16)
        t_rs = p.add("gpsimd", lambda e, q=lc: e.collective_compute(
            "ReduceScatter", OP.add, replica_groups=[list(range(CORES))],
            ins=[rs_in[q].ap().opt()], outs=[rs_out[q].ap().opt()]),
            waits=[last_rs_dma[0], last_rs_dma[1]], inc="scc")
        for tp in range(2):
            tin = {}
            tz2 = {}
            tmv = {}
            trc2 = {}
            for u in range(2):
                tt = tp * 2 + u
                tx1t = p.add("sync", lambda e, lc=lc, tt=tt, u=u: e.dma_start(
                    out=x1T_sl(u), in_=ag2_in.ap()[0:D, lc * TC + tt * 128:
                                                   lc * TC + (tt + 1) * 128],
                    transpose=True),
                    waits=[t_ag2, last_ln_dma[u], last_ln_read[u]], inc="ss", n=16)
                tmo = p.add("sync", lambda e, lc=lc, tt=tt, u=u: e.dma_start(
                    out=moe_sl(u), in_=rs_out[lc].ap()[tt * 128:(tt + 1) * 128, :]),
                    waits=[t_rs], inc="ss", n=16)
                tin[u] = (tx1t, tmo)
            for u in range(2):
                tz2[u] = p.add("vector", lambda e, u=u: e.tensor_add(
                    z2_sl(u), x1T_sl(u), moe_sl(u)),
                    waits=list(tin[u]), inc="sv")
                last_ln_read[u] = tz2[u]
            for u in range(2):
                p.add("vector", lambda e, u=u: e.bn_stats(
                    sm["st12"][:, u * 12:u * 12 + 6], z2_sl(u)[:, 0:512]),
                    waits=[tz2[u]])
                p.add("vector", lambda e, u=u: e.bn_stats(
                    sm["st12"][:, u * 12 + 6:u * 12 + 12], z2_sl(u)[:, 512:1024]))
            for u in range(2):
                tmv[u] = p.add("vector", lambda e, u=u: e.bn_aggr(
                    sm["mv2"][:, u * 2:u * 2 + 2], sm["st12"][:, u * 12:u * 12 + 12]),
                    inc="sv")
            spacer()
            spacer()
            tmvg = p.add("vector", lambda e: e.tensor_copy(sm["dmy"][:, :], sm["tmpc"][:, :]),
                         inc="sv")
            tmv = {0: tmvg, 1: tmvg}
            for u in range(2):
                tsq2 = p.add("scalar", lambda e, u=u: e.activation(
                    sm["rstd2"][:, u:u + 1], sm["mv2"][:, u * 2 + 1:u * 2 + 2],
                    AF.Sqrt, bias=sm["epsc"][:, 0:1]), waits=[tmv[u]], inc="sc")
                trc2[u] = tsq2
            for u in range(2):
                trc2[u] = p.add("vector", lambda e, u=u: e.reciprocal(
                    sm["rstd2"][:, u:u + 1], sm["rstd2"][:, u:u + 1]),
                    waits=[trc2[u]], inc="sv")
            spacer()
            spacer()
            touts = {}
            for u in range(2):
                touts[u] = p.add("vector", lambda e, u=u: e.tensor_scalar(
                    z2_sl(u), z2_sl(u), sm["mv2"][:, u * 2:u * 2 + 1],
                    sm["rstd2"][:, u:u + 1], OP.subtract, OP.mult),
                    waits=[trc2[u]], inc="sv")
            for u in range(2):
                tt = tp * 2 + u
                last_ln_dma[u] = p.add("sync", lambda e, lc=lc, tt=tt, u=u: e.dma_start(
                    out=out_d[lc * TC + tt * 128:lc * TC + (tt + 1) * 128, :],
                    in_=z2_sl(u)), waits=[touts[u]], inc="so", n=16)
                t_out_count += 16

    p.add("vector", lambda e: e.tensor_copy(sm["dmy"][:, :], sm["tmpc"][:, :]),
          waits=[("so", t_out_count)])

    with nc.Block() as block:
        p.emit(block)
    es.close()
    return nc


_NC_CACHE = None


def kernel(L0_attn_x, L0_attn_W_in, L0_attn_b_in, L0_attn_W_gx, L0_attn_b_gx,
           L0_attn_W_ga, L0_attn_b_ga, L0_attn_a_param, L0_attn_W_out,
           L0_attn_b_out, L0_ffn_gate_w, L0_ffn_w1, L0_ffn_w2):
    global _NC_CACHE
    f = np.float32
    x = np.asarray(L0_attn_x, f)
    xT = np.ascontiguousarray(x[0].T)

    def colmajor8(v):
        return np.ascontiguousarray(np.asarray(v, f).reshape(8, 128).T)

    W_in = np.ascontiguousarray(np.asarray(L0_attn_W_in, f))
    W_gx = np.ascontiguousarray(np.asarray(L0_attn_W_gx, f))
    W_ga = np.ascontiguousarray(np.asarray(L0_attn_W_ga, f))
    W_out = np.ascontiguousarray(np.asarray(L0_attn_W_out, f))
    gw = np.asarray(L0_ffn_gate_w, f)
    w1 = np.asarray(L0_ffn_w1, f)
    w2 = np.asarray(L0_ffn_w2, f)
    bins = colmajor8(L0_attn_b_in)
    bgxs = colmajor8(L0_attn_b_gx)
    bgas = colmajor8(L0_attn_b_ga)
    bouts = colmajor8(L0_attn_b_out)
    apars = colmajor8(L0_attn_a_param)
    idn = np.eye(128, dtype=f)
    gwp = np.ascontiguousarray(gw.T.reshape(8, 128, 8).transpose(1, 0, 2).reshape(128, 64))

    if _NC_CACHE is None:
        _NC_CACHE = build_nc()
    nc = _NC_CACHE

    in_maps = []
    for i in range(CORES):
        selc = np.zeros((128, 8), f)
        if i > 0:
            selc[:, i - 1] = 1.0
        sel8 = np.zeros((8, 128), f)
        sel8[i, :] = 1.0
        smalls = np.concatenate([bins, bgxs, bgas, bouts, apars, selc, gwp],
                                axis=1).astype(f)
        in_maps.append({
            "xT": np.ascontiguousarray(xT[:, i * TOK:(i + 1) * TOK]),
            "w_in": W_in, "w_gx": W_gx, "w_ga": W_ga, "w_out": W_out,
            "smalls_h": np.ascontiguousarray(smalls),
            "w1T": np.ascontiguousarray(w1[i].T),
            "w2T": np.ascontiguousarray(w2[i].T),
            "sel8_h": sel8, "idn_h": idn,
        })

    res = run_bass_kernel_spmd(nc, in_maps, core_ids=list(range(CORES)))
    out = np.concatenate([res.results[i]["out"] for i in range(CORES)], axis=0)
    return out.reshape(1, S, D).astype(np.float32)


# version 05
# speedup vs baseline: 1.0629x; 1.0629x over previous
import numpy as np
from contextlib import ExitStack

import concourse.bass as bass
import concourse.mybir as mybir
from concourse.bass_utils import run_bass_kernel_spmd

F32 = mybir.dt.float32
BF16 = mybir.dt.bfloat16
AF = mybir.ActivationFunctionType
OP = mybir.AluOpType
AX_X = mybir.AxisListType.X

CORES = 8
D = 1024
S = 16384
TOK = S // CORES
TC = 512
NCHUNK = TOK // TC
FF = 4096
FI = FF // 128
TB = 512
NQ = 4
EPS = 1e-5


class P:

    ENGS = ("sync", "gpsimd", "vector", "scalar", "tensor")

    def __init__(self, nc, es):
        self.nc = nc
        self.ops = {e: [] for e in self.ENGS}
        self.sems = {}
        self.cnt = {}
        for name in ("sg", "ss", "sv", "sc", "st", "scc", "so"):
            self.sems[name] = es.enter_context(nc.semaphore(name))
            self.cnt[name] = 0

    def add(self, eng, fn, waits=(), inc=None, n=1):
        ws = tuple(w for w in waits if w is not None and w[1] > 0)
        self.ops[eng].append((fn, ws, inc, n))
        if inc is not None:
            self.cnt[inc] += n
            return (inc, self.cnt[inc])
        return None

    def tok(self, name):
        return (name, self.cnt[name])

    def _run(self, name, eng):
        seen = {}
        for fn, waits, inc, n in self.ops[name]:
            for (s, v) in waits:
                if seen.get(s, 0) < v:
                    eng.wait_ge(self.sems[s], v)
                    seen[s] = v
            instr = fn(eng)
            if inc is not None:
                instr.then_inc(self.sems[inc], n)

    def emit(self, block):
        runner = self._run

        @block.sync
        def _(e):
            runner("sync", e)

        @block.gpsimd
        def _(e):
            runner("gpsimd", e)

        @block.vector
        def _(e):
            runner("vector", e)

        @block.scalar
        def _(e):
            runner("scalar", e)

        @block.tensor
        def _(e):
            runner("tensor", e)


def build_nc():
    nc = bass.Bass("TRN2", target_bir_lowering=False)

    xT = nc.dram_tensor("xT", [D, TOK], F32, kind="ExternalInput").ap()
    w_in = nc.dram_tensor("w_in", [D, D], F32, kind="ExternalInput").ap()
    w_gx = nc.dram_tensor("w_gx", [D, D], F32, kind="ExternalInput").ap()
    w_ga = nc.dram_tensor("w_ga", [D, D], F32, kind="ExternalInput").ap()
    w_out = nc.dram_tensor("w_out", [D, D], F32, kind="ExternalInput").ap()
    smalls_h = nc.dram_tensor("smalls_h", [128, 128], F32, kind="ExternalInput").ap()
    w1T = nc.dram_tensor("w1T", [D, FF], F32, kind="ExternalInput").ap()
    w2T = nc.dram_tensor("w2T", [FF, D], F32, kind="ExternalInput").ap()
    idn_h = nc.dram_tensor("idn_h", [128, 128], F32, kind="ExternalInput").ap()
    out_d = nc.dram_tensor("out", [TOK, D], F32, kind="ExternalOutput").ap()

    ag1_in = nc.dram_tensor("ag1_in", [2, D], F32)
    ag1_out = nc.dram_tensor("ag1_out", [2 * CORES, D], F32)
    agq_in = [nc.dram_tensor(f"agq_in_{q}", [D + 8, TC], BF16) for q in range(NQ)]
    agq_out = [nc.dram_tensor(f"agq_out_{q}", [(D + 8) * CORES, TC], BF16) for q in range(NQ)]
    rs_in = [nc.dram_tensor(f"rs_in_{q}", [CORES * TC, D], BF16) for q in range(NQ)]
    rs_out = [nc.dram_tensor(f"rs_out_{q}", [TC, D], BF16) for q in range(NQ)]

    es = ExitStack()

    AW1 = es.enter_context(nc.sbuf_tensor("AW1", [128, 32768], BF16))
    AW2 = es.enter_context(nc.sbuf_tensor("AW2", [128, 32768], BF16))
    AX = es.enter_context(nc.sbuf_tensor("AX", [128, 8192], BF16))
    AB = es.enter_context(nc.sbuf_tensor("AB", [128, 14336], BF16))
    AFr = es.enter_context(nc.sbuf_tensor("AF", [128, 8192], F32))
    AFb = AFr.ap().bitcast(BF16)
    ABf = AB.ap().bitcast(F32)

    SML = es.enter_context(nc.sbuf_tensor("SML", [128, 128], F32))
    sm = {}
    sm["b_in"] = SML[:, 0:8]
    sm["bgx_hf"] = SML[:, 8:16]
    sm["bga_hf"] = SML[:, 16:24]
    sm["b_out"] = SML[:, 24:32]
    sm["aptmp"] = SML[:, 32:40]
    sm["selc"] = SML[:, 40:48]
    sm["gwtf"] = SML[:, 48:112]
    sm["ohm"] = SML[:, 112:120]
    for nm, shp, dt in [
        ("msp_h", [128, 8], F32),
        ("hcar", [128, 8], F32), ("apcar", [128, 8], F32),
        ("cin", [128, 8], F32), ("tmpc", [128, 8], F32), ("dmy", [128, 8], F32),
        ("aesum", [128, 128], F32), ("cpre", [128, 64], F32),
        ("idn", [128, 128], F32),
        ("onesc", [128, 1], BF16), ("onesf", [1, 128], F32),
        ("glt4", [128, 32], F32), ("mk4", [128, 32], F32),
        ("ms4", [128, 32], F32), ("comb4", [128, 32], F32),
        ("st12", [128, 24], F32), ("mv2", [128, 4], F32), ("rstd2", [128, 2], F32),
        ("cmtok0", [128, 32], BF16), ("cmquar", [128, 256], BF16),
        ("wsel", [128, 8], F32),
        ("epsc", [128, 1], F32), ("onec", [128, 1], F32), ("spt", [128, 8], F32),
    ]:
        sm[nm] = es.enter_context(nc.sbuf_tensor(nm, shp, dt))
    sm["asum"] = sm["aesum"][:, 0:64]
    sm["esum"] = sm["aesum"][:, 64:128]
    sm["c24"] = sm["mk4"]
    sm["m14"] = sm["st12"][:, 0:4]
    sm["m24"] = sm["st12"][:, 4:8]
    sm["dd4"] = sm["st12"][:, 8:12]
    sm["th4"] = sm["st12"][:, 12:16]
    sm["g14"] = sm["st12"][:, 16:20]
    sm["g24"] = sm["st12"][:, 20:24]

    PA0 = es.enter_context(nc.psum_tensor("PA0", [128, 512], F32))
    PA1 = es.enter_context(nc.psum_tensor("PA1", [128, 512], F32))
    PA2 = es.enter_context(nc.psum_tensor("PA2", [128, 512], F32))
    PSP = es.enter_context(nc.psum_tensor("PSP", [128, 512], F32))
    PY0 = es.enter_context(nc.psum_tensor("PY0", [128, 1024], F32))
    PY1 = es.enter_context(nc.psum_tensor("PY1", [128, 1024], F32))

    p = P(nc, es)
    T0 = ("sv", 0)

    def aw1_w(base, ki, ii):
        c = base + ki * 1024 + ii * 128
        return AW1[:, c:c + 128]

    def aw1_w1(ki, fi):
        c = ki * 4096 + fi * 128
        return AW1[:, c:c + 128]

    def hloc_sl(ii, c):
        a = ii * TOK + c * TC
        return AW2[:, a:a + TC]

    def apb_sl(ii, c):
        a = 16384 + ii * TOK + c * TC
        return AW2[:, a:a + TC]

    def w2_rhs(fk, half):
        a = fk * 1024 + half * 512
        return AW2[:, a:a + 512]

    def x0b_sl(ki):
        return AX[:, ki * TC:(ki + 1) * TC]

    def xinb_sl(ii):
        return AX[:, 4096 + ii * TC:4096 + (ii + 1) * TC]

    def xinf_sl(ii):
        return AFr[:, ii * TC:(ii + 1) * TC]

    def afull_sl(ii):
        return AFr[:, 4096 + ii * TC:4096 + (ii + 1) * TC]

    def slot(k):
        return ABf[:, 6144 + k * 512:6144 + (k + 1) * 512]

    def ufull_sl(ii):
        return AB[:, ii * TC:(ii + 1) * TC]

    def xfull_sl(ii):
        return AB[:, 8192 + ii * 512:8192 + (ii + 1) * 512]

    def hfix_sl(ii):
        return AB[:, ii * TC:(ii + 1) * TC]

    def x1c_sl(ii):
        return AB[:, 4096 + ii * TC:4096 + (ii + 1) * TC]

    def zb_slot(k):
        return AB[:, 8192 + k * 512:8192 + (k + 1) * 512]

    def zsq_slot(k):
        return AB[:, 9216 + k * 512:9216 + (k + 1) * 512]

    def z_sl(ii):
        return AFr[:, ii * TC:(ii + 1) * TC]

    def rows(k):
        return AFr[0:1, 4096 + k * 512:4096 + (k + 1) * 512]

    def x1tmp():
        return AFr[:, 6144:6656]

    def glog_sl():
        return ABf[0:8, 5120:5632]

    def combT_sl(c):
        return AB[0:8, 12288 + c * TC:12288 + (c + 1) * TC]

    def xg_sl(ki):
        return AX[:, ki * TB:(ki + 1) * TB]

    def xw_sl(ki):
        return AX[:, 4096 + ki * TB:4096 + (ki + 1) * TB]

    def h1s_sl(fk, off, ln):
        a = fk * TB + off
        return AFb[:, a:a + ln]

    def x1T_sl(par):
        return AB[:, par * 1024:(par + 1) * 1024]

    def moe_sl(par):
        return AB[:, 2048 + par * 1024:2048 + (par + 1) * 1024]

    def rsst_sl(par):
        return AB[:, 4096 + par * 1024:4096 + (par + 1) * 1024]

    def wb_sl(par):
        return AB[:, 6144 + par * 512:6144 + (par + 1) * 512]

    def cm8_sl(par):
        return AB[0:8, 7168 + par * 512:7168 + (par + 1) * 512]

    def cm8_full(par):
        return AB[:, 7168 + par * 512:7168 + (par + 1) * 512]

    def z2_sl(par):
        return ABf[:, 4096 + par * 1024:4096 + (par + 1) * 1024]

    def spacer():
        p.add("vector", lambda e: e.tensor_copy(sm["dmy"][:, :], sm["tmpc"][:, :]))

    def vgap():
        spacer()
        return p.add("vector", lambda e: e.tensor_copy(sm["dmy"][:, :], sm["tmpc"][:, :]),
                     inc="sv")

    t_w = {}
    for nm, src, base in [("in", w_in, 0), ("gx", w_gx, 8192),
                          ("ga", w_ga, 16384), ("out", w_out, 24576)]:
        src3 = src.rearrange("(ki p) c -> p ki c", p=128)
        dst3 = AW1[:, base:base + 8192].rearrange("p (ki c) -> p ki c", ki=8)
        t_w[nm] = p.add("gpsimd",
                        lambda e, o=dst3, i=src3: e.dma_start(out=o, in_=i),
                        inc="sg", n=16)


    t_small = p.add("sync", lambda e: e.dma_start(out=SML[:, :], in_=smalls_h),
                    inc="ss", n=16)
    t_idn = p.add("sync", lambda e: e.dma_start(out=sm["idn"][:, :], in_=idn_h),
                  waits=[t_small], inc="ss", n=16)
    t_gwtf = t_small
    small_loads = [t_small] * 6
    t_sml = t_small

    p.add("vector", lambda e: e.memset(sm["hcar"][:, :], 0.0))
    p.add("vector", lambda e: e.memset(sm["apcar"][:, :], 1.0))
    p.add("vector", lambda e: e.memset(sm["cin"][:, :], 0.0))
    p.add("vector", lambda e: e.memset(sm["tmpc"][:, :], 0.0))
    p.add("vector", lambda e: e.memset(sm["onesc"][:, :], 1.0))
    p.add("vector", lambda e: e.memset(sm["epsc"][:, :], EPS))
    p.add("vector", lambda e: e.memset(sm["onec"][:, :], 1.0))
    t_ones = p.add("vector", lambda e: e.memset(sm["onesf"][:, :], 1.0), inc="sv")
    def sgap(n=8):
        for _ in range(n):
            p.add("scalar", lambda e: e.activation(sm["dmy"][:, :], sm["tmpc"][:, :],
                                                   AF.Copy))
    ta1 = p.add("scalar", lambda e: e.activation(
        sm["spt"][:, :], sm["aptmp"][:, :], AF.Abs),
        waits=[small_loads[4]], inc="sc")
    t_r = p.add("scalar", lambda e: e.activation(
        sm["msp_h"][:, :], sm["aptmp"][:, :], AF.Relu), inc="sc")
    sgap()
    t_e = p.add("scalar", lambda e: e.activation(
        sm["aptmp"][:, :], sm["spt"][:, :], AF.Exp, scale=-1.0), inc="sc")
    sgap()
    t_l = p.add("scalar", lambda e: e.activation(
        sm["spt"][:, :], sm["aptmp"][:, :], AF.Ln, bias=sm["onec"][:, 0:1]),
        waits=[t_ones], inc="sc")
    sgap()
    tad = p.add("vector", lambda e: e.tensor_add(
        sm["msp_h"][:, :], sm["msp_h"][:, :], sm["spt"][:, :]),
        waits=[t_l, t_r], inc="sv")
    spacer()
    spacer()
    t_msp = p.add("vector", lambda e: e.tensor_scalar_mul(
        sm["msp_h"][:, :], sm["msp_h"][:, :], -4.0), waits=[tad], inc="sv")
    t_bgxh = p.add("vector", lambda e: e.tensor_scalar_mul(
        sm["bgx_hf"][:, :], sm["bgx_hf"][:, :], 0.5), waits=[small_loads[1]], inc="sv")
    t_bgah = p.add("vector", lambda e: e.tensor_scalar_mul(
        sm["bga_hf"][:, :], sm["bga_hf"][:, :], 0.5), waits=[small_loads[2]], inc="sv")

    rotors = [PA0, PA1, PA2, PSP]
    rot_evac = [T0, T0, T0, T0]
    rot_i = 0

    def next_rot():
        nonlocal rot_i
        r = rot_i % 4
        rot_i += 1
        return r

    t_x0 = [None] * NCHUNK
    t_xin_mm_last = T0
    slot_free = [T0, T0]
    t_u_done = {}
    t_x_done = {}
    t_sct = {}

    for c in range(NCHUNK):
        wx = [t_xin_mm_last] if c > 0 else []
        src3 = xT.rearrange("(ki p) t -> p ki t", p=128)[:, :, c * TC:(c + 1) * TC]
        dst3 = AX[:, 0:4096].rearrange("p (ki t) -> p ki t", ki=8)
        t_x0[c] = p.add("gpsimd", lambda e, o=dst3, i=src3: e.dma_start(out=o, in_=i),
                        waits=wx, inc="sg", n=16)

        t_evac_xin = []
        for ii in range(8):
            r = next_rot()
            tok = None
            for ki in range(8):
                tok = p.add("tensor", lambda e, ps=rotors[r], l=aw1_w(0, ki, ii),
                            rr=x0b_sl(ki), s=(ki == 0), t=(ki == 7):
                            e.matmul(ps[:, :], l, rr, start=s, stop=t),
                            waits=([t_x0[c], t_w["in"], rot_evac[r]] if ki == 0 else []),
                            inc=("st" if ki == 7 else None))
            p.add("vector", lambda e, ps=rotors[r], o=xinf_sl(ii), b=sm["b_in"][:, ii:ii + 1]:
                  e.tensor_scalar_add(o, ps[:, :], b), waits=[tok, t_sml])
            e2 = p.add("vector", lambda e, ps=rotors[r], o=xinb_sl(ii), b=sm["b_in"][:, ii:ii + 1]:
                       e.tensor_scalar_add(o, ps[:, :], b), inc="sv")
            rot_evac[r] = e2
            t_evac_xin.append(e2)
        t_evac_xin_g = vgap()
        t_xin_mm_last = p.tok("st")

        for ii in range(8):
            sA = slot(0)
            sB = slot(1)
            rA = next_rot()
            tokA = None
            for ki in range(8):
                tokA = p.add("tensor", lambda e, ps=rotors[rA], l=aw1_w(16384, ki, ii),
                             rr=xinb_sl(ki), s=(ki == 0), t=(ki == 7):
                             e.matmul(ps[:, :], l, rr, start=s, stop=t),
                             waits=([t_evac_xin_g, t_w["ga"], rot_evac[rA]] if ki == 0 else []),
                             inc=("st" if ki == 7 else None))
            rB = next_rot()
            tokB = None
            for ki in range(8):
                tokB = p.add("tensor", lambda e, ps=rotors[rB], l=aw1_w(8192, ki, ii),
                             rr=xinb_sl(ki), s=(ki == 0), t=(ki == 7):
                             e.matmul(ps[:, :], l, rr, start=s, stop=t),
                             waits=([t_w["gx"], rot_evac[rB]] if ki == 0 else []),
                             inc=("st" if ki == 7 else None))
            ta = p.add("scalar", lambda e, ps=rotors[rA], o=sB, b=sm["bga_hf"][:, ii:ii + 1]:
                       e.activation(o, ps[:, :], AF.Tanh, bias=b, scale=0.5),
                       waits=[tokA, t_bgah, slot_free[1]], inc="sc")
            rot_evac[rA] = ta
            tb = p.add("scalar", lambda e, ps=rotors[rB], o=sA, b=sm["bgx_hf"][:, ii:ii + 1]:
                       e.activation(o, ps[:, :], AF.Tanh, bias=b, scale=0.5),
                       waits=[tokB, t_bgxh, slot_free[0]], inc="sc")
            rot_evac[rB] = tb
            tl = p.add("vector", lambda e, s=sB, m=sm["msp_h"][:, ii:ii + 1]:
                       e.tensor_scalar(s, s, m, m, OP.mult, OP.add),
                       waits=[ta, t_msp], inc="sv")
            tu = p.add("vector", lambda e, s=sA, o=ufull_sl(ii), xi=xinf_sl(ii):
                       e.scalar_tensor_tensor(o, s, 1.0, xi, OP.add, OP.mult),
                       waits=[tb], inc="sv")
            t_u_done[(c, ii)] = tu
            p.add("scalar", lambda e, s=sB, o=afull_sl(ii):
                  e.activation(o, s, AF.Exp), waits=[tu], inc="sc")
            tq = p.add("scalar", lambda e, s=sB, o=sA:
                       e.activation(o, s, AF.Exp, scale=2.0), waits=[tu], inc="sc")
            slot_free[1] = tq
            p.add("vector", lambda e, s=sA: e.tensor_scalar(
                s, s, -1.0, 1.0, OP.mult, OP.add), waits=[tq])
            p.add("vector", lambda e, s=sA: e.tensor_scalar_max(s, s, 0.0))
            tx3 = p.add("vector", lambda e, s=sA, o=xfull_sl(ii): e.tensor_copy(o, s),
                        inc="sv")
            slot_free[0] = tx3
            t_x_done[(c, ii)] = tx3

        for ii in range(8):
            t_sct[(c, ii)] = p.add("scalar", lambda e, s=xfull_sl(ii):
                                   e.activation(s, s, AF.Sqrt, scale=0.25),
                                   waits=[t_x_done[(c, ii)]], inc="sc")

        for i0 in range(0, 8, 2):
            prs = [i0, i0 + 1]
            bsl = {ii: AFr[:, (ii % 2) * 512:(ii % 2) * 512 + 512] for ii in prs}
            hsl = {ii: AFr[:, 1024 + (ii % 2) * 512:1024 + (ii % 2) * 512 + 512] for ii in prs}
            asl = {ii: AFr[:, 2048 + (ii % 2) * 512:2048 + (ii % 2) * 512 + 512] for ii in prs}
            for ii in prs:
                p.add("vector", lambda e, o=bsl[ii], u=ufull_sl(ii), s=xfull_sl(ii):
                      e.tensor_mul(o, u, s),
                      waits=[t_sct[(c, ii)], t_u_done[(c, ii)]])
            for ii in prs:
                p.add("vector", lambda e, o=hsl[ii], a=afull_sl(ii), b=bsl[ii],
                      i0_=sm["hcar"][:, ii:ii + 1]:
                      e.tensor_tensor_scan(o, a, b, i0_, OP.mult, OP.add))
            for ii in prs:
                p.add("vector", lambda e, o=asl[ii], a=afull_sl(ii),
                      i0_=sm["apcar"][:, ii:ii + 1]:
                      e.tensor_tensor_scan(o, a, a, i0_, OP.mult, OP.bypass))
            for ii in prs:
                p.add("vector", lambda e, o=hloc_sl(ii, c), i=hsl[ii]: e.tensor_copy(o, i))
            for ii in prs:
                p.add("vector", lambda e, o=apb_sl(ii, c), i=asl[ii]: e.tensor_copy(o, i))
            for ii in prs:
                p.add("vector", lambda e, o=sm["hcar"][:, ii:ii + 1], i=hsl[ii][:, 511:512]:
                      e.tensor_copy(o, i))
            for ii in prs:
                p.add("vector", lambda e, o=sm["apcar"][:, ii:ii + 1], i=asl[ii][:, 511:512]:
                      e.tensor_copy(o, i), inc=("sv" if ii == i0 + 1 else None))

    t_lastscan = vgap()
    ag1i3 = ag1_in.ap().rearrange("r (p ii) -> r p ii", ii=8)
    tg1 = p.add("gpsimd", lambda e: e.dma_start(out=ag1i3[0], in_=sm["apcar"][:, :]),
                waits=[t_lastscan], inc="sg", n=16)
    tg2 = p.add("gpsimd", lambda e: e.dma_start(out=ag1i3[1], in_=sm["hcar"][:, :]),
                inc="sg", n=16)
    t_ag1 = p.add("gpsimd", lambda e: e.collective_compute(
        "AllGather", OP.bypass, replica_groups=[list(range(CORES))],
        ins=[ag1_in.ap().opt()], outs=[ag1_out.ap().opt()]),
        waits=[tg1, tg2], inc="scc")
    ag1o3 = ag1_out.ap().rearrange("(r two) (p ii) -> two p r ii", two=2, ii=8)
    tga = p.add("gpsimd", lambda e: e.dma_start(
        out=sm["asum"].rearrange("p (r ii) -> p r ii", ii=8), in_=ag1o3[0]),
        waits=[t_ag1], inc="sg", n=16)
    tge = p.add("gpsimd", lambda e: e.dma_start(
        out=sm["esum"].rearrange("p (r ii) -> p r ii", ii=8), in_=ag1o3[1]),
        waits=[tga], inc="sg", n=16)
    p.add("vector", lambda e: e.tensor_copy(sm["cpre"][:, 0:8], sm["esum"][:, 0:8]),
          waits=[tge])
    for k in range(1, 7):
        spacer()
        p.add("vector", lambda e, k=k: e.tensor_mul(
            sm["tmpc"][:, :], sm["asum"][:, k * 8:(k + 1) * 8],
            sm["cpre"][:, (k - 1) * 8:k * 8]))
        spacer()
        p.add("vector", lambda e, k=k: e.tensor_add(
            sm["cpre"][:, k * 8:(k + 1) * 8], sm["tmpc"][:, :],
            sm["esum"][:, k * 8:(k + 1) * 8]))
    for k in range(7):
        spacer()
        spacer()
        p.add("vector", lambda e, k=k: e.scalar_tensor_tensor(
            sm["cin"][:, :], sm["cpre"][:, k * 8:(k + 1) * 8],
            sm["selc"][:, k:k + 1], sm["cin"][:, :], OP.mult, OP.add))
    spacer()
    spacer()
    t_cin = p.add("vector", lambda e: e.tensor_copy(sm["dmy"][:, :], sm["cin"][:, :]),
                  inc="sv")

    rot3 = [PA2, PSP]
    rot3_evac = [rot_evac[2], rot_evac[3]]
    pa0_free = rot_evac[0]
    pa1_free = rot_evac[1]
    t_glog_mm = T0
    t_agq = [None] * NQ
    t_x1dma_prev = ("sg", 0)
    t_zev = T0
    t_wout_mm = T0
    t_hf = []
    tct = T0

    for c in range(NCHUNK):
        src3 = xT.rearrange("(ki p) t -> p ki t", p=128)[:, :, c * TC:(c + 1) * TC]
        dst3 = AX[:, 0:4096].rearrange("p (ki t) -> p ki t", ki=8)
        t_x0r = p.add("gpsimd", lambda e, o=dst3, i=src3: e.dma_start(out=o, in_=i),
                      waits=[t_lastscan] + ([t_zev] if c > 0 else []), inc="sg", n=16)
        t_hf = []
        for ii in range(8):
            t_hf.append(p.add("vector", lambda e, o=hfix_sl(ii), a=apb_sl(ii, c),
                              s=sm["cin"][:, ii:ii + 1], h=hloc_sl(ii, c):
                              e.scalar_tensor_tensor(o, a, s, h, OP.mult, OP.add),
                              waits=[t_cin] + ([t_wout_mm] if c > 0 else []), inc="sv"))
        t_hf_g = vgap()
        t_zs = []
        for ii in range(8):
            r = ii % 2
            tok = None
            for ki in range(8):
                tok = p.add("tensor", lambda e, ps=rot3[r], l=aw1_w(24576, ki, ii),
                            rr=hfix_sl(ki), s=(ki == 0), t=(ki == 7):
                            e.matmul(ps[:, :], l, rr, start=s, stop=t),
                            waits=([t_hf_g, t_w["out"], rot3_evac[r]] if ki == 0 else []),
                            inc=("st" if ki == 7 else None))
            tz = p.add("vector", lambda e, ps=rot3[r], o=z_sl(ii),
                       b=sm["b_out"][:, ii:ii + 1], x0=x0b_sl(ii):
                       e.scalar_tensor_tensor(o, ps[:, :], b, x0, OP.add, OP.add),
                       waits=[tok, t_x0r], inc="sv")
            rot3_evac[r] = tz
            t_zs.append(tz)
        t_wout_mm = p.tok("st")
        t_zev = t_zs[-1]
        t_slot_mm = [T0, T0]
        tok_sum = tok_sq = None
        for ii in range(8):
            k = ii % 2
            p.add("vector", lambda e, o=zb_slot(k), z=z_sl(ii): e.tensor_copy(o, z),
                  waits=[t_zs[ii], t_slot_mm[k]])
            p.add("vector", lambda e, o=zsq_slot(k), z=z_sl(ii): e.tensor_mul(o, z, z))
            tzb = tzq = vgap()
            tok_sum = p.add("tensor", lambda e, rr=zb_slot(k), s=(ii == 0), t=(ii == 7):
                            e.matmul(PA0[0:1, :], sm["onesc"][:, :], rr, start=s, stop=t),
                            waits=[tzb, t_ones] + ([pa0_free] if ii == 0 else []),
                            inc="st")
            tok_sq = p.add("tensor", lambda e, rr=zsq_slot(k), s=(ii == 0), t=(ii == 7):
                           e.matmul(PA1[0:1, :], sm["onesc"][:, :], rr, start=s, stop=t),
                           waits=[tzq] + ([pa1_free] if ii == 0 else []),
                           inc="st")
            t_slot_mm[k] = tok_sq
        trm = p.add("vector", lambda e: e.tensor_scalar_mul(rows(0), PA0[0:1, :], 1.0 / D),
                    waits=[tok_sum], inc="sv")
        p.add("vector", lambda e: e.tensor_scalar_mul(rows(1), PA1[0:1, :], 1.0 / D),
              waits=[tok_sq])
        spacer()
        p.add("vector", lambda e: e.tensor_mul(rows(2), rows(0), rows(0)))
        spacer()
        p.add("vector", lambda e: e.tensor_sub(rows(1), rows(1), rows(2)))
        spacer()
        spacer()
        tvar = p.add("vector", lambda e: e.tensor_copy(sm["dmy"][:, :], sm["tmpc"][:, :]),
                     inc="sv")
        tsq = p.add("scalar", lambda e: e.activation(rows(2), rows(1), AF.Sqrt, bias=sm["epsc"][0:1, 0:1]),
                    waits=[tvar], inc="sc")
        trc = p.add("vector", lambda e: e.reciprocal(rows(3), rows(2)), waits=[tsq],
                    inc="sv")
        spacer()
        spacer()
        p.add("vector", lambda e: e.tensor_mul(rows(2), rows(0), rows(3)))
        tmr = vgap()
        tbr = p.add("tensor", lambda e: e.matmul(PA1[:, :], sm["onesf"][:, :], rows(3),
                                                 start=True, stop=True),
                    waits=[tmr, tvar, t_ones], inc="st")
        tbm = p.add("tensor", lambda e: e.matmul(PA0[:, :], sm["onesf"][:, :], rows(2),
                                                 start=True, stop=True),
                    waits=[tmr, trm], inc="st")
        tx1last = None
        for ii in range(8):
            p.add("vector", lambda e, z=z_sl(ii): e.tensor_mul(x1tmp(), z, PA1[:, :]),
                  waits=([tbr, tbm, t_glog_mm, t_x1dma_prev] if ii == 0 else []))
            p.add("vector", lambda e, z=z_sl(ii): e.tensor_sub(z, x1tmp(), PA0[:, :]))
            p.add("vector", lambda e, o=x1c_sl(ii), z=z_sl(ii): e.tensor_copy(o, z))
            tx1last = None
        tx1last = vgap()
        pa0_free = tx1last
        pa1_free = tx1last
        t_x1dma_prev = p.add("gpsimd", lambda e, c=c: e.dma_start(
            out=agq_in[c].ap()[0:D, :].rearrange("(ki p) t -> p ki t", p=128),
            in_=AB[:, 4096:8192].rearrange("p (ki t) -> p ki t", ki=8)),
            waits=[tx1last], inc="sg", n=16)
        tok = None
        for ki in range(8):
            tok = p.add("tensor", lambda e, l=sm["gwtf"][:, ki * 8:(ki + 1) * 8],
                        rr=z_sl(ki), s=(ki == 0), t=(ki == 7):
                        e.matmul(PSP[0:8, :], l, rr, start=s, stop=t),
                        waits=([tx1last, t_gwtf, rot3_evac[1]] if ki == 0 else []),
                        inc=("st" if ki == 7 else None))
        t_glog_mm = tok
        tgl = p.add("vector", lambda e: e.tensor_copy(glog_sl(), PSP[0:8, :]),
                    waits=[tok], inc="sv")
        rot3_evac[1] = tgl
        tprev = rot3_evac[0]
        for j in range(4):
            ttr = p.add("tensor", lambda e, j=j: e.transpose(
                PA2[:, 0:8], glog_sl()[0:8, j * 128:(j + 1) * 128],
                sm["idn"][0:8, 0:8]),
                waits=[tgl, t_idn, tprev], inc="st")
            tprev = p.add("vector", lambda e, j=j: e.tensor_copy(
                sm["glt4"][:, j * 8:(j + 1) * 8], PA2[:, 0:8]),
                waits=[ttr], inc="sv")
        rot3_evac[0] = tprev
        for j in range(4):
            p.add("vector", lambda e, j=j: e.reduce_max(
                sm["m14"][:, j:j + 1], sm["glt4"][:, j * 8:(j + 1) * 8], axis=AX_X))
        spacer()
        spacer()
        for j in range(4):
            p.add("vector", lambda e, j=j: e.tensor_scalar(
                sm["mk4"][:, j * 8:(j + 1) * 8], sm["glt4"][:, j * 8:(j + 1) * 8],
                sm["m14"][:, j:j + 1], 1.0, OP.is_equal, OP.mult))
        spacer()
        for j in range(4):
            p.add("vector", lambda e, j=j: e.scalar_tensor_tensor(
                sm["ms4"][:, j * 8:(j + 1) * 8], sm["mk4"][:, j * 8:(j + 1) * 8],
                -1e30, sm["glt4"][:, j * 8:(j + 1) * 8], OP.mult, OP.add))
        spacer()
        for j in range(4):
            p.add("vector", lambda e, j=j: e.reduce_max(
                sm["m24"][:, j:j + 1], sm["ms4"][:, j * 8:(j + 1) * 8], axis=AX_X))
        spacer()
        spacer()
        p.add("vector", lambda e: e.tensor_sub(sm["dd4"], sm["m24"],
                                               sm["m14"]))
        spacer()
        spacer()
        tdd = p.add("vector", lambda e: e.tensor_copy(sm["dmy"][:, :], sm["tmpc"][:, :]),
                    inc="sv")
        tth = p.add("scalar", lambda e: e.activation(sm["th4"], sm["dd4"],
                                                     AF.Tanh, scale=0.5),
                    waits=[tdd], inc="sc")
        p.add("vector", lambda e: e.tensor_scalar(sm["g24"], sm["th4"],
                                                  0.5, 0.5, OP.mult, OP.add),
              waits=[tth])
        p.add("vector", lambda e: e.tensor_scalar(sm["g14"], sm["th4"],
                                                  -0.5, 0.5, OP.mult, OP.add))
        spacer()
        spacer()
        for j in range(4):
            p.add("vector", lambda e, j=j: e.tensor_scalar(
                sm["comb4"][:, j * 8:(j + 1) * 8], sm["glt4"][:, j * 8:(j + 1) * 8],
                sm["m14"][:, j:j + 1], sm["g14"][:, j:j + 1], OP.is_equal, OP.mult))
        spacer()
        for j in range(4):
            p.add("vector", lambda e, j=j: e.tensor_scalar(
                sm["c24"][:, j * 8:(j + 1) * 8], sm["ms4"][:, j * 8:(j + 1) * 8],
                sm["m24"][:, j:j + 1], sm["g24"][:, j:j + 1], OP.is_equal, OP.mult))
        spacer()
        p.add("vector", lambda e: e.tensor_add(sm["comb4"][:, :], sm["comb4"][:, :],
                                               sm["c24"][:, :]))
        spacer()
        spacer()
        tcb = p.add("vector", lambda e: e.tensor_copy(sm["cmtok0"][:, :], sm["comb4"][:, :]),
                    inc="sv")
        tct = vgap()
        tcdma = p.add("gpsimd", lambda e, c=c: e.dma_start(
            out=agq_in[c].ap()[D:D + 8, :].rearrange(
                "r t -> (r t)").rearrange("(p c) -> p c", p=128),
            in_=sm["cmtok0"][:, :]),
            waits=[tct], inc="sg", n=16)
        t_agq[c] = p.add("gpsimd", lambda e, c=c: e.collective_compute(
            "AllGather", OP.bypass, replica_groups=[list(range(CORES))],
            ins=[agq_in[c].ap().opt()], outs=[agq_out[c].ap().opt()]),
            waits=[tcdma, t_x1dma_prev], inc="scc")

    t_tensorA_done = p.tok("st")

    tw1 = p.add("gpsimd", lambda e: e.dma_start(
        out=AW1.ap().rearrange("p (ki f) -> p ki f", ki=8),
        in_=w1T.rearrange("(ki p) f -> p ki f", p=128)),
        waits=[t_tensorA_done], inc="sg", n=16)
    tw2 = p.add("gpsimd", lambda e: e.dma_start(
        out=AW2.ap().rearrange("p (fk d) -> p fk d", fk=32),
        in_=w2T.rearrange("(fk p) d -> p fk d", p=128)),
        waits=[t_hf[-1]], inc="sg", n=16)

    h1_rot = [PA0, PA1, PSP]
    h1_evac = [pa0_free, pa1_free, rot3_evac[1]]
    h1_mm = [T0, T0, T0]
    pyb = [PY0, PY1]
    py_evac = [T0, T0]
    t_xg = {}
    t_c8 = {}
    t_xw = {}
    wsel_done = {}
    t_wsel_read = {}
    t_cmtok_free = tw1
    h1_mm_last = {-1: T0, -2: T0}
    last_rs_dma = [("sg", 0), ("sg", 0)]
    t_out_count = 0
    hro = 0
    pyo = 0
    last_ln_dma = [("so", 0), ("so", 0)]
    last_ln_read = [T0, T0]
    t_rs_q = {}
    NCI = NQ * CORES

    def xbuf(ci):
        par = ci % 2
        return lambda ki: AX[:, par * 4096 + ki * TB:par * 4096 + (ki + 1) * TB]

    def issue_loads(ci):
        lc, r = ci // CORES, ci % CORES
        par = ci % 2
        xs = xbuf(ci)
        t_xg[ci] = p.add("sync", lambda e, r=r, lc=lc, par=par: e.dma_start(
            out=AX[:, par * 4096:par * 4096 + 4096].rearrange("p (ki t) -> p ki t", ki=8),
            in_=agq_out[lc].ap()[r * (D + 8):r * (D + 8) + D, :].rearrange(
                "(ki p) t -> p ki t", p=128)),
            waits=[t_agq[lc], t_zev, h1_mm_last.get(ci - 2, T0)] +
                  ([t_tensorA_done] if ci < 2 else []), inc="ss", n=16)
        if r == 0:
            src3 = agq_out[lc].ap().rearrange(
                "(rk row) t -> rk row t", row=D + 8)[:, D:D + 8, :].rearrange(
                "rk row t -> rk (row t)").rearrange(
                "rk (p c) -> p rk c", p=128)
            t_c8[ci] = p.add("sync", lambda e, s3=src3: e.dma_start(
                out=sm["cmquar"].ap().rearrange("p (rk c) -> p rk c", c=32),
                in_=s3),
                waits=[t_agq[lc], wsel_done.get(ci - 1, T0)] +
                      ([t_cmtok_free] if ci < 2 else []), inc="ss", n=16)
        else:
            t_c8[ci] = t_c8[ci - 1]

    def issue_wb_xw(ci):
        par = ci % 2
        r_ = ci % CORES
        for tt in range(4):
            p.add("vector", lambda e, tt=tt, r_=r_: e.tensor_mul(
                sm["mk4"][:, tt * 8:(tt + 1) * 8],
                sm["cmquar"][:, r_ * 32 + tt * 8:r_ * 32 + (tt + 1) * 8],
                sm["ohm"]),
                waits=([t_c8[ci], t_wsel_read.get(ci - 2, T0)] if tt == 0 else []))
        for tt in range(4):
            p.add("vector", lambda e, tt=tt, par=par: e.reduce_sum(
                sm["wsel"][:, par * 4 + tt:par * 4 + tt + 1],
                sm["mk4"][:, tt * 8:(tt + 1) * 8], axis=AX_X))
        wsel_done[ci] = vgap()
        t_xw[ci] = t_xg[ci]

    def do_ln2(lc):
        nonlocal t_out_count
        for tp_ in range(2):
            tin = {}
            tz2 = {}
            tmv = {}
            trc2 = {}
            for u in range(2):
                tt = tp_ * 2 + u
                tx1t = p.add("sync", lambda e, lc=lc, tt=tt, u=u: e.dma_start(
                    out=x1T_sl(u), in_=agq_in[lc].ap()[0:D, tt * 128:(tt + 1) * 128],
                    transpose=True),
                    waits=[t_agq[lc], last_ln_dma[u], last_ln_read[u]], inc="ss", n=16)
                tmo = p.add("sync", lambda e, lc=lc, tt=tt, u=u: e.dma_start(
                    out=moe_sl(u), in_=rs_out[lc].ap()[tt * 128:(tt + 1) * 128, :]),
                    waits=[t_rs_q[lc], last_ln_dma[u], last_ln_read[u]], inc="ss", n=16)
                tin[u] = (tx1t, tmo)
            for u in range(2):
                tz2[u] = p.add("vector", lambda e, u=u: e.tensor_add(
                    z2_sl(u), x1T_sl(u), moe_sl(u)),
                    waits=list(tin[u]), inc="sv")
                last_ln_read[u] = tz2[u]
            for u in range(2):
                p.add("vector", lambda e, u=u: e.bn_stats(
                    sm["st12"][:, u * 12:u * 12 + 6], z2_sl(u)[:, 0:512]),
                    waits=[tz2[u]])
                p.add("vector", lambda e, u=u: e.bn_stats(
                    sm["st12"][:, u * 12 + 6:u * 12 + 12], z2_sl(u)[:, 512:1024]))
            for u in range(2):
                tmv[u] = p.add("vector", lambda e, u=u: e.bn_aggr(
                    sm["mv2"][:, u * 2:u * 2 + 2], sm["st12"][:, u * 12:u * 12 + 12]),
                    inc="sv")
            spacer()
            spacer()
            tmvg = p.add("vector", lambda e: e.tensor_copy(sm["dmy"][:, :], sm["tmpc"][:, :]),
                         inc="sv")
            tmv = {0: tmvg, 1: tmvg}
            for u in range(2):
                tsq2 = p.add("scalar", lambda e, u=u: e.activation(
                    sm["rstd2"][:, u:u + 1], sm["mv2"][:, u * 2 + 1:u * 2 + 2],
                    AF.Sqrt, bias=sm["epsc"][:, 0:1]), waits=[tmv[u]], inc="sc")
                trc2[u] = tsq2
            for u in range(2):
                trc2[u] = p.add("vector", lambda e, u=u: e.reciprocal(
                    sm["rstd2"][:, u:u + 1], sm["rstd2"][:, u:u + 1]),
                    waits=[trc2[u]], inc="sv")
            spacer()
            spacer()
            touts = {}
            for u in range(2):
                p.add("vector", lambda e, u=u: e.tensor_scalar(
                    z2_sl(u), z2_sl(u), sm["mv2"][:, u * 2:u * 2 + 1],
                    sm["rstd2"][:, u:u + 1], OP.subtract, OP.mult),
                    waits=[trc2[u]])
            tog = vgap()
            for u in range(2):
                touts[u] = tog
            for u in range(2):
                tt = tp_ * 2 + u
                last_ln_dma[u] = p.add("sync", lambda e, lc=lc, tt=tt, u=u: e.dma_start(
                    out=out_d[lc * TC + tt * 128:(lc * TC + (tt + 1) * 128), :],
                    in_=z2_sl(u)), waits=[touts[u]], inc="so", n=16)
                t_out_count += 16

    issue_loads(0)
    issue_loads(1)
    issue_wb_xw(0)
    for ci in range(NCI):
        lc, r = ci // CORES, ci % CORES
        xs = xbuf(ci)
        for fi in range(FI):
            rr_ = hro % 3
            hro += 1
            tok = None
            for ki in range(8):
                tok = p.add("tensor", lambda e, ps=h1_rot[rr_], l=aw1_w1(ki, fi),
                            rhs=xs(ki), s=(ki == 0), t=(ki == 7):
                            e.matmul(ps[:, :], l, rhs, start=s, stop=t),
                            waits=([t_xw[ci], tw1, h1_evac[rr_]] if ki == 0 else []),
                            inc=("st" if ki == 7 else None))
            h1_mm[rr_] = tok
            if fi % 2 == 0:
                t_ev = p.add("vector", lambda e, ps=h1_rot[rr_], o=h1s_sl(fi, 0, TB):
                             e.tensor_scalar(o, ps[:, :], 0.0, None, OP.max),
                             waits=[tok], inc="sv")
            else:
                t_ev = p.add("scalar", lambda e, ps=h1_rot[rr_], o=h1s_sl(fi, 0, TB):
                             e.activation(o, ps[:, :], AF.Relu),
                             waits=[tok], inc="sc")
            h1_evac[rr_] = t_ev
        h1_mm_last[ci] = h1_mm[(hro - 1) % 3]
        h1s_done_v = vgap()
        h1s_done_c = p.add("scalar", lambda e: e.activation(
            sm["dmy"][:, :], sm["tmpc"][:, :], AF.Copy), inc="sc")
        if ci + 2 < NCI:
            issue_loads(ci + 2)
        if ci + 1 < NCI:
            issue_wb_xw(ci + 1)
        for tt in range(4):
            pb = pyo % 2
            pyo += 1
            tok = None
            for half in range(2):
                for fk in range(FI):
                    is_first = (half == 0 and fk == 0)
                    is_last = (half == 1 and fk == FI - 1)
                    tok = p.add("tensor", lambda e, ps=pyb[pb], h=half,
                                l=h1s_sl(fk, tt * 128, 128),
                                rhs=w2_rhs(fk, half), s=(fk == 0), t=(fk == FI - 1):
                                e.matmul(ps[:, h * 512:(h + 1) * 512], l, rhs,
                                         start=s, stop=t),
                                waits=([h1s_done_v, h1s_done_c, tw2, py_evac[pb]]
                                       if is_first else []),
                                inc=("st" if is_last else None))
            spar = tt % 2
            par = ci % 2
            p.add("vector", lambda e, ps=pyb[pb], o=rsst_sl(spar), tt=tt, par=par:
                  e.tensor_scalar(o, ps[:, :], sm["wsel"][:, par * 4 + tt:par * 4 + tt + 1],
                                  None, OP.mult),
                  waits=[tok, last_rs_dma[spar], wsel_done[ci]])
            tev = vgap()
            py_evac[pb] = tev
            t_wsel_read[ci] = tev
            last_rs_dma[spar] = p.add(
                "gpsimd", lambda e, q=lc, r=r, tt=tt, spar=spar:
                e.dma_start(out=rs_in[q].ap()[r * TC + tt * 128:
                                              r * TC + (tt + 1) * 128, :],
                            in_=rsst_sl(spar)),
                waits=[tev], inc="sg", n=16)
        if r == CORES - 1:
            t_rs_q[lc] = p.add("gpsimd", lambda e, q=lc: e.collective_compute(
                "ReduceScatter", OP.add, replica_groups=[list(range(CORES))],
                ins=[rs_in[q].ap().opt()], outs=[rs_out[q].ap().opt()]),
                waits=[last_rs_dma[0], last_rs_dma[1]], inc="scc")
        if r == 1 and lc > 0:
            do_ln2(lc - 1)
    do_ln2(NQ - 1)

    p.add("vector", lambda e: e.tensor_copy(sm["dmy"][:, :], sm["tmpc"][:, :]),
          waits=[("so", t_out_count)])

    with nc.Block() as block:
        p.emit(block)
    es.close()
    return nc


_NC_CACHE = None


def kernel(L0_attn_x, L0_attn_W_in, L0_attn_b_in, L0_attn_W_gx, L0_attn_b_gx,
           L0_attn_W_ga, L0_attn_b_ga, L0_attn_a_param, L0_attn_W_out,
           L0_attn_b_out, L0_ffn_gate_w, L0_ffn_w1, L0_ffn_w2):
    global _NC_CACHE
    f = np.float32
    x = np.asarray(L0_attn_x, f)
    xT = np.ascontiguousarray(x[0].T)

    def colmajor8(v):
        return np.ascontiguousarray(np.asarray(v, f).reshape(8, 128).T)

    W_in = np.ascontiguousarray(np.asarray(L0_attn_W_in, f))
    W_gx = np.ascontiguousarray(np.asarray(L0_attn_W_gx, f))
    W_ga = np.ascontiguousarray(np.asarray(L0_attn_W_ga, f))
    W_out = np.ascontiguousarray(np.asarray(L0_attn_W_out, f))
    gw = np.asarray(L0_ffn_gate_w, f)
    w1 = np.asarray(L0_ffn_w1, f)
    w2 = np.asarray(L0_ffn_w2, f)
    bins = colmajor8(L0_attn_b_in)
    bgxs = colmajor8(L0_attn_b_gx)
    bgas = colmajor8(L0_attn_b_ga)
    bouts = colmajor8(L0_attn_b_out)
    apars = colmajor8(L0_attn_a_param)
    idn = np.eye(128, dtype=f)
    gwp = np.ascontiguousarray(gw.T.reshape(8, 128, 8).transpose(1, 0, 2).reshape(128, 64))

    if _NC_CACHE is None:
        _NC_CACHE = build_nc()
    nc = _NC_CACHE

    in_maps = []
    for i in range(CORES):
        selc = np.zeros((128, 8), f)
        if i > 0:
            selc[:, i - 1] = 1.0
        ohm = np.tile((np.arange(8) == i).astype(f), (128, 1))
        smalls = np.concatenate([bins, bgxs, bgas, bouts, apars, selc, gwp, ohm,
                                 np.zeros((128, 8), f)], axis=1).astype(f)
        in_maps.append({
            "xT": np.ascontiguousarray(xT[:, i * TOK:(i + 1) * TOK]),
            "w_in": W_in, "w_gx": W_gx, "w_ga": W_ga, "w_out": W_out,
            "smalls_h": np.ascontiguousarray(smalls),
            "w1T": np.ascontiguousarray(w1[i].T),
            "w2T": np.ascontiguousarray(w2[i].T),
            "idn_h": idn,
        })

    res = run_bass_kernel_spmd(nc, in_maps, core_ids=list(range(CORES)))
    out = np.concatenate([res.results[i]["out"] for i in range(CORES)], axis=0)
    return out.reshape(1, S, D).astype(np.float32)


# version 06
# speedup vs baseline: 1.0655x; 1.0024x over previous
import numpy as np
from contextlib import ExitStack

import concourse.bass as bass
import concourse.mybir as mybir
from concourse.bass_utils import run_bass_kernel_spmd

F32 = mybir.dt.float32
BF16 = mybir.dt.bfloat16
AF = mybir.ActivationFunctionType
OP = mybir.AluOpType
AX_X = mybir.AxisListType.X

CORES = 8
D = 1024
S = 16384
TOK = S // CORES
TC = 512
NCHUNK = TOK // TC
FF = 4096
FI = FF // 128
TB = 512
NQ = 4
EPS = 1e-5


class P:

    ENGS = ("sync", "gpsimd", "vector", "scalar", "tensor")

    def __init__(self, nc, es):
        self.nc = nc
        self.ops = {e: [] for e in self.ENGS}
        self.sems = {}
        self.cnt = {}
        for name in ("sg", "ss", "sv", "sc", "st", "scc", "so"):
            self.sems[name] = es.enter_context(nc.semaphore(name))
            self.cnt[name] = 0

    def add(self, eng, fn, waits=(), inc=None, n=1):
        ws = tuple(w for w in waits if w is not None and w[1] > 0)
        self.ops[eng].append((fn, ws, inc, n))
        if inc is not None:
            self.cnt[inc] += n
            return (inc, self.cnt[inc])
        return None

    def tok(self, name):
        return (name, self.cnt[name])

    def _run(self, name, eng):
        seen = {}
        for fn, waits, inc, n in self.ops[name]:
            for (s, v) in waits:
                if seen.get(s, 0) < v:
                    eng.wait_ge(self.sems[s], v)
                    seen[s] = v
            instr = fn(eng)
            if inc is not None:
                instr.then_inc(self.sems[inc], n)

    def emit(self, block):
        runner = self._run

        @block.sync
        def _(e):
            runner("sync", e)

        @block.gpsimd
        def _(e):
            runner("gpsimd", e)

        @block.vector
        def _(e):
            runner("vector", e)

        @block.scalar
        def _(e):
            runner("scalar", e)

        @block.tensor
        def _(e):
            runner("tensor", e)


def build_nc():
    nc = bass.Bass("TRN2", target_bir_lowering=False)

    xT = nc.dram_tensor("xT", [D, TOK], F32, kind="ExternalInput").ap()
    w_in = nc.dram_tensor("w_in", [D, D], F32, kind="ExternalInput").ap()
    w_gx = nc.dram_tensor("w_gx", [D, D], F32, kind="ExternalInput").ap()
    w_ga = nc.dram_tensor("w_ga", [D, D], F32, kind="ExternalInput").ap()
    w_out = nc.dram_tensor("w_out", [D, D], F32, kind="ExternalInput").ap()
    smalls_h = nc.dram_tensor("smalls_h", [128, 128], F32, kind="ExternalInput").ap()
    w1T = nc.dram_tensor("w1T", [D, FF], F32, kind="ExternalInput").ap()
    w2T = nc.dram_tensor("w2T", [FF, D], F32, kind="ExternalInput").ap()
    idn_h = nc.dram_tensor("idn_h", [128, 128], F32, kind="ExternalInput").ap()
    out_d = nc.dram_tensor("out", [TOK, D], F32, kind="ExternalOutput").ap()

    ag1_in = nc.dram_tensor("ag1_in", [2, D], F32)
    ag1_out = nc.dram_tensor("ag1_out", [2 * CORES, D], F32)
    agq_in = [nc.dram_tensor(f"agq_in_{q}", [D + 8, TC], BF16) for q in range(NQ)]
    agq_out = [nc.dram_tensor(f"agq_out_{q}", [(D + 8) * CORES, TC], BF16) for q in range(NQ)]
    rs_in = [nc.dram_tensor(f"rs_in_{q}", [CORES * TC, D], BF16) for q in range(NQ)]
    rs_out = [nc.dram_tensor(f"rs_out_{q}", [TC, D], BF16) for q in range(NQ)]

    es = ExitStack()

    AW1 = es.enter_context(nc.sbuf_tensor("AW1", [128, 32768], BF16))
    AW2 = es.enter_context(nc.sbuf_tensor("AW2", [128, 32768], BF16))
    AX = es.enter_context(nc.sbuf_tensor("AX", [128, 8192], BF16))
    AB = es.enter_context(nc.sbuf_tensor("AB", [128, 14336], BF16))
    AFr = es.enter_context(nc.sbuf_tensor("AF", [128, 8192], F32))
    AFb = AFr.ap().bitcast(BF16)
    ABf = AB.ap().bitcast(F32)

    SML = es.enter_context(nc.sbuf_tensor("SML", [128, 128], F32))
    sm = {}
    sm["b_in"] = SML[:, 0:8]
    sm["bgx_hf"] = SML[:, 8:16]
    sm["bga_hf"] = SML[:, 16:24]
    sm["b_out"] = SML[:, 24:32]
    sm["aptmp"] = SML[:, 32:40]
    sm["selc"] = SML[:, 40:48]
    sm["gwtf"] = SML[:, 48:112]
    sm["ohm"] = SML[:, 112:120]
    for nm, shp, dt in [
        ("msp_h", [128, 8], F32),
        ("hcar", [128, 8], F32), ("apcar", [128, 8], F32),
        ("cin", [128, 8], F32), ("tmpc", [128, 8], F32), ("dmy", [128, 8], F32),
        ("aesum", [128, 128], F32), ("cpre", [128, 64], F32),
        ("idn", [128, 128], F32),
        ("onesc", [128, 1], BF16), ("onesf", [1, 128], F32),
        ("glt4", [128, 32], F32), ("mk4", [128, 32], F32),
        ("ms4", [128, 32], F32), ("comb4", [128, 32], F32),
        ("st12", [128, 24], F32), ("mv2", [128, 4], F32), ("rstd2", [128, 2], F32),
        ("cmtok0", [128, 32], BF16), ("cmquar", [128, 256], BF16),
        ("wsel", [128, 8], F32),
        ("epsc", [128, 1], F32), ("onec", [128, 1], F32), ("spt", [128, 8], F32),
    ]:
        sm[nm] = es.enter_context(nc.sbuf_tensor(nm, shp, dt))
    sm["asum"] = sm["aesum"][:, 0:64]
    sm["esum"] = sm["aesum"][:, 64:128]
    sm["c24"] = sm["mk4"]
    sm["m14"] = sm["st12"][:, 0:4]
    sm["m24"] = sm["st12"][:, 4:8]
    sm["dd4"] = sm["st12"][:, 8:12]
    sm["th4"] = sm["st12"][:, 12:16]
    sm["g14"] = sm["st12"][:, 16:20]
    sm["g24"] = sm["st12"][:, 20:24]

    PA0 = es.enter_context(nc.psum_tensor("PA0", [128, 512], F32))
    PA1 = es.enter_context(nc.psum_tensor("PA1", [128, 512], F32))
    PA2 = es.enter_context(nc.psum_tensor("PA2", [128, 512], F32))
    PSP = es.enter_context(nc.psum_tensor("PSP", [128, 512], F32))
    PY0 = es.enter_context(nc.psum_tensor("PY0", [128, 1024], F32))
    PY1 = es.enter_context(nc.psum_tensor("PY1", [128, 1024], F32))

    p = P(nc, es)
    T0 = ("sv", 0)

    def aw1_w(base, ki, ii):
        c = base + ki * 1024 + ii * 128
        return AW1[:, c:c + 128]

    def aw1_w1(ki, fi):
        c = ki * 4096 + fi * 128
        return AW1[:, c:c + 128]

    def hloc_sl(ii, c):
        a = ii * TOK + c * TC
        return AW2[:, a:a + TC]

    def apb_sl(ii, c):
        a = 16384 + ii * TOK + c * TC
        return AW2[:, a:a + TC]

    def w2_rhs(fk, half):
        a = fk * 1024 + half * 512
        return AW2[:, a:a + 512]

    def x0b_sl(ki):
        return AX[:, ki * TC:(ki + 1) * TC]

    def xinb_sl(ii):
        return AX[:, 4096 + ii * TC:4096 + (ii + 1) * TC]

    def xinf_sl(ii):
        return AFr[:, ii * TC:(ii + 1) * TC]

    def afull_sl(ii):
        return AFr[:, 4096 + ii * TC:4096 + (ii + 1) * TC]

    def slot(k):
        return ABf[:, 6144 + k * 512:6144 + (k + 1) * 512]

    def ufull_sl(ii):
        return AB[:, ii * TC:(ii + 1) * TC]

    def xfull_sl(ii):
        return AB[:, 8192 + ii * 512:8192 + (ii + 1) * 512]

    def hfix_sl(ii):
        return AB[:, ii * TC:(ii + 1) * TC]

    def x1c_sl(ii):
        return AB[:, 4096 + ii * TC:4096 + (ii + 1) * TC]

    def zb_slot(k):
        return AB[:, 8192 + k * 512:8192 + (k + 1) * 512]

    def zsq_slot(k):
        return AB[:, 9216 + k * 512:9216 + (k + 1) * 512]

    def z_sl(ii):
        return AFr[:, ii * TC:(ii + 1) * TC]

    def rows(k):
        return AFr[0:1, 4096 + k * 512:4096 + (k + 1) * 512]

    def x1tmp():
        return AFr[:, 6144:6656]

    def glog_sl():
        return ABf[0:8, 5120:5632]

    def combT_sl(c):
        return AB[0:8, 12288 + c * TC:12288 + (c + 1) * TC]

    def xg_sl(ki):
        return AX[:, ki * TB:(ki + 1) * TB]

    def xw_sl(ki):
        return AX[:, 4096 + ki * TB:4096 + (ki + 1) * TB]

    def h1s_sl(fk, off, ln):
        a = fk * TB + off
        return AFb[:, a:a + ln]

    def x1T_sl(par):
        return AB[:, par * 1024:(par + 1) * 1024]

    def moe_sl(par):
        return AB[:, 2048 + par * 1024:2048 + (par + 1) * 1024]

    def rsst_sl(par):
        return AB[:, 4096 + par * 1024:4096 + (par + 1) * 1024]

    def wb_sl(par):
        return AB[:, 6144 + par * 512:6144 + (par + 1) * 512]

    def cm8_sl(par):
        return AB[0:8, 7168 + par * 512:7168 + (par + 1) * 512]

    def cm8_full(par):
        return AB[:, 7168 + par * 512:7168 + (par + 1) * 512]

    def z2_sl(par):
        return ABf[:, 4096 + par * 1024:4096 + (par + 1) * 1024]

    def spacer():
        p.add("vector", lambda e: e.tensor_copy(sm["dmy"][:, :], sm["tmpc"][:, :]))

    def vgap():
        spacer()
        return p.add("vector", lambda e: e.tensor_copy(sm["dmy"][:, :], sm["tmpc"][:, :]),
                     inc="sv")

    t_w = {}
    for nm, src, base in [("in", w_in, 0), ("gx", w_gx, 8192),
                          ("ga", w_ga, 16384), ("out", w_out, 24576)]:
        src3 = src.rearrange("(ki p) c -> p ki c", p=128)
        dst3 = AW1[:, base:base + 8192].rearrange("p (ki c) -> p ki c", ki=8)
        t_w[nm] = p.add("gpsimd",
                        lambda e, o=dst3, i=src3: e.dma_start(out=o, in_=i),
                        inc="sg", n=16)


    t_small = p.add("sync", lambda e: e.dma_start(out=SML[:, :], in_=smalls_h),
                    inc="ss", n=16)
    t_idn = p.add("sync", lambda e: e.dma_start(out=sm["idn"][:, :], in_=idn_h),
                  waits=[t_small], inc="ss", n=16)
    t_gwtf = t_small
    small_loads = [t_small] * 6
    t_sml = t_small

    p.add("vector", lambda e: e.memset(sm["hcar"][:, :], 0.0))
    p.add("vector", lambda e: e.memset(sm["apcar"][:, :], 1.0))
    p.add("vector", lambda e: e.memset(sm["cin"][:, :], 0.0))
    p.add("vector", lambda e: e.memset(sm["tmpc"][:, :], 0.0))
    p.add("vector", lambda e: e.memset(sm["onesc"][:, :], 1.0))
    p.add("vector", lambda e: e.memset(sm["epsc"][:, :], EPS))
    p.add("vector", lambda e: e.memset(sm["onec"][:, :], 1.0))
    t_ones = p.add("vector", lambda e: e.memset(sm["onesf"][:, :], 1.0), inc="sv")
    def sgap(n=8):
        for _ in range(n):
            p.add("scalar", lambda e: e.activation(sm["dmy"][:, :], sm["tmpc"][:, :],
                                                   AF.Copy))
    ta1 = p.add("scalar", lambda e: e.activation(
        sm["spt"][:, :], sm["aptmp"][:, :], AF.Abs),
        waits=[small_loads[4]], inc="sc")
    t_r = p.add("scalar", lambda e: e.activation(
        sm["msp_h"][:, :], sm["aptmp"][:, :], AF.Relu), inc="sc")
    sgap()
    t_e = p.add("scalar", lambda e: e.activation(
        sm["aptmp"][:, :], sm["spt"][:, :], AF.Exp, scale=-1.0), inc="sc")
    sgap()
    t_l = p.add("scalar", lambda e: e.activation(
        sm["spt"][:, :], sm["aptmp"][:, :], AF.Ln, bias=sm["onec"][:, 0:1]),
        waits=[t_ones], inc="sc")
    sgap()
    tad = p.add("vector", lambda e: e.tensor_add(
        sm["msp_h"][:, :], sm["msp_h"][:, :], sm["spt"][:, :]),
        waits=[t_l, t_r], inc="sv")
    spacer()
    spacer()
    t_msp = p.add("vector", lambda e: e.tensor_scalar_mul(
        sm["msp_h"][:, :], sm["msp_h"][:, :], -4.0), waits=[tad], inc="sv")
    t_bgxh = p.add("vector", lambda e: e.tensor_scalar_mul(
        sm["bgx_hf"][:, :], sm["bgx_hf"][:, :], 0.5), waits=[small_loads[1]], inc="sv")
    t_bgah = p.add("vector", lambda e: e.tensor_scalar_mul(
        sm["bga_hf"][:, :], sm["bga_hf"][:, :], 0.5), waits=[small_loads[2]], inc="sv")

    rotors = [PA0, PA1, PA2, PSP]
    rot_evac = [T0, T0, T0, T0]
    rot_i = 0

    def next_rot():
        nonlocal rot_i
        r = rot_i % 4
        rot_i += 1
        return r

    t_x0 = [None] * NCHUNK
    t_xin_mm_last = T0
    slot_free = [T0, T0]
    t_u_done = {}
    t_x_done = {}
    t_sct = {}

    for c in range(NCHUNK):
        wx = [t_xin_mm_last] if c > 0 else []
        src3 = xT.rearrange("(ki p) t -> p ki t", p=128)[:, :, c * TC:(c + 1) * TC]
        dst3 = AX[:, 0:4096].rearrange("p (ki t) -> p ki t", ki=8)
        t_x0[c] = p.add("gpsimd", lambda e, o=dst3, i=src3: e.dma_start(out=o, in_=i),
                        waits=wx, inc="sg", n=16)

        t_evac_xin = []
        for ii in range(8):
            r = next_rot()
            tok = None
            for ki in range(8):
                tok = p.add("tensor", lambda e, ps=rotors[r], l=aw1_w(0, ki, ii),
                            rr=x0b_sl(ki), s=(ki == 0), t=(ki == 7):
                            e.matmul(ps[:, :], l, rr, start=s, stop=t),
                            waits=([t_x0[c], t_w["in"], rot_evac[r]] if ki == 0 else []),
                            inc=("st" if ki == 7 else None))
            p.add("vector", lambda e, ps=rotors[r], o=xinf_sl(ii), b=sm["b_in"][:, ii:ii + 1]:
                  e.tensor_scalar_add(o, ps[:, :], b), waits=[tok, t_sml])
            e2 = p.add("vector", lambda e, ps=rotors[r], o=xinb_sl(ii), b=sm["b_in"][:, ii:ii + 1]:
                       e.tensor_scalar_add(o, ps[:, :], b), inc="sv")
            rot_evac[r] = e2
            t_evac_xin.append(e2)
        t_evac_xin_g = vgap()
        t_xin_mm_last = p.tok("st")

        for ii in range(8):
            sA = slot(0)
            sB = slot(1)
            rA = next_rot()
            tokA = None
            for ki in range(8):
                tokA = p.add("tensor", lambda e, ps=rotors[rA], l=aw1_w(16384, ki, ii),
                             rr=xinb_sl(ki), s=(ki == 0), t=(ki == 7):
                             e.matmul(ps[:, :], l, rr, start=s, stop=t),
                             waits=([t_evac_xin_g, t_w["ga"], rot_evac[rA]] if ki == 0 else []),
                             inc=("st" if ki == 7 else None))
            rB = next_rot()
            tokB = None
            for ki in range(8):
                tokB = p.add("tensor", lambda e, ps=rotors[rB], l=aw1_w(8192, ki, ii),
                             rr=xinb_sl(ki), s=(ki == 0), t=(ki == 7):
                             e.matmul(ps[:, :], l, rr, start=s, stop=t),
                             waits=([t_w["gx"], rot_evac[rB]] if ki == 0 else []),
                             inc=("st" if ki == 7 else None))
            ta = p.add("scalar", lambda e, ps=rotors[rA], o=sB, b=sm["bga_hf"][:, ii:ii + 1]:
                       e.activation(o, ps[:, :], AF.Tanh, bias=b, scale=0.5),
                       waits=[tokA, t_bgah, slot_free[1]], inc="sc")
            rot_evac[rA] = ta
            tb = p.add("scalar", lambda e, ps=rotors[rB], o=sA, b=sm["bgx_hf"][:, ii:ii + 1]:
                       e.activation(o, ps[:, :], AF.Tanh, bias=b, scale=0.5),
                       waits=[tokB, t_bgxh, slot_free[0]], inc="sc")
            rot_evac[rB] = tb
            tl = p.add("vector", lambda e, s=sB, m=sm["msp_h"][:, ii:ii + 1]:
                       e.tensor_scalar(s, s, m, m, OP.mult, OP.add),
                       waits=[ta, t_msp], inc="sv")
            tu = p.add("vector", lambda e, s=sA, o=ufull_sl(ii), xi=xinf_sl(ii):
                       e.scalar_tensor_tensor(o, s, 1.0, xi, OP.add, OP.mult),
                       waits=[tb], inc="sv")
            t_u_done[(c, ii)] = tu
            p.add("scalar", lambda e, s=sB, o=afull_sl(ii):
                  e.activation(o, s, AF.Exp), waits=[tu], inc="sc")
            tq = p.add("scalar", lambda e, s=sB, o=sA:
                       e.activation(o, s, AF.Exp, scale=2.0), waits=[tu], inc="sc")
            slot_free[1] = tq
            p.add("vector", lambda e, s=sA: e.tensor_scalar(
                s, s, -1.0, 1.0, OP.mult, OP.add), waits=[tq])
            p.add("vector", lambda e, s=sA: e.tensor_scalar_max(s, s, 0.0))
            tx3 = p.add("vector", lambda e, s=sA, o=xfull_sl(ii): e.tensor_copy(o, s),
                        inc="sv")
            slot_free[0] = tx3
            t_x_done[(c, ii)] = tx3

        for ii in range(8):
            t_sct[(c, ii)] = p.add("scalar", lambda e, s=xfull_sl(ii):
                                   e.activation(s, s, AF.Sqrt, scale=0.25),
                                   waits=[t_x_done[(c, ii)]], inc="sc")

        for i0 in range(0, 8, 2):
            prs = [i0, i0 + 1]
            bsl = {ii: AFr[:, (ii % 2) * 512:(ii % 2) * 512 + 512] for ii in prs}
            hsl = {ii: AFr[:, 1024 + (ii % 2) * 512:1024 + (ii % 2) * 512 + 512] for ii in prs}
            asl = {ii: AFr[:, 2048 + (ii % 2) * 512:2048 + (ii % 2) * 512 + 512] for ii in prs}
            for ii in prs:
                p.add("vector", lambda e, o=bsl[ii], u=ufull_sl(ii), s=xfull_sl(ii):
                      e.tensor_mul(o, u, s),
                      waits=[t_sct[(c, ii)], t_u_done[(c, ii)]])
            for ii in prs:
                p.add("vector", lambda e, o=hsl[ii], a=afull_sl(ii), b=bsl[ii],
                      i0_=sm["hcar"][:, ii:ii + 1]:
                      e.tensor_tensor_scan(o, a, b, i0_, OP.mult, OP.add))
            for ii in prs:
                p.add("vector", lambda e, o=asl[ii], a=afull_sl(ii),
                      i0_=sm["apcar"][:, ii:ii + 1]:
                      e.tensor_tensor_scan(o, a, a, i0_, OP.mult, OP.bypass))
            for ii in prs:
                p.add("vector", lambda e, o=hloc_sl(ii, c), i=hsl[ii]: e.tensor_copy(o, i))
            for ii in prs:
                p.add("vector", lambda e, o=apb_sl(ii, c), i=asl[ii]: e.tensor_copy(o, i))
            for ii in prs:
                p.add("vector", lambda e, o=sm["hcar"][:, ii:ii + 1], i=hsl[ii][:, 511:512]:
                      e.tensor_copy(o, i))
            for ii in prs:
                p.add("vector", lambda e, o=sm["apcar"][:, ii:ii + 1], i=asl[ii][:, 511:512]:
                      e.tensor_copy(o, i), inc=("sv" if ii == i0 + 1 else None))

    t_gates_done = p.tok("st")
    t_lastscan = vgap()
    ag1i3 = ag1_in.ap().rearrange("r (p ii) -> r p ii", ii=8)
    tg1 = p.add("gpsimd", lambda e: e.dma_start(out=ag1i3[0], in_=sm["apcar"][:, :]),
                waits=[t_lastscan], inc="sg", n=16)
    tg2 = p.add("gpsimd", lambda e: e.dma_start(out=ag1i3[1], in_=sm["hcar"][:, :]),
                inc="sg", n=16)
    t_ag1 = p.add("gpsimd", lambda e: e.collective_compute(
        "AllGather", OP.bypass, replica_groups=[list(range(CORES))],
        ins=[ag1_in.ap().opt()], outs=[ag1_out.ap().opt()]),
        waits=[tg1, tg2], inc="scc")
    ag1o3 = ag1_out.ap().rearrange("(r two) (p ii) -> two p r ii", two=2, ii=8)
    tga = p.add("gpsimd", lambda e: e.dma_start(
        out=sm["asum"].rearrange("p (r ii) -> p r ii", ii=8), in_=ag1o3[0]),
        waits=[t_ag1], inc="sg", n=16)
    tge = p.add("gpsimd", lambda e: e.dma_start(
        out=sm["esum"].rearrange("p (r ii) -> p r ii", ii=8), in_=ag1o3[1]),
        waits=[tga], inc="sg", n=16)
    p.add("vector", lambda e: e.tensor_copy(sm["cpre"][:, 0:8], sm["esum"][:, 0:8]),
          waits=[tge])
    for k in range(1, 7):
        spacer()
        p.add("vector", lambda e, k=k: e.tensor_mul(
            sm["tmpc"][:, :], sm["asum"][:, k * 8:(k + 1) * 8],
            sm["cpre"][:, (k - 1) * 8:k * 8]))
        spacer()
        p.add("vector", lambda e, k=k: e.tensor_add(
            sm["cpre"][:, k * 8:(k + 1) * 8], sm["tmpc"][:, :],
            sm["esum"][:, k * 8:(k + 1) * 8]))
    for k in range(7):
        spacer()
        spacer()
        p.add("vector", lambda e, k=k: e.scalar_tensor_tensor(
            sm["cin"][:, :], sm["cpre"][:, k * 8:(k + 1) * 8],
            sm["selc"][:, k:k + 1], sm["cin"][:, :], OP.mult, OP.add))
    spacer()
    spacer()
    t_cin = p.add("vector", lambda e: e.tensor_copy(sm["dmy"][:, :], sm["cin"][:, :]),
                  inc="sv")

    rot3 = [PA2, PSP]
    rot3_evac = [rot_evac[2], rot_evac[3]]
    pa0_free = rot_evac[0]
    pa1_free = rot_evac[1]
    t_glog_mm = T0
    t_agq = [None] * NQ
    t_x1dma_prev = ("sg", 0)
    t_zev = T0
    t_wout_mm = T0
    t_hf = []
    tct = T0

    for c in range(NCHUNK):
        src3 = xT.rearrange("(ki p) t -> p ki t", p=128)[:, :, c * TC:(c + 1) * TC]
        dst3 = AX[:, 0:4096].rearrange("p (ki t) -> p ki t", ki=8)
        t_x0r = p.add("gpsimd", lambda e, o=dst3, i=src3: e.dma_start(out=o, in_=i),
                      waits=[t_lastscan] + ([t_zev] if c > 0 else []), inc="sg", n=16)
        t_hf = []
        for ii in range(8):
            t_hf.append(p.add("vector", lambda e, o=hfix_sl(ii), a=apb_sl(ii, c),
                              s=sm["cin"][:, ii:ii + 1], h=hloc_sl(ii, c):
                              e.scalar_tensor_tensor(o, a, s, h, OP.mult, OP.add),
                              waits=[t_cin] + ([t_wout_mm] if c > 0 else []), inc="sv"))
        t_hf_g = vgap()
        t_zs = []
        for ii in range(8):
            r = ii % 2
            tok = None
            for ki in range(8):
                tok = p.add("tensor", lambda e, ps=rot3[r], l=aw1_w(24576, ki, ii),
                            rr=hfix_sl(ki), s=(ki == 0), t=(ki == 7):
                            e.matmul(ps[:, :], l, rr, start=s, stop=t),
                            waits=([t_hf_g, t_w["out"], rot3_evac[r]] if ki == 0 else []),
                            inc=("st" if ki == 7 else None))
            tz = p.add("vector", lambda e, ps=rot3[r], o=z_sl(ii),
                       b=sm["b_out"][:, ii:ii + 1], x0=x0b_sl(ii):
                       e.scalar_tensor_tensor(o, ps[:, :], b, x0, OP.add, OP.add),
                       waits=[tok, t_x0r], inc="sv")
            rot3_evac[r] = tz
            t_zs.append(tz)
        t_wout_mm = p.tok("st")
        t_zev = t_zs[-1]
        t_slot_mm = [T0, T0]
        tok_sum = tok_sq = None
        for ii in range(8):
            k = ii % 2
            p.add("vector", lambda e, o=zb_slot(k), z=z_sl(ii): e.tensor_copy(o, z),
                  waits=[t_zs[ii], t_slot_mm[k]])
            p.add("vector", lambda e, o=zsq_slot(k), z=z_sl(ii): e.tensor_mul(o, z, z))
            tzb = tzq = vgap()
            tok_sum = p.add("tensor", lambda e, rr=zb_slot(k), s=(ii == 0), t=(ii == 7):
                            e.matmul(PA0[0:1, :], sm["onesc"][:, :], rr, start=s, stop=t),
                            waits=[tzb, t_ones] + ([pa0_free] if ii == 0 else []),
                            inc="st")
            tok_sq = p.add("tensor", lambda e, rr=zsq_slot(k), s=(ii == 0), t=(ii == 7):
                           e.matmul(PA1[0:1, :], sm["onesc"][:, :], rr, start=s, stop=t),
                           waits=[tzq] + ([pa1_free] if ii == 0 else []),
                           inc="st")
            t_slot_mm[k] = tok_sq
        trm = p.add("vector", lambda e: e.tensor_scalar_mul(rows(0), PA0[0:1, :], 1.0 / D),
                    waits=[tok_sum], inc="sv")
        p.add("vector", lambda e: e.tensor_scalar_mul(rows(1), PA1[0:1, :], 1.0 / D),
              waits=[tok_sq])
        spacer()
        p.add("vector", lambda e: e.tensor_mul(rows(2), rows(0), rows(0)))
        spacer()
        p.add("vector", lambda e: e.tensor_sub(rows(1), rows(1), rows(2)))
        spacer()
        spacer()
        tvar = p.add("vector", lambda e: e.tensor_copy(sm["dmy"][:, :], sm["tmpc"][:, :]),
                     inc="sv")
        tsq = p.add("scalar", lambda e: e.activation(rows(2), rows(1), AF.Sqrt, bias=sm["epsc"][0:1, 0:1]),
                    waits=[tvar], inc="sc")
        trc = p.add("vector", lambda e: e.reciprocal(rows(3), rows(2)), waits=[tsq],
                    inc="sv")
        spacer()
        spacer()
        p.add("vector", lambda e: e.tensor_mul(rows(2), rows(0), rows(3)))
        tmr = vgap()
        tbr = p.add("tensor", lambda e: e.matmul(PA1[:, :], sm["onesf"][:, :], rows(3),
                                                 start=True, stop=True),
                    waits=[tmr, tvar, t_ones], inc="st")
        tbm = p.add("tensor", lambda e: e.matmul(PA0[:, :], sm["onesf"][:, :], rows(2),
                                                 start=True, stop=True),
                    waits=[tmr, trm], inc="st")
        tx1last = None
        for ii in range(8):
            p.add("vector", lambda e, z=z_sl(ii): e.tensor_mul(x1tmp(), z, PA1[:, :]),
                  waits=([tbr, tbm, t_glog_mm, t_x1dma_prev] if ii == 0 else []))
            p.add("vector", lambda e, z=z_sl(ii): e.tensor_sub(z, x1tmp(), PA0[:, :]))
            p.add("vector", lambda e, o=x1c_sl(ii), z=z_sl(ii): e.tensor_copy(o, z))
            tx1last = None
        tx1last = vgap()
        pa0_free = tx1last
        pa1_free = tx1last
        t_x1dma_prev = p.add("gpsimd", lambda e, c=c: e.dma_start(
            out=agq_in[c].ap()[0:D, :].rearrange("(ki p) t -> p ki t", p=128),
            in_=AB[:, 4096:8192].rearrange("p (ki t) -> p ki t", ki=8)),
            waits=[tx1last], inc="sg", n=16)
        tok = None
        for ki in range(8):
            tok = p.add("tensor", lambda e, l=sm["gwtf"][:, ki * 8:(ki + 1) * 8],
                        rr=z_sl(ki), s=(ki == 0), t=(ki == 7):
                        e.matmul(PSP[0:8, :], l, rr, start=s, stop=t),
                        waits=([tx1last, t_gwtf, rot3_evac[1]] if ki == 0 else []),
                        inc=("st" if ki == 7 else None))
        t_glog_mm = tok
        tgl = p.add("vector", lambda e: e.tensor_copy(glog_sl(), PSP[0:8, :]),
                    waits=[tok], inc="sv")
        rot3_evac[1] = tgl
        tprev = rot3_evac[0]
        for j in range(4):
            ttr = p.add("tensor", lambda e, j=j: e.transpose(
                PA2[:, 0:8], glog_sl()[0:8, j * 128:(j + 1) * 128],
                sm["idn"][0:8, 0:8]),
                waits=[tgl, t_idn, tprev], inc="st")
            tprev = p.add("vector", lambda e, j=j: e.tensor_copy(
                sm["glt4"][:, j * 8:(j + 1) * 8], PA2[:, 0:8]),
                waits=[ttr], inc="sv")
        rot3_evac[0] = tprev
        for j in range(4):
            p.add("vector", lambda e, j=j: e.reduce_max(
                sm["m14"][:, j:j + 1], sm["glt4"][:, j * 8:(j + 1) * 8], axis=AX_X))
        spacer()
        spacer()
        for j in range(4):
            p.add("vector", lambda e, j=j: e.tensor_scalar(
                sm["mk4"][:, j * 8:(j + 1) * 8], sm["glt4"][:, j * 8:(j + 1) * 8],
                sm["m14"][:, j:j + 1], 1.0, OP.is_equal, OP.mult))
        spacer()
        for j in range(4):
            p.add("vector", lambda e, j=j: e.scalar_tensor_tensor(
                sm["ms4"][:, j * 8:(j + 1) * 8], sm["mk4"][:, j * 8:(j + 1) * 8],
                -1e30, sm["glt4"][:, j * 8:(j + 1) * 8], OP.mult, OP.add))
        spacer()
        for j in range(4):
            p.add("vector", lambda e, j=j: e.reduce_max(
                sm["m24"][:, j:j + 1], sm["ms4"][:, j * 8:(j + 1) * 8], axis=AX_X))
        spacer()
        spacer()
        p.add("vector", lambda e: e.tensor_sub(sm["dd4"], sm["m24"],
                                               sm["m14"]))
        spacer()
        spacer()
        tdd = p.add("vector", lambda e: e.tensor_copy(sm["dmy"][:, :], sm["tmpc"][:, :]),
                    inc="sv")
        tth = p.add("scalar", lambda e: e.activation(sm["th4"], sm["dd4"],
                                                     AF.Tanh, scale=0.5),
                    waits=[tdd], inc="sc")
        p.add("vector", lambda e: e.tensor_scalar(sm["g24"], sm["th4"],
                                                  0.5, 0.5, OP.mult, OP.add),
              waits=[tth])
        p.add("vector", lambda e: e.tensor_scalar(sm["g14"], sm["th4"],
                                                  -0.5, 0.5, OP.mult, OP.add))
        spacer()
        spacer()
        for j in range(4):
            p.add("vector", lambda e, j=j: e.tensor_scalar(
                sm["comb4"][:, j * 8:(j + 1) * 8], sm["glt4"][:, j * 8:(j + 1) * 8],
                sm["m14"][:, j:j + 1], sm["g14"][:, j:j + 1], OP.is_equal, OP.mult))
        spacer()
        for j in range(4):
            p.add("vector", lambda e, j=j: e.tensor_scalar(
                sm["c24"][:, j * 8:(j + 1) * 8], sm["ms4"][:, j * 8:(j + 1) * 8],
                sm["m24"][:, j:j + 1], sm["g24"][:, j:j + 1], OP.is_equal, OP.mult))
        spacer()
        p.add("vector", lambda e: e.tensor_add(sm["comb4"][:, :], sm["comb4"][:, :],
                                               sm["c24"][:, :]))
        spacer()
        spacer()
        tcb = p.add("vector", lambda e: e.tensor_copy(sm["cmtok0"][:, :], sm["comb4"][:, :]),
                    inc="sv")
        tct = vgap()
        tcdma = p.add("gpsimd", lambda e, c=c: e.dma_start(
            out=agq_in[c].ap()[D:D + 8, :].rearrange(
                "r t -> (r t)").rearrange("(p c) -> p c", p=128),
            in_=sm["cmtok0"][:, :]),
            waits=[tct], inc="sg", n=16)
        t_agq[c] = p.add("gpsimd", lambda e, c=c: e.collective_compute(
            "AllGather", OP.bypass, replica_groups=[list(range(CORES))],
            ins=[agq_in[c].ap().opt()], outs=[agq_out[c].ap().opt()]),
            waits=[tcdma, t_x1dma_prev], inc="scc")

    t_tensorA_done = p.tok("st")

    tw1a = p.add("gpsimd", lambda e: e.dma_start(
        out=AW1[:, 0:24576].rearrange("p (ki f) -> p ki f", ki=6),
        in_=w1T[0:768, :].rearrange("(ki p) f -> p ki f", p=128)),
        waits=[t_gates_done], inc="sg", n=16)
    tw1 = p.add("gpsimd", lambda e: e.dma_start(
        out=AW1[:, 24576:32768].rearrange("p (ki f) -> p ki f", ki=2),
        in_=w1T[768:1024, :].rearrange("(ki p) f -> p ki f", p=128)),
        waits=[t_tensorA_done, tw1a], inc="sg", n=16)
    tw2 = p.add("gpsimd", lambda e: e.dma_start(
        out=AW2.ap().rearrange("p (fk d) -> p fk d", fk=32),
        in_=w2T.rearrange("(fk p) d -> p fk d", p=128)),
        waits=[t_hf[-1]], inc="sg", n=16)

    h1_rot = [PA0, PA1, PSP]
    h1_evac = [pa0_free, pa1_free, rot3_evac[1]]
    h1_mm = [T0, T0, T0]
    pyb = [PY0, PY1]
    py_evac = [T0, T0]
    t_xg = {}
    t_c8 = {}
    t_xw = {}
    wsel_done = {}
    t_wsel_read = {}
    t_cmtok_free = tw1
    h1_mm_last = {-1: T0, -2: T0}
    last_rs_dma = [("sg", 0), ("sg", 0)]
    t_out_count = 0
    hro = 0
    pyo = 0
    last_ln_dma = [("so", 0), ("so", 0)]
    last_ln_read = [T0, T0]
    t_rs_q = {}
    NCI = NQ * CORES

    def xbuf(ci):
        par = ci % 2
        return lambda ki: AX[:, par * 4096 + ki * TB:par * 4096 + (ki + 1) * TB]

    def issue_loads(ci):
        lc, r = ci // CORES, ci % CORES
        par = ci % 2
        xs = xbuf(ci)
        t_xg[ci] = p.add("sync", lambda e, r=r, lc=lc, par=par: e.dma_start(
            out=AX[:, par * 4096:par * 4096 + 4096].rearrange("p (ki t) -> p ki t", ki=8),
            in_=agq_out[lc].ap()[r * (D + 8):r * (D + 8) + D, :].rearrange(
                "(ki p) t -> p ki t", p=128)),
            waits=[t_agq[lc], t_zev, h1_mm_last.get(ci - 2, T0)] +
                  ([t_tensorA_done] if ci < 2 else []), inc="ss", n=16)
        if r == 0:
            src3 = agq_out[lc].ap().rearrange(
                "(rk row) t -> rk row t", row=D + 8)[:, D:D + 8, :].rearrange(
                "rk row t -> rk (row t)").rearrange(
                "rk (p c) -> p rk c", p=128)
            t_c8[ci] = p.add("sync", lambda e, s3=src3: e.dma_start(
                out=sm["cmquar"].ap().rearrange("p (rk c) -> p rk c", c=32),
                in_=s3),
                waits=[t_agq[lc], wsel_done.get(ci - 1, T0)] +
                      ([t_cmtok_free] if ci < 2 else []), inc="ss", n=16)
        else:
            t_c8[ci] = t_c8[ci - 1]

    def issue_wb_xw(ci):
        par = ci % 2
        r_ = ci % CORES
        for tt in range(4):
            p.add("vector", lambda e, tt=tt, r_=r_: e.tensor_mul(
                sm["mk4"][:, tt * 8:(tt + 1) * 8],
                sm["cmquar"][:, r_ * 32 + tt * 8:r_ * 32 + (tt + 1) * 8],
                sm["ohm"]),
                waits=([t_c8[ci], t_wsel_read.get(ci - 2, T0)] if tt == 0 else []))
        for tt in range(4):
            p.add("vector", lambda e, tt=tt, par=par: e.reduce_sum(
                sm["wsel"][:, par * 4 + tt:par * 4 + tt + 1],
                sm["mk4"][:, tt * 8:(tt + 1) * 8], axis=AX_X))
        wsel_done[ci] = vgap()
        t_xw[ci] = t_xg[ci]

    def do_ln2(lc):
        nonlocal t_out_count
        for tp_ in range(2):
            tin = {}
            tz2 = {}
            tmv = {}
            trc2 = {}
            for u in range(2):
                tt = tp_ * 2 + u
                tx1t = p.add("scalar", lambda e, lc=lc, tt=tt, u=u: e.dma_start(
                    out=x1T_sl(u), in_=agq_in[lc].ap()[0:D, tt * 128:(tt + 1) * 128],
                    transpose=True),
                    waits=[t_agq[lc], last_ln_dma[u], last_ln_read[u]], inc="ss", n=16)
                tmo = p.add("scalar", lambda e, lc=lc, tt=tt, u=u: e.dma_start(
                    out=moe_sl(u), in_=rs_out[lc].ap()[tt * 128:(tt + 1) * 128, :]),
                    waits=[t_rs_q[lc], last_ln_dma[u], last_ln_read[u]], inc="ss", n=16)
                tin[u] = (tx1t, tmo)
            for u in range(2):
                tz2[u] = p.add("vector", lambda e, u=u: e.tensor_add(
                    z2_sl(u), x1T_sl(u), moe_sl(u)),
                    waits=list(tin[u]), inc="sv")
                last_ln_read[u] = tz2[u]
            for u in range(2):
                p.add("vector", lambda e, u=u: e.bn_stats(
                    sm["st12"][:, u * 12:u * 12 + 6], z2_sl(u)[:, 0:512]),
                    waits=[tz2[u]])
                p.add("vector", lambda e, u=u: e.bn_stats(
                    sm["st12"][:, u * 12 + 6:u * 12 + 12], z2_sl(u)[:, 512:1024]))
            for u in range(2):
                tmv[u] = p.add("vector", lambda e, u=u: e.bn_aggr(
                    sm["mv2"][:, u * 2:u * 2 + 2], sm["st12"][:, u * 12:u * 12 + 12]),
                    inc="sv")
            spacer()
            spacer()
            tmvg = p.add("vector", lambda e: e.tensor_copy(sm["dmy"][:, :], sm["tmpc"][:, :]),
                         inc="sv")
            tmv = {0: tmvg, 1: tmvg}
            for u in range(2):
                tsq2 = p.add("scalar", lambda e, u=u: e.activation(
                    sm["rstd2"][:, u:u + 1], sm["mv2"][:, u * 2 + 1:u * 2 + 2],
                    AF.Sqrt, bias=sm["epsc"][:, 0:1]), waits=[tmv[u]], inc="sc")
                trc2[u] = tsq2
            for u in range(2):
                trc2[u] = p.add("vector", lambda e, u=u: e.reciprocal(
                    sm["rstd2"][:, u:u + 1], sm["rstd2"][:, u:u + 1]),
                    waits=[trc2[u]], inc="sv")
            spacer()
            spacer()
            touts = {}
            for u in range(2):
                p.add("vector", lambda e, u=u: e.tensor_scalar(
                    z2_sl(u), z2_sl(u), sm["mv2"][:, u * 2:u * 2 + 1],
                    sm["rstd2"][:, u:u + 1], OP.subtract, OP.mult),
                    waits=[trc2[u]])
            tog = vgap()
            for u in range(2):
                touts[u] = tog
            for u in range(2):
                tt = tp_ * 2 + u
                last_ln_dma[u] = p.add("scalar", lambda e, lc=lc, tt=tt, u=u: e.dma_start(
                    out=out_d[lc * TC + tt * 128:(lc * TC + (tt + 1) * 128), :],
                    in_=z2_sl(u)), waits=[touts[u]], inc="so", n=16)
                t_out_count += 16

    issue_loads(0)
    issue_loads(1)
    issue_wb_xw(0)
    for ci in range(NCI):
        lc, r = ci // CORES, ci % CORES
        xs = xbuf(ci)
        for fi in range(FI):
            rr_ = hro % 3
            hro += 1
            tok = None
            for ki in range(8):
                tok = p.add("tensor", lambda e, ps=h1_rot[rr_], l=aw1_w1(ki, fi),
                            rhs=xs(ki), s=(ki == 0), t=(ki == 7):
                            e.matmul(ps[:, :], l, rhs, start=s, stop=t),
                            waits=([t_xw[ci], tw1, h1_evac[rr_]] if ki == 0 else []),
                            inc=("st" if ki == 7 else None))
            h1_mm[rr_] = tok
            if fi % 2 == 0:
                t_ev = p.add("vector", lambda e, ps=h1_rot[rr_], o=h1s_sl(fi, 0, TB):
                             e.tensor_scalar(o, ps[:, :], 0.0, None, OP.max),
                             waits=[tok], inc="sv")
            else:
                t_ev = p.add("scalar", lambda e, ps=h1_rot[rr_], o=h1s_sl(fi, 0, TB):
                             e.activation(o, ps[:, :], AF.Relu),
                             waits=[tok], inc="sc")
            h1_evac[rr_] = t_ev
        h1_mm_last[ci] = h1_mm[(hro - 1) % 3]
        h1s_done_v = vgap()
        h1s_done_c = p.add("scalar", lambda e: e.activation(
            sm["dmy"][:, :], sm["tmpc"][:, :], AF.Copy), inc="sc")
        if ci + 2 < NCI:
            issue_loads(ci + 2)
        if ci + 1 < NCI:
            issue_wb_xw(ci + 1)
        for tt in range(4):
            pb = pyo % 2
            pyo += 1
            tok = None
            for half in range(2):
                for fk in range(FI):
                    is_first = (half == 0 and fk == 0)
                    is_last = (half == 1 and fk == FI - 1)
                    tok = p.add("tensor", lambda e, ps=pyb[pb], h=half,
                                l=h1s_sl(fk, tt * 128, 128),
                                rhs=w2_rhs(fk, half), s=(fk == 0), t=(fk == FI - 1):
                                e.matmul(ps[:, h * 512:(h + 1) * 512], l, rhs,
                                         start=s, stop=t),
                                waits=([h1s_done_v, h1s_done_c, tw2, py_evac[pb]]
                                       if is_first else []),
                                inc=("st" if is_last else None))
            spar = tt % 2
            par = ci % 2
            p.add("vector", lambda e, ps=pyb[pb], o=rsst_sl(spar), tt=tt, par=par:
                  e.tensor_scalar(o, ps[:, :], sm["wsel"][:, par * 4 + tt:par * 4 + tt + 1],
                                  None, OP.mult),
                  waits=[tok, last_rs_dma[spar], wsel_done[ci]])
            tev = vgap()
            py_evac[pb] = tev
            t_wsel_read[ci] = tev
            last_rs_dma[spar] = p.add(
                "gpsimd", lambda e, q=lc, r=r, tt=tt, spar=spar:
                e.dma_start(out=rs_in[q].ap()[r * TC + tt * 128:
                                              r * TC + (tt + 1) * 128, :],
                            in_=rsst_sl(spar)),
                waits=[tev], inc="sg", n=16)
        if r == CORES - 1:
            t_rs_q[lc] = p.add("gpsimd", lambda e, q=lc: e.collective_compute(
                "ReduceScatter", OP.add, replica_groups=[list(range(CORES))],
                ins=[rs_in[q].ap().opt()], outs=[rs_out[q].ap().opt()]),
                waits=[last_rs_dma[0], last_rs_dma[1]], inc="scc")
        if r == 1 and lc > 0:
            do_ln2(lc - 1)
    do_ln2(NQ - 1)

    p.add("vector", lambda e: e.tensor_copy(sm["dmy"][:, :], sm["tmpc"][:, :]),
          waits=[("so", t_out_count)])

    with nc.Block() as block:
        p.emit(block)
    es.close()
    return nc


_NC_CACHE = None


def kernel(L0_attn_x, L0_attn_W_in, L0_attn_b_in, L0_attn_W_gx, L0_attn_b_gx,
           L0_attn_W_ga, L0_attn_b_ga, L0_attn_a_param, L0_attn_W_out,
           L0_attn_b_out, L0_ffn_gate_w, L0_ffn_w1, L0_ffn_w2):
    global _NC_CACHE
    f = np.float32
    x = np.asarray(L0_attn_x, f)
    xT = np.ascontiguousarray(x[0].T)

    def colmajor8(v):
        return np.ascontiguousarray(np.asarray(v, f).reshape(8, 128).T)

    W_in = np.ascontiguousarray(np.asarray(L0_attn_W_in, f))
    W_gx = np.ascontiguousarray(np.asarray(L0_attn_W_gx, f))
    W_ga = np.ascontiguousarray(np.asarray(L0_attn_W_ga, f))
    W_out = np.ascontiguousarray(np.asarray(L0_attn_W_out, f))
    gw = np.asarray(L0_ffn_gate_w, f)
    w1 = np.asarray(L0_ffn_w1, f)
    w2 = np.asarray(L0_ffn_w2, f)
    bins = colmajor8(L0_attn_b_in)
    bgxs = colmajor8(L0_attn_b_gx)
    bgas = colmajor8(L0_attn_b_ga)
    bouts = colmajor8(L0_attn_b_out)
    apars = colmajor8(L0_attn_a_param)
    idn = np.eye(128, dtype=f)
    gwp = np.ascontiguousarray(gw.T.reshape(8, 128, 8).transpose(1, 0, 2).reshape(128, 64))

    if _NC_CACHE is None:
        _NC_CACHE = build_nc()
    nc = _NC_CACHE

    in_maps = []
    for i in range(CORES):
        selc = np.zeros((128, 8), f)
        if i > 0:
            selc[:, i - 1] = 1.0
        ohm = np.tile((np.arange(8) == i).astype(f), (128, 1))
        smalls = np.concatenate([bins, bgxs, bgas, bouts, apars, selc, gwp, ohm,
                                 np.zeros((128, 8), f)], axis=1).astype(f)
        in_maps.append({
            "xT": np.ascontiguousarray(xT[:, i * TOK:(i + 1) * TOK]),
            "w_in": W_in, "w_gx": W_gx, "w_ga": W_ga, "w_out": W_out,
            "smalls_h": np.ascontiguousarray(smalls),
            "w1T": np.ascontiguousarray(w1[i].T),
            "w2T": np.ascontiguousarray(w2[i].T),
            "idn_h": idn,
        })

    res = run_bass_kernel_spmd(nc, in_maps, core_ids=list(range(CORES)))
    out = np.concatenate([res.results[i]["out"] for i in range(CORES)], axis=0)
    return out.reshape(1, S, D).astype(np.float32)


# version 07
# speedup vs baseline: 1.0673x; 1.0017x over previous
import numpy as np
from contextlib import ExitStack

import concourse.bass as bass
import concourse.mybir as mybir
from concourse.bass_utils import run_bass_kernel_spmd

F32 = mybir.dt.float32
BF16 = mybir.dt.bfloat16
AF = mybir.ActivationFunctionType
OP = mybir.AluOpType
AX_X = mybir.AxisListType.X

CORES = 8
D = 1024
S = 16384
TOK = S // CORES
TC = 512
NCHUNK = TOK // TC
FF = 4096
FI = FF // 128
TB = 512
NQ = 4
EPS = 1e-5


class P:

    ENGS = ("sync", "gpsimd", "vector", "scalar", "tensor")

    def __init__(self, nc, es):
        self.nc = nc
        self.ops = {e: [] for e in self.ENGS}
        self.sems = {}
        self.cnt = {}
        for name in ("sg", "ss", "sv", "sc", "st", "scc", "so"):
            self.sems[name] = es.enter_context(nc.semaphore(name))
            self.cnt[name] = 0

    def add(self, eng, fn, waits=(), inc=None, n=1):
        ws = tuple(w for w in waits if w is not None and w[1] > 0)
        self.ops[eng].append((fn, ws, inc, n))
        if inc is not None:
            self.cnt[inc] += n
            return (inc, self.cnt[inc])
        return None

    def tok(self, name):
        return (name, self.cnt[name])

    def _run(self, name, eng):
        seen = {}
        for fn, waits, inc, n in self.ops[name]:
            for (s, v) in waits:
                if seen.get(s, 0) < v:
                    eng.wait_ge(self.sems[s], v)
                    seen[s] = v
            instr = fn(eng)
            if inc is not None:
                instr.then_inc(self.sems[inc], n)

    def emit(self, block):
        runner = self._run

        @block.sync
        def _(e):
            runner("sync", e)

        @block.gpsimd
        def _(e):
            runner("gpsimd", e)

        @block.vector
        def _(e):
            runner("vector", e)

        @block.scalar
        def _(e):
            runner("scalar", e)

        @block.tensor
        def _(e):
            runner("tensor", e)


def build_nc():
    nc = bass.Bass("TRN2", target_bir_lowering=False)

    xT = nc.dram_tensor("xT", [D, TOK], F32, kind="ExternalInput").ap()
    w_in = nc.dram_tensor("w_in", [D, D], F32, kind="ExternalInput").ap()
    w_gx = nc.dram_tensor("w_gx", [D, D], F32, kind="ExternalInput").ap()
    w_ga = nc.dram_tensor("w_ga", [D, D], F32, kind="ExternalInput").ap()
    w_out = nc.dram_tensor("w_out", [D, D], F32, kind="ExternalInput").ap()
    smalls_h = nc.dram_tensor("smalls_h", [128, 128], F32, kind="ExternalInput").ap()
    w1T = nc.dram_tensor("w1T", [D, FF], F32, kind="ExternalInput").ap()
    w2T = nc.dram_tensor("w2T", [FF, D], F32, kind="ExternalInput").ap()
    idn_h = nc.dram_tensor("idn_h", [128, 128], F32, kind="ExternalInput").ap()
    out_d = nc.dram_tensor("out", [TOK, D], F32, kind="ExternalOutput").ap()

    ag1_in = nc.dram_tensor("ag1_in", [2, D], F32)
    ag1_out = nc.dram_tensor("ag1_out", [2 * CORES, D], F32)
    agq_in = [nc.dram_tensor(f"agq_in_{q}", [D + 8, TC], BF16) for q in range(NQ)]
    agq_out = [nc.dram_tensor(f"agq_out_{q}", [(D + 8) * CORES, TC], BF16) for q in range(NQ)]
    rs_in = [nc.dram_tensor(f"rs_in_{q}", [CORES * TC, D], BF16) for q in range(NQ)]
    rs_out = [nc.dram_tensor(f"rs_out_{q}", [TC, D], BF16) for q in range(NQ)]

    es = ExitStack()

    AW1 = es.enter_context(nc.sbuf_tensor("AW1", [128, 32768], BF16))
    AW2 = es.enter_context(nc.sbuf_tensor("AW2", [128, 32768], BF16))
    AX = es.enter_context(nc.sbuf_tensor("AX", [128, 8192], BF16))
    AB = es.enter_context(nc.sbuf_tensor("AB", [128, 14336], BF16))
    AFr = es.enter_context(nc.sbuf_tensor("AF", [128, 8192], F32))
    AFb = AFr.ap().bitcast(BF16)
    ABf = AB.ap().bitcast(F32)

    SML = es.enter_context(nc.sbuf_tensor("SML", [128, 128], F32))
    sm = {}
    sm["b_in"] = SML[:, 0:8]
    sm["bgx_hf"] = SML[:, 8:16]
    sm["bga_hf"] = SML[:, 16:24]
    sm["b_out"] = SML[:, 24:32]
    sm["aptmp"] = SML[:, 32:40]
    sm["selc"] = SML[:, 40:48]
    sm["gwtf"] = SML[:, 48:112]
    sm["ohm"] = SML[:, 112:120]
    for nm, shp, dt in [
        ("msp_h", [128, 8], F32),
        ("hcar", [128, 8], F32), ("apcar", [128, 8], F32),
        ("cin", [128, 8], F32), ("tmpc", [128, 8], F32), ("dmy", [128, 8], F32),
        ("aesum", [128, 128], F32), ("cpre", [128, 64], F32),
        ("idn", [128, 128], F32),
        ("onesc", [128, 1], BF16), ("onesf", [1, 128], F32),
        ("glt4", [128, 32], F32), ("mk4", [128, 32], F32),
        ("ms4", [128, 32], F32), ("comb4", [128, 32], F32),
        ("st12", [128, 24], F32), ("mv2", [128, 4], F32), ("rstd2", [128, 2], F32),
        ("cmtok0", [128, 32], BF16), ("cmquar", [128, 256], BF16),
        ("wsel", [128, 8], F32), ("sdmy", [128, 8], F32),
        ("epsc", [128, 1], F32), ("onec", [128, 1], F32), ("spt", [128, 8], F32),
    ]:
        sm[nm] = es.enter_context(nc.sbuf_tensor(nm, shp, dt))
    sm["asum"] = sm["aesum"][:, 0:64]
    sm["esum"] = sm["aesum"][:, 64:128]
    sm["c24"] = sm["mk4"]
    sm["m14"] = sm["st12"][:, 0:4]
    sm["m24"] = sm["st12"][:, 4:8]
    sm["dd4"] = sm["st12"][:, 8:12]
    sm["th4"] = sm["st12"][:, 12:16]
    sm["g14"] = sm["st12"][:, 16:20]
    sm["g24"] = sm["st12"][:, 20:24]

    PA0 = es.enter_context(nc.psum_tensor("PA0", [128, 512], F32))
    PA1 = es.enter_context(nc.psum_tensor("PA1", [128, 512], F32))
    PA2 = es.enter_context(nc.psum_tensor("PA2", [128, 512], F32))
    PSP = es.enter_context(nc.psum_tensor("PSP", [128, 512], F32))
    PY0 = es.enter_context(nc.psum_tensor("PY0", [128, 1024], F32))
    PY1 = es.enter_context(nc.psum_tensor("PY1", [128, 1024], F32))

    p = P(nc, es)
    T0 = ("sv", 0)

    def aw1_w(base, ki, ii):
        c = base + ki * 1024 + ii * 128
        return AW1[:, c:c + 128]

    def aw1_w1(ki, fi):
        c = ki * 4096 + fi * 128
        return AW1[:, c:c + 128]

    def hloc_sl(ii, c):
        a = ii * TOK + c * TC
        return AW2[:, a:a + TC]

    def apb_sl(ii, c):
        a = 16384 + ii * TOK + c * TC
        return AW2[:, a:a + TC]

    def w2_rhs(fk, half):
        a = fk * 1024 + half * 512
        return AW2[:, a:a + 512]

    def x0b_sl(ki):
        return AX[:, ki * TC:(ki + 1) * TC]

    def xinb_sl(ii):
        return AX[:, 4096 + ii * TC:4096 + (ii + 1) * TC]

    def xinf_sl(ii):
        return AFr[:, ii * TC:(ii + 1) * TC]

    def afull_sl(ii):
        return AFr[:, 4096 + ii * TC:4096 + (ii + 1) * TC]

    def slot(k):
        return ABf[:, 6144 + k * 512:6144 + (k + 1) * 512]

    def ufull_sl(ii):
        return AB[:, ii * TC:(ii + 1) * TC]

    def xfull_sl(ii):
        return AB[:, 8192 + ii * 512:8192 + (ii + 1) * 512]

    def hfix_sl(ii):
        return AB[:, ii * TC:(ii + 1) * TC]

    def x1c_sl(ii):
        return AB[:, 4096 + ii * TC:4096 + (ii + 1) * TC]

    def zb_slot(k):
        return AB[:, 8192 + k * 512:8192 + (k + 1) * 512]

    def zsq_slot(k):
        return AB[:, 9216 + k * 512:9216 + (k + 1) * 512]

    def z_sl(ii):
        return AFr[:, ii * TC:(ii + 1) * TC]

    def rows(k):
        return AFr[0:1, 4096 + k * 512:4096 + (k + 1) * 512]

    def x1tmp():
        return AFr[:, 6144:6656]

    def glog_sl():
        return ABf[0:8, 5120:5632]

    def combT_sl(c):
        return AB[0:8, 12288 + c * TC:12288 + (c + 1) * TC]

    def xg_sl(ki):
        return AX[:, ki * TB:(ki + 1) * TB]

    def xw_sl(ki):
        return AX[:, 4096 + ki * TB:4096 + (ki + 1) * TB]

    def h1s_sl(fk, off, ln):
        a = fk * TB + off
        return AFb[:, a:a + ln]

    def x1T_sl(par):
        return AB[:, par * 1024:(par + 1) * 1024]

    def moe_sl(par):
        return AB[:, 2048 + par * 1024:2048 + (par + 1) * 1024]

    def rsst_sl(par):
        return AB[:, 4096 + par * 1024:4096 + (par + 1) * 1024]

    def wb_sl(par):
        return AB[:, 6144 + par * 512:6144 + (par + 1) * 512]

    def cm8_sl(par):
        return AB[0:8, 7168 + par * 512:7168 + (par + 1) * 512]

    def cm8_full(par):
        return AB[:, 7168 + par * 512:7168 + (par + 1) * 512]

    def z2_sl(par):
        return ABf[:, 4096 + par * 1024:4096 + (par + 1) * 1024]

    def spacer():
        p.add("vector", lambda e: e.tensor_copy(sm["dmy"][:, :], sm["tmpc"][:, :]))

    def vgap():
        spacer()
        return p.add("vector", lambda e: e.tensor_copy(sm["dmy"][:, :], sm["tmpc"][:, :]),
                     inc="sv")

    t_w = {}
    for nm, src, base in [("in", w_in, 0), ("gx", w_gx, 8192),
                          ("ga", w_ga, 16384), ("out", w_out, 24576)]:
        src3 = src.rearrange("(ki p) c -> p ki c", p=128)
        dst3 = AW1[:, base:base + 8192].rearrange("p (ki c) -> p ki c", ki=8)
        t_w[nm] = p.add("gpsimd",
                        lambda e, o=dst3, i=src3: e.dma_start(out=o, in_=i),
                        inc="sg", n=16)


    t_small = p.add("sync", lambda e: e.dma_start(out=SML[:, :], in_=smalls_h),
                    inc="ss", n=16)
    t_idn = p.add("sync", lambda e: e.dma_start(out=sm["idn"][:, :], in_=idn_h),
                  waits=[t_small], inc="ss", n=16)
    t_gwtf = t_small
    small_loads = [t_small] * 6
    t_sml = t_small

    p.add("vector", lambda e: e.memset(sm["hcar"][:, :], 0.0))
    p.add("vector", lambda e: e.memset(sm["apcar"][:, :], 1.0))
    p.add("vector", lambda e: e.memset(sm["cin"][:, :], 0.0))
    p.add("vector", lambda e: e.memset(sm["tmpc"][:, :], 0.0))
    p.add("vector", lambda e: e.memset(sm["onesc"][:, :], 1.0))
    p.add("vector", lambda e: e.memset(sm["epsc"][:, :], EPS))
    p.add("vector", lambda e: e.memset(sm["onec"][:, :], 1.0))
    t_ones = p.add("vector", lambda e: e.memset(sm["onesf"][:, :], 1.0), inc="sv")
    def sgap(n=8):
        for _ in range(n):
            p.add("scalar", lambda e: e.activation(sm["sdmy"][:, :], sm["b_in"][:, :],
                                                   AF.Copy))
    ta1 = p.add("scalar", lambda e: e.activation(
        sm["spt"][:, :], sm["aptmp"][:, :], AF.Abs),
        waits=[small_loads[4]], inc="sc")
    t_r = p.add("scalar", lambda e: e.activation(
        sm["msp_h"][:, :], sm["aptmp"][:, :], AF.Relu), inc="sc")
    sgap()
    t_e = p.add("scalar", lambda e: e.activation(
        sm["aptmp"][:, :], sm["spt"][:, :], AF.Exp, scale=-1.0), inc="sc")
    sgap()
    t_l = p.add("scalar", lambda e: e.activation(
        sm["spt"][:, :], sm["aptmp"][:, :], AF.Ln, bias=sm["onec"][:, 0:1]),
        waits=[t_ones], inc="sc")
    sgap()
    tad = p.add("vector", lambda e: e.tensor_add(
        sm["msp_h"][:, :], sm["msp_h"][:, :], sm["spt"][:, :]),
        waits=[t_l, t_r], inc="sv")
    spacer()
    spacer()
    t_msp = p.add("vector", lambda e: e.tensor_scalar_mul(
        sm["msp_h"][:, :], sm["msp_h"][:, :], -4.0), waits=[tad], inc="sv")
    t_bgxh = p.add("vector", lambda e: e.tensor_scalar_mul(
        sm["bgx_hf"][:, :], sm["bgx_hf"][:, :], 0.5), waits=[small_loads[1]], inc="sv")
    t_bgah = p.add("vector", lambda e: e.tensor_scalar_mul(
        sm["bga_hf"][:, :], sm["bga_hf"][:, :], 0.5), waits=[small_loads[2]], inc="sv")

    rotors = [PA0, PA1, PA2, PSP]
    rot_evac = [T0, T0, T0, T0]
    rot_i = 0

    def next_rot():
        nonlocal rot_i
        r = rot_i % 4
        rot_i += 1
        return r

    t_x0 = [None] * NCHUNK
    t_xin_mm_last = T0
    slot_free = [T0, T0]
    t_u_done = {}
    t_x_done = {}
    t_sct = {}

    for c in range(NCHUNK):
        wx = [t_xin_mm_last] if c > 0 else []
        src3 = xT.rearrange("(ki p) t -> p ki t", p=128)[:, :, c * TC:(c + 1) * TC]
        dst3 = AX[:, 0:4096].rearrange("p (ki t) -> p ki t", ki=8)
        t_x0[c] = p.add("gpsimd", lambda e, o=dst3, i=src3: e.dma_start(out=o, in_=i),
                        waits=wx, inc="sg", n=16)

        t_evac_xin = []
        for ii in range(8):
            r = next_rot()
            tok = None
            for ki in range(8):
                tok = p.add("tensor", lambda e, ps=rotors[r], l=aw1_w(0, ki, ii),
                            rr=x0b_sl(ki), s=(ki == 0), t=(ki == 7):
                            e.matmul(ps[:, :], l, rr, start=s, stop=t),
                            waits=([t_x0[c], t_w["in"], rot_evac[r]] if ki == 0 else []),
                            inc=("st" if ki == 7 else None))
            p.add("vector", lambda e, ps=rotors[r], o=xinf_sl(ii), b=sm["b_in"][:, ii:ii + 1]:
                  e.tensor_scalar_add(o, ps[:, :], b), waits=[tok, t_sml])
            e2 = p.add("vector", lambda e, ps=rotors[r], o=xinb_sl(ii), b=sm["b_in"][:, ii:ii + 1]:
                       e.tensor_scalar_add(o, ps[:, :], b), inc="sv")
            rot_evac[r] = e2
            t_evac_xin.append(e2)
        t_evac_xin_g = vgap()
        t_xin_mm_last = p.tok("st")

        for ii in range(8):
            sA = slot(0)
            sB = slot(1)
            rA = next_rot()
            tokA = None
            for ki in range(8):
                tokA = p.add("tensor", lambda e, ps=rotors[rA], l=aw1_w(16384, ki, ii),
                             rr=xinb_sl(ki), s=(ki == 0), t=(ki == 7):
                             e.matmul(ps[:, :], l, rr, start=s, stop=t),
                             waits=([t_evac_xin_g, t_w["ga"], rot_evac[rA]] if ki == 0 else []),
                             inc=("st" if ki == 7 else None))
            rB = next_rot()
            tokB = None
            for ki in range(8):
                tokB = p.add("tensor", lambda e, ps=rotors[rB], l=aw1_w(8192, ki, ii),
                             rr=xinb_sl(ki), s=(ki == 0), t=(ki == 7):
                             e.matmul(ps[:, :], l, rr, start=s, stop=t),
                             waits=([t_w["gx"], rot_evac[rB]] if ki == 0 else []),
                             inc=("st" if ki == 7 else None))
            ta = p.add("scalar", lambda e, ps=rotors[rA], o=sB, b=sm["bga_hf"][:, ii:ii + 1]:
                       e.activation(o, ps[:, :], AF.Tanh, bias=b, scale=0.5),
                       waits=[tokA, t_bgah, slot_free[1]], inc="sc")
            rot_evac[rA] = ta
            tb = p.add("scalar", lambda e, ps=rotors[rB], o=sA, b=sm["bgx_hf"][:, ii:ii + 1]:
                       e.activation(o, ps[:, :], AF.Tanh, bias=b, scale=0.5),
                       waits=[tokB, t_bgxh, slot_free[0]], inc="sc")
            rot_evac[rB] = tb
            tl = p.add("vector", lambda e, s=sB, m=sm["msp_h"][:, ii:ii + 1]:
                       e.tensor_scalar(s, s, m, m, OP.mult, OP.add),
                       waits=[ta, t_msp], inc="sv")
            tu = p.add("vector", lambda e, s=sA, o=ufull_sl(ii), xi=xinf_sl(ii):
                       e.scalar_tensor_tensor(o, s, 1.0, xi, OP.add, OP.mult),
                       waits=[tb], inc="sv")
            t_u_done[(c, ii)] = tu
            p.add("scalar", lambda e, s=sB, o=afull_sl(ii):
                  e.activation(o, s, AF.Exp), waits=[tu], inc="sc")
            tq = p.add("scalar", lambda e, s=sB, o=sA:
                       e.activation(o, s, AF.Exp, scale=2.0), waits=[tu], inc="sc")
            slot_free[1] = tq
            p.add("vector", lambda e, s=sA: e.tensor_scalar(
                s, s, -1.0, 1.0, OP.mult, OP.add), waits=[tq])
            p.add("vector", lambda e, s=sA: e.tensor_scalar_max(s, s, 0.0))
            tx3 = p.add("vector", lambda e, s=sA, o=xfull_sl(ii): e.tensor_copy(o, s),
                        inc="sv")
            slot_free[0] = tx3
            t_x_done[(c, ii)] = tx3

        for ii in range(8):
            t_sct[(c, ii)] = p.add("scalar", lambda e, s=xfull_sl(ii):
                                   e.activation(s, s, AF.Sqrt, scale=0.25),
                                   waits=[t_x_done[(c, ii)]], inc="sc")

        for i0 in range(0, 8, 2):
            prs = [i0, i0 + 1]
            bsl = {ii: AFr[:, (ii % 2) * 512:(ii % 2) * 512 + 512] for ii in prs}
            hsl = {ii: AFr[:, 1024 + (ii % 2) * 512:1024 + (ii % 2) * 512 + 512] for ii in prs}
            asl = {ii: AFr[:, 2048 + (ii % 2) * 512:2048 + (ii % 2) * 512 + 512] for ii in prs}
            for ii in prs:
                p.add("vector", lambda e, o=bsl[ii], u=ufull_sl(ii), s=xfull_sl(ii):
                      e.tensor_mul(o, u, s),
                      waits=[t_sct[(c, ii)], t_u_done[(c, ii)]])
            for ii in prs:
                p.add("vector", lambda e, o=hsl[ii], a=afull_sl(ii), b=bsl[ii],
                      i0_=sm["hcar"][:, ii:ii + 1]:
                      e.tensor_tensor_scan(o, a, b, i0_, OP.mult, OP.add))
            for ii in prs:
                p.add("vector", lambda e, o=asl[ii], a=afull_sl(ii),
                      i0_=sm["apcar"][:, ii:ii + 1]:
                      e.tensor_tensor_scan(o, a, a, i0_, OP.mult, OP.bypass))
            for ii in prs:
                p.add("vector", lambda e, o=hloc_sl(ii, c), i=hsl[ii]: e.tensor_copy(o, i))
            for ii in prs:
                p.add("vector", lambda e, o=apb_sl(ii, c), i=asl[ii]: e.tensor_copy(o, i))
            for ii in prs:
                p.add("vector", lambda e, o=sm["hcar"][:, ii:ii + 1], i=hsl[ii][:, 511:512]:
                      e.tensor_copy(o, i))
            for ii in prs:
                p.add("vector", lambda e, o=sm["apcar"][:, ii:ii + 1], i=asl[ii][:, 511:512]:
                      e.tensor_copy(o, i), inc=("sv" if ii == i0 + 1 else None))

    t_gates_done = p.tok("st")
    t_lastscan = vgap()
    ag1i3 = ag1_in.ap().rearrange("r (p ii) -> r p ii", ii=8)
    tg1 = p.add("gpsimd", lambda e: e.dma_start(out=ag1i3[0], in_=sm["apcar"][:, :]),
                waits=[t_lastscan], inc="sg", n=16)
    tg2 = p.add("gpsimd", lambda e: e.dma_start(out=ag1i3[1], in_=sm["hcar"][:, :]),
                inc="sg", n=16)
    t_ag1 = p.add("gpsimd", lambda e: e.collective_compute(
        "AllGather", OP.bypass, replica_groups=[list(range(CORES))],
        ins=[ag1_in.ap().opt()], outs=[ag1_out.ap().opt()]),
        waits=[tg1, tg2], inc="scc")
    ag1o3 = ag1_out.ap().rearrange("(r two) (p ii) -> two p r ii", two=2, ii=8)
    tga = p.add("gpsimd", lambda e: e.dma_start(
        out=sm["asum"].rearrange("p (r ii) -> p r ii", ii=8), in_=ag1o3[0]),
        waits=[t_ag1], inc="sg", n=16)
    tge = p.add("gpsimd", lambda e: e.dma_start(
        out=sm["esum"].rearrange("p (r ii) -> p r ii", ii=8), in_=ag1o3[1]),
        waits=[tga], inc="sg", n=16)
    p.add("vector", lambda e: e.tensor_copy(sm["cpre"][:, 0:8], sm["esum"][:, 0:8]),
          waits=[tge])
    for k in range(1, 7):
        spacer()
        p.add("vector", lambda e, k=k: e.tensor_mul(
            sm["tmpc"][:, :], sm["asum"][:, k * 8:(k + 1) * 8],
            sm["cpre"][:, (k - 1) * 8:k * 8]))
        spacer()
        p.add("vector", lambda e, k=k: e.tensor_add(
            sm["cpre"][:, k * 8:(k + 1) * 8], sm["tmpc"][:, :],
            sm["esum"][:, k * 8:(k + 1) * 8]))
    for k in range(7):
        spacer()
        spacer()
        p.add("vector", lambda e, k=k: e.scalar_tensor_tensor(
            sm["cin"][:, :], sm["cpre"][:, k * 8:(k + 1) * 8],
            sm["selc"][:, k:k + 1], sm["cin"][:, :], OP.mult, OP.add))
    spacer()
    spacer()
    t_cin = p.add("vector", lambda e: e.tensor_copy(sm["dmy"][:, :], sm["cin"][:, :]),
                  inc="sv")

    rot3 = [PA2, PSP]
    rot3_evac = [rot_evac[2], rot_evac[3]]
    pa0_free = rot_evac[0]
    pa1_free = rot_evac[1]
    t_glog_mm = T0
    t_agq = [None] * NQ
    t_x1dma_prev = ("sg", 0)
    t_zev = T0
    t_wout_mm = T0
    t_hf = []
    tct = T0

    for c in range(NCHUNK):
        src3 = xT.rearrange("(ki p) t -> p ki t", p=128)[:, :, c * TC:(c + 1) * TC]
        dst3 = AX[:, 0:4096].rearrange("p (ki t) -> p ki t", ki=8)
        t_x0r = p.add("gpsimd", lambda e, o=dst3, i=src3: e.dma_start(out=o, in_=i),
                      waits=[t_lastscan] + ([t_zev] if c > 0 else []), inc="sg", n=16)
        t_hf = []
        for ii in range(8):
            t_hf.append(p.add("vector", lambda e, o=hfix_sl(ii), a=apb_sl(ii, c),
                              s=sm["cin"][:, ii:ii + 1], h=hloc_sl(ii, c):
                              e.scalar_tensor_tensor(o, a, s, h, OP.mult, OP.add),
                              waits=[t_cin] + ([t_wout_mm] if c > 0 else []), inc="sv"))
        t_hf_g = vgap()
        t_zs = []
        for ii in range(8):
            r = ii % 2
            tok = None
            for ki in range(8):
                tok = p.add("tensor", lambda e, ps=rot3[r], l=aw1_w(24576, ki, ii),
                            rr=hfix_sl(ki), s=(ki == 0), t=(ki == 7):
                            e.matmul(ps[:, :], l, rr, start=s, stop=t),
                            waits=([t_hf_g, t_w["out"], rot3_evac[r]] if ki == 0 else []),
                            inc=("st" if ki == 7 else None))
            tz = p.add("vector", lambda e, ps=rot3[r], o=z_sl(ii),
                       b=sm["b_out"][:, ii:ii + 1], x0=x0b_sl(ii):
                       e.scalar_tensor_tensor(o, ps[:, :], b, x0, OP.add, OP.add),
                       waits=[tok, t_x0r], inc="sv")
            rot3_evac[r] = tz
            t_zs.append(tz)
        t_wout_mm = p.tok("st")
        t_zev = t_zs[-1]
        t_slot_mm = [T0, T0]
        tok_sum = tok_sq = None
        for ii in range(8):
            k = ii % 2
            p.add("vector", lambda e, o=zb_slot(k), z=z_sl(ii): e.tensor_copy(o, z),
                  waits=[t_zs[ii], t_slot_mm[k]])
            p.add("vector", lambda e, o=zsq_slot(k), z=z_sl(ii): e.tensor_mul(o, z, z))
            tzb = tzq = vgap()
            tok_sum = p.add("tensor", lambda e, rr=zb_slot(k), s=(ii == 0), t=(ii == 7):
                            e.matmul(PA0[0:1, :], sm["onesc"][:, :], rr, start=s, stop=t),
                            waits=[tzb, t_ones] + ([pa0_free] if ii == 0 else []),
                            inc="st")
            tok_sq = p.add("tensor", lambda e, rr=zsq_slot(k), s=(ii == 0), t=(ii == 7):
                           e.matmul(PA1[0:1, :], sm["onesc"][:, :], rr, start=s, stop=t),
                           waits=[tzq] + ([pa1_free] if ii == 0 else []),
                           inc="st")
            t_slot_mm[k] = tok_sq
        trm = p.add("vector", lambda e: e.tensor_scalar_mul(rows(0), PA0[0:1, :], 1.0 / D),
                    waits=[tok_sum], inc="sv")
        p.add("vector", lambda e: e.tensor_scalar_mul(rows(1), PA1[0:1, :], 1.0 / D),
              waits=[tok_sq])
        spacer()
        p.add("vector", lambda e: e.tensor_mul(rows(2), rows(0), rows(0)))
        spacer()
        p.add("vector", lambda e: e.tensor_sub(rows(1), rows(1), rows(2)))
        spacer()
        spacer()
        tvar = p.add("vector", lambda e: e.tensor_copy(sm["dmy"][:, :], sm["tmpc"][:, :]),
                     inc="sv")
        tsq = p.add("scalar", lambda e: e.activation(rows(2), rows(1), AF.Sqrt, bias=sm["epsc"][0:1, 0:1]),
                    waits=[tvar], inc="sc")
        trc = p.add("vector", lambda e: e.reciprocal(rows(3), rows(2)), waits=[tsq],
                    inc="sv")
        spacer()
        spacer()
        p.add("vector", lambda e: e.tensor_mul(rows(2), rows(0), rows(3)))
        tmr = vgap()
        tbr = p.add("tensor", lambda e: e.matmul(PA1[:, :], sm["onesf"][:, :], rows(3),
                                                 start=True, stop=True),
                    waits=[tmr, tvar, t_ones], inc="st")
        tbm = p.add("tensor", lambda e: e.matmul(PA0[:, :], sm["onesf"][:, :], rows(2),
                                                 start=True, stop=True),
                    waits=[tmr, trm], inc="st")
        tx1last = None
        for ii in range(8):
            p.add("vector", lambda e, z=z_sl(ii): e.tensor_mul(x1tmp(), z, PA1[:, :]),
                  waits=([tbr, tbm, t_glog_mm, t_x1dma_prev] if ii == 0 else []))
            p.add("vector", lambda e, z=z_sl(ii): e.tensor_sub(z, x1tmp(), PA0[:, :]))
            p.add("vector", lambda e, o=x1c_sl(ii), z=z_sl(ii): e.tensor_copy(o, z))
            tx1last = None
        tx1last = vgap()
        pa0_free = tx1last
        pa1_free = tx1last
        t_x1dma_prev = p.add("gpsimd", lambda e, c=c: e.dma_start(
            out=agq_in[c].ap()[0:D, :].rearrange("(ki p) t -> p ki t", p=128),
            in_=AB[:, 4096:8192].rearrange("p (ki t) -> p ki t", ki=8)),
            waits=[tx1last], inc="sg", n=16)
        tok = None
        for ki in range(8):
            tok = p.add("tensor", lambda e, l=sm["gwtf"][:, ki * 8:(ki + 1) * 8],
                        rr=z_sl(ki), s=(ki == 0), t=(ki == 7):
                        e.matmul(PSP[0:8, :], l, rr, start=s, stop=t),
                        waits=([tx1last, t_gwtf, rot3_evac[1]] if ki == 0 else []),
                        inc=("st" if ki == 7 else None))
        t_glog_mm = tok
        tgl = p.add("vector", lambda e: e.tensor_copy(glog_sl(), PSP[0:8, :]),
                    waits=[tok], inc="sv")
        rot3_evac[1] = tgl
        tprev = rot3_evac[0]
        for j in range(4):
            ttr = p.add("tensor", lambda e, j=j: e.transpose(
                PA2[:, 0:8], glog_sl()[0:8, j * 128:(j + 1) * 128],
                sm["idn"][0:8, 0:8]),
                waits=[tgl, t_idn, tprev], inc="st")
            tprev = p.add("vector", lambda e, j=j: e.tensor_copy(
                sm["glt4"][:, j * 8:(j + 1) * 8], PA2[:, 0:8]),
                waits=[ttr], inc="sv")
        rot3_evac[0] = tprev
        for j in range(4):
            p.add("vector", lambda e, j=j: e.reduce_max(
                sm["m14"][:, j:j + 1], sm["glt4"][:, j * 8:(j + 1) * 8], axis=AX_X))
        spacer()
        spacer()
        for j in range(4):
            p.add("vector", lambda e, j=j: e.tensor_scalar(
                sm["mk4"][:, j * 8:(j + 1) * 8], sm["glt4"][:, j * 8:(j + 1) * 8],
                sm["m14"][:, j:j + 1], 1.0, OP.is_equal, OP.mult))
        spacer()
        for j in range(4):
            p.add("vector", lambda e, j=j: e.scalar_tensor_tensor(
                sm["ms4"][:, j * 8:(j + 1) * 8], sm["mk4"][:, j * 8:(j + 1) * 8],
                -1e30, sm["glt4"][:, j * 8:(j + 1) * 8], OP.mult, OP.add))
        spacer()
        for j in range(4):
            p.add("vector", lambda e, j=j: e.reduce_max(
                sm["m24"][:, j:j + 1], sm["ms4"][:, j * 8:(j + 1) * 8], axis=AX_X))
        spacer()
        spacer()
        p.add("vector", lambda e: e.tensor_sub(sm["dd4"], sm["m24"],
                                               sm["m14"]))
        spacer()
        spacer()
        tdd = p.add("vector", lambda e: e.tensor_copy(sm["dmy"][:, :], sm["tmpc"][:, :]),
                    inc="sv")
        tth = p.add("scalar", lambda e: e.activation(sm["th4"], sm["dd4"],
                                                     AF.Tanh, scale=0.5),
                    waits=[tdd], inc="sc")
        p.add("vector", lambda e: e.tensor_scalar(sm["g24"], sm["th4"],
                                                  0.5, 0.5, OP.mult, OP.add),
              waits=[tth])
        p.add("vector", lambda e: e.tensor_scalar(sm["g14"], sm["th4"],
                                                  -0.5, 0.5, OP.mult, OP.add))
        spacer()
        spacer()
        for j in range(4):
            p.add("vector", lambda e, j=j: e.tensor_scalar(
                sm["comb4"][:, j * 8:(j + 1) * 8], sm["glt4"][:, j * 8:(j + 1) * 8],
                sm["m14"][:, j:j + 1], sm["g14"][:, j:j + 1], OP.is_equal, OP.mult))
        spacer()
        for j in range(4):
            p.add("vector", lambda e, j=j: e.tensor_scalar(
                sm["c24"][:, j * 8:(j + 1) * 8], sm["ms4"][:, j * 8:(j + 1) * 8],
                sm["m24"][:, j:j + 1], sm["g24"][:, j:j + 1], OP.is_equal, OP.mult))
        spacer()
        p.add("vector", lambda e: e.tensor_add(sm["comb4"][:, :], sm["comb4"][:, :],
                                               sm["c24"][:, :]))
        spacer()
        spacer()
        tcb = p.add("vector", lambda e: e.tensor_copy(sm["cmtok0"][:, :], sm["comb4"][:, :]),
                    inc="sv")
        tct = vgap()
        tcdma = p.add("gpsimd", lambda e, c=c: e.dma_start(
            out=agq_in[c].ap()[D:D + 8, :].rearrange(
                "r t -> (r t)").rearrange("(p c) -> p c", p=128),
            in_=sm["cmtok0"][:, :]),
            waits=[tct], inc="sg", n=16)
        t_agq[c] = p.add("gpsimd", lambda e, c=c: e.collective_compute(
            "AllGather", OP.bypass, replica_groups=[list(range(CORES))],
            ins=[agq_in[c].ap().opt()], outs=[agq_out[c].ap().opt()]),
            waits=[tcdma, t_x1dma_prev], inc="scc")

    t_tensorA_done = p.tok("st")

    tw1a = p.add("gpsimd", lambda e: e.dma_start(
        out=AW1[:, 0:24576].rearrange("p (ki f) -> p ki f", ki=6),
        in_=w1T[0:768, :].rearrange("(ki p) f -> p ki f", p=128)),
        waits=[t_gates_done], inc="sg", n=16)
    tw1 = p.add("gpsimd", lambda e: e.dma_start(
        out=AW1[:, 24576:32768].rearrange("p (ki f) -> p ki f", ki=2),
        in_=w1T[768:1024, :].rearrange("(ki p) f -> p ki f", p=128)),
        waits=[t_tensorA_done, tw1a], inc="sg", n=16)
    tw2 = p.add("gpsimd", lambda e: e.dma_start(
        out=AW2.ap().rearrange("p (fk d) -> p fk d", fk=32),
        in_=w2T.rearrange("(fk p) d -> p fk d", p=128)),
        waits=[t_hf[-1]], inc="sg", n=16)

    h1_rot = [PA0, PA1, PSP]
    h1_evac = [pa0_free, pa1_free, rot3_evac[1]]
    h1_mm = [T0, T0, T0]
    pyb = [PY0, PY1]
    py_evac = [T0, T0]
    t_xg = {}
    t_c8 = {}
    t_xw = {}
    wsel_done = {}
    t_wsel_read = {}
    t_cmtok_free = tw1
    h1_mm_last = {-1: T0, -2: T0}
    last_rs_dma = [("sg", 0), ("sg", 0)]
    t_out_count = 0
    hro = 0
    pyo = 0
    last_ln_dma = [("so", 0), ("so", 0)]
    last_ln_read = [T0, T0]
    t_rs_q = {}
    NCI = NQ * CORES

    def xbuf(ci):
        par = ci % 2
        return lambda ki: AX[:, par * 4096 + ki * TB:par * 4096 + (ki + 1) * TB]

    def issue_loads(ci):
        lc, r = ci // CORES, ci % CORES
        par = ci % 2
        xs = xbuf(ci)
        t_xg[ci] = p.add("sync", lambda e, r=r, lc=lc, par=par: e.dma_start(
            out=AX[:, par * 4096:par * 4096 + 4096].rearrange("p (ki t) -> p ki t", ki=8),
            in_=agq_out[lc].ap()[r * (D + 8):r * (D + 8) + D, :].rearrange(
                "(ki p) t -> p ki t", p=128)),
            waits=[t_agq[lc], t_zev, h1_mm_last.get(ci - 2, T0)] +
                  ([t_tensorA_done] if ci < 2 else []), inc="ss", n=16)
        if r == 0:
            src3 = agq_out[lc].ap().rearrange(
                "(rk row) t -> rk row t", row=D + 8)[:, D:D + 8, :].rearrange(
                "rk row t -> rk (row t)").rearrange(
                "rk (p c) -> p rk c", p=128)
            t_c8[ci] = p.add("sync", lambda e, s3=src3: e.dma_start(
                out=sm["cmquar"].ap().rearrange("p (rk c) -> p rk c", c=32),
                in_=s3),
                waits=[t_agq[lc], wsel_done.get(ci - 1, T0)] +
                      ([t_cmtok_free] if ci < 2 else []), inc="ss", n=16)
        else:
            t_c8[ci] = t_c8[ci - 1]

    def issue_wb_xw(ci):
        par = ci % 2
        r_ = ci % CORES
        for tt in range(4):
            p.add("vector", lambda e, tt=tt, r_=r_: e.tensor_mul(
                sm["mk4"][:, tt * 8:(tt + 1) * 8],
                sm["cmquar"][:, r_ * 32 + tt * 8:r_ * 32 + (tt + 1) * 8],
                sm["ohm"]),
                waits=([t_c8[ci], t_wsel_read.get(ci - 2, T0)] if tt == 0 else []))
        for tt in range(4):
            p.add("vector", lambda e, tt=tt, par=par: e.reduce_sum(
                sm["wsel"][:, par * 4 + tt:par * 4 + tt + 1],
                sm["mk4"][:, tt * 8:(tt + 1) * 8], axis=AX_X))
        wsel_done[ci] = vgap()
        t_xw[ci] = t_xg[ci]

    def do_ln2(lc):
        nonlocal t_out_count
        for tp_ in range(2):
            tin = {}
            tz2 = {}
            tmv = {}
            trc2 = {}
            for u in range(2):
                tt = tp_ * 2 + u
                tx1t = p.add("scalar", lambda e, lc=lc, tt=tt, u=u: e.dma_start(
                    out=x1T_sl(u), in_=agq_in[lc].ap()[0:D, tt * 128:(tt + 1) * 128],
                    transpose=True),
                    waits=[t_agq[lc], last_ln_dma[u], last_ln_read[u]], inc="ss", n=16)
                tmo = p.add("scalar", lambda e, lc=lc, tt=tt, u=u: e.dma_start(
                    out=moe_sl(u), in_=rs_out[lc].ap()[tt * 128:(tt + 1) * 128, :]),
                    waits=[t_rs_q[lc], last_ln_dma[u], last_ln_read[u]], inc="ss", n=16)
                tin[u] = (tx1t, tmo)
            for u in range(2):
                tz2[u] = p.add("vector", lambda e, u=u: e.tensor_add(
                    z2_sl(u), x1T_sl(u), moe_sl(u)),
                    waits=list(tin[u]), inc="sv")
                last_ln_read[u] = tz2[u]
            for u in range(2):
                p.add("vector", lambda e, u=u: e.bn_stats(
                    sm["st12"][:, u * 12:u * 12 + 6], z2_sl(u)[:, 0:512]),
                    waits=[tz2[u]])
                p.add("vector", lambda e, u=u: e.bn_stats(
                    sm["st12"][:, u * 12 + 6:u * 12 + 12], z2_sl(u)[:, 512:1024]))
            for u in range(2):
                tmv[u] = p.add("vector", lambda e, u=u: e.bn_aggr(
                    sm["mv2"][:, u * 2:u * 2 + 2], sm["st12"][:, u * 12:u * 12 + 12]),
                    inc="sv")
            spacer()
            spacer()
            tmvg = p.add("vector", lambda e: e.tensor_copy(sm["dmy"][:, :], sm["tmpc"][:, :]),
                         inc="sv")
            tmv = {0: tmvg, 1: tmvg}
            for u in range(2):
                tsq2 = p.add("scalar", lambda e, u=u: e.activation(
                    sm["rstd2"][:, u:u + 1], sm["mv2"][:, u * 2 + 1:u * 2 + 2],
                    AF.Sqrt, bias=sm["epsc"][:, 0:1]), waits=[tmv[u]], inc="sc")
                trc2[u] = tsq2
            for u in range(2):
                trc2[u] = p.add("vector", lambda e, u=u: e.reciprocal(
                    sm["rstd2"][:, u:u + 1], sm["rstd2"][:, u:u + 1]),
                    waits=[trc2[u]], inc="sv")
            spacer()
            spacer()
            touts = {}
            for u in range(2):
                p.add("vector", lambda e, u=u: e.tensor_scalar(
                    z2_sl(u), z2_sl(u), sm["mv2"][:, u * 2:u * 2 + 1],
                    sm["rstd2"][:, u:u + 1], OP.subtract, OP.mult),
                    waits=[trc2[u]])
            tog = vgap()
            for u in range(2):
                touts[u] = tog
            for u in range(2):
                tt = tp_ * 2 + u
                last_ln_dma[u] = p.add("scalar", lambda e, lc=lc, tt=tt, u=u: e.dma_start(
                    out=out_d[lc * TC + tt * 128:(lc * TC + (tt + 1) * 128), :],
                    in_=z2_sl(u)), waits=[touts[u]], inc="so", n=16)
                t_out_count += 16

    issue_loads(0)
    issue_loads(1)
    issue_wb_xw(0)
    for ci in range(NCI):
        lc, r = ci // CORES, ci % CORES
        xs = xbuf(ci)
        for fi in range(FI):
            rr_ = hro % 3
            hro += 1
            tok = None
            for ki in range(8):
                tok = p.add("tensor", lambda e, ps=h1_rot[rr_], l=aw1_w1(ki, fi),
                            rhs=xs(ki), s=(ki == 0), t=(ki == 7):
                            e.matmul(ps[:, :], l, rhs, start=s, stop=t),
                            waits=([t_xw[ci], tw1, h1_evac[rr_]] if ki == 0 else []),
                            inc=("st" if ki == 7 else None))
            h1_mm[rr_] = tok
            if fi % 2 == 0:
                t_ev = p.add("vector", lambda e, ps=h1_rot[rr_], o=h1s_sl(fi, 0, TB):
                             e.tensor_scalar(o, ps[:, :], 0.0, None, OP.max),
                             waits=[tok], inc="sv")
            else:
                t_ev = p.add("scalar", lambda e, ps=h1_rot[rr_], o=h1s_sl(fi, 0, TB):
                             e.activation(o, ps[:, :], AF.Relu),
                             waits=[tok], inc="sc")
            h1_evac[rr_] = t_ev
        h1_mm_last[ci] = h1_mm[(hro - 1) % 3]
        h1s_done_v = vgap()
        h1s_done_c = p.add("scalar", lambda e: e.activation(
            sm["sdmy"][:, :], sm["b_in"][:, :], AF.Copy), inc="sc")
        if ci + 2 < NCI:
            issue_loads(ci + 2)
        if ci + 1 < NCI:
            issue_wb_xw(ci + 1)
        for tt in range(4):
            pb = pyo % 2
            pyo += 1
            tok = None
            for half in range(2):
                for fk in range(FI):
                    is_first = (half == 0 and fk == 0)
                    is_last = (half == 1 and fk == FI - 1)
                    tok = p.add("tensor", lambda e, ps=pyb[pb], h=half,
                                l=h1s_sl(fk, tt * 128, 128),
                                rhs=w2_rhs(fk, half), s=(fk == 0), t=(fk == FI - 1):
                                e.matmul(ps[:, h * 512:(h + 1) * 512], l, rhs,
                                         start=s, stop=t),
                                waits=([h1s_done_v, h1s_done_c, tw2, py_evac[pb]]
                                       if is_first else []),
                                inc=("st" if is_last else None))
            spar = tt % 2
            par = ci % 2
            p.add("vector", lambda e, ps=pyb[pb], o=rsst_sl(spar), tt=tt, par=par:
                  e.tensor_scalar(o, ps[:, :], sm["wsel"][:, par * 4 + tt:par * 4 + tt + 1],
                                  None, OP.mult),
                  waits=[tok, last_rs_dma[spar], wsel_done[ci]])
            tev = vgap()
            py_evac[pb] = tev
            t_wsel_read[ci] = tev
            last_rs_dma[spar] = p.add(
                "gpsimd", lambda e, q=lc, r=r, tt=tt, spar=spar:
                e.dma_start(out=rs_in[q].ap()[r * TC + tt * 128:
                                              r * TC + (tt + 1) * 128, :],
                            in_=rsst_sl(spar)),
                waits=[tev], inc="sg", n=16)
        if r == CORES - 1:
            t_rs_q[lc] = p.add("gpsimd", lambda e, q=lc: e.collective_compute(
                "ReduceScatter", OP.add, replica_groups=[list(range(CORES))],
                ins=[rs_in[q].ap().opt()], outs=[rs_out[q].ap().opt()]),
                waits=[last_rs_dma[0], last_rs_dma[1]], inc="scc")
        if r == 1 and lc > 0:
            do_ln2(lc - 1)
    do_ln2(NQ - 1)

    p.add("vector", lambda e: e.tensor_copy(sm["dmy"][:, :], sm["tmpc"][:, :]),
          waits=[("so", t_out_count)])

    with nc.Block() as block:
        p.emit(block)
    es.close()
    return nc


_NC_CACHE = None


def kernel(L0_attn_x, L0_attn_W_in, L0_attn_b_in, L0_attn_W_gx, L0_attn_b_gx,
           L0_attn_W_ga, L0_attn_b_ga, L0_attn_a_param, L0_attn_W_out,
           L0_attn_b_out, L0_ffn_gate_w, L0_ffn_w1, L0_ffn_w2):
    global _NC_CACHE
    f = np.float32
    x = np.asarray(L0_attn_x, f)
    xT = np.ascontiguousarray(x[0].T)

    def colmajor8(v):
        return np.ascontiguousarray(np.asarray(v, f).reshape(8, 128).T)

    W_in = np.ascontiguousarray(np.asarray(L0_attn_W_in, f))
    W_gx = np.ascontiguousarray(np.asarray(L0_attn_W_gx, f))
    W_ga = np.ascontiguousarray(np.asarray(L0_attn_W_ga, f))
    W_out = np.ascontiguousarray(np.asarray(L0_attn_W_out, f))
    gw = np.asarray(L0_ffn_gate_w, f)
    w1 = np.asarray(L0_ffn_w1, f)
    w2 = np.asarray(L0_ffn_w2, f)
    bins = colmajor8(L0_attn_b_in)
    bgxs = colmajor8(L0_attn_b_gx)
    bgas = colmajor8(L0_attn_b_ga)
    bouts = colmajor8(L0_attn_b_out)
    apars = colmajor8(L0_attn_a_param)
    idn = np.eye(128, dtype=f)
    gwp = np.ascontiguousarray(gw.T.reshape(8, 128, 8).transpose(1, 0, 2).reshape(128, 64))

    if _NC_CACHE is None:
        _NC_CACHE = build_nc()
    nc = _NC_CACHE

    in_maps = []
    for i in range(CORES):
        selc = np.zeros((128, 8), f)
        if i > 0:
            selc[:, i - 1] = 1.0
        ohm = np.tile((np.arange(8) == i).astype(f), (128, 1))
        smalls = np.concatenate([bins, bgxs, bgas, bouts, apars, selc, gwp, ohm,
                                 np.zeros((128, 8), f)], axis=1).astype(f)
        in_maps.append({
            "xT": np.ascontiguousarray(xT[:, i * TOK:(i + 1) * TOK]),
            "w_in": W_in, "w_gx": W_gx, "w_ga": W_ga, "w_out": W_out,
            "smalls_h": np.ascontiguousarray(smalls),
            "w1T": np.ascontiguousarray(w1[i].T),
            "w2T": np.ascontiguousarray(w2[i].T),
            "idn_h": idn,
        })

    res = run_bass_kernel_spmd(nc, in_maps, core_ids=list(range(CORES)))
    out = np.concatenate([res.results[i]["out"] for i in range(CORES)], axis=0)
    return out.reshape(1, S, D).astype(np.float32)


# version 09
# speedup vs baseline: 1.0681x; 1.0007x over previous
import numpy as np
from contextlib import ExitStack

import concourse.bass as bass
import concourse.mybir as mybir
from concourse.bass_utils import run_bass_kernel_spmd

F32 = mybir.dt.float32
BF16 = mybir.dt.bfloat16
AF = mybir.ActivationFunctionType
OP = mybir.AluOpType
AX_X = mybir.AxisListType.X

CORES = 8
D = 1024
S = 16384
TOK = S // CORES
TC = 512
NCHUNK = TOK // TC
FF = 4096
FI = FF // 128
TB = 512
NQ = 4
EPS = 1e-5


class P:

    ENGS = ("sync", "gpsimd", "vector", "scalar", "tensor")

    def __init__(self, nc, es):
        self.nc = nc
        self.ops = {e: [] for e in self.ENGS}
        self.sems = {}
        self.cnt = {}
        for name in ("sg", "ss", "sv", "sc", "st", "scc", "so"):
            self.sems[name] = es.enter_context(nc.semaphore(name))
            self.cnt[name] = 0

    def add(self, eng, fn, waits=(), inc=None, n=1):
        ws = tuple(w for w in waits if w is not None and w[1] > 0)
        self.ops[eng].append((fn, ws, inc, n))
        if inc is not None:
            self.cnt[inc] += n
            return (inc, self.cnt[inc])
        return None

    def tok(self, name):
        return (name, self.cnt[name])

    def _run(self, name, eng):
        seen = {}
        for fn, waits, inc, n in self.ops[name]:
            for (s, v) in waits:
                if seen.get(s, 0) < v:
                    eng.wait_ge(self.sems[s], v)
                    seen[s] = v
            instr = fn(eng)
            if inc is not None:
                instr.then_inc(self.sems[inc], n)

    def emit(self, block):
        runner = self._run

        @block.sync
        def _(e):
            runner("sync", e)

        @block.gpsimd
        def _(e):
            runner("gpsimd", e)

        @block.vector
        def _(e):
            runner("vector", e)

        @block.scalar
        def _(e):
            runner("scalar", e)

        @block.tensor
        def _(e):
            runner("tensor", e)


def build_nc():
    nc = bass.Bass("TRN2", target_bir_lowering=False, detect_race_conditions=False)

    xT = nc.dram_tensor("xT", [D, TOK], F32, kind="ExternalInput").ap()
    w_in = nc.dram_tensor("w_in", [D, D], F32, kind="ExternalInput").ap()
    w_gx = nc.dram_tensor("w_gx", [D, D], F32, kind="ExternalInput").ap()
    w_ga = nc.dram_tensor("w_ga", [D, D], F32, kind="ExternalInput").ap()
    w_out = nc.dram_tensor("w_out", [D, D], F32, kind="ExternalInput").ap()
    smalls_h = nc.dram_tensor("smalls_h", [128, 128], F32, kind="ExternalInput").ap()
    w1T = nc.dram_tensor("w1T", [D, FF], F32, kind="ExternalInput").ap()
    w2T = nc.dram_tensor("w2T", [FF, D], F32, kind="ExternalInput").ap()
    idn_h = nc.dram_tensor("idn_h", [128, 128], F32, kind="ExternalInput").ap()
    out_d = nc.dram_tensor("out", [TOK, D], F32, kind="ExternalOutput").ap()

    ag1_in = nc.dram_tensor("ag1_in", [2, D], F32)
    ag1_out = nc.dram_tensor("ag1_out", [2 * CORES, D], F32)
    agq_in = [nc.dram_tensor(f"agq_in_{q}", [D + 8, TC], BF16) for q in range(NQ)]
    agq_out = [nc.dram_tensor(f"agq_out_{q}", [(D + 8) * CORES, TC], BF16) for q in range(NQ)]
    rs_in = [nc.dram_tensor(f"rs_in_{q}", [CORES * TC, D], BF16) for q in range(NQ)]
    rs_out = [nc.dram_tensor(f"rs_out_{q}", [TC, D], BF16) for q in range(NQ)]

    es = ExitStack()

    AW1 = es.enter_context(nc.sbuf_tensor("AW1", [128, 32768], BF16))
    AW2 = es.enter_context(nc.sbuf_tensor("AW2", [128, 32768], BF16))
    AX = es.enter_context(nc.sbuf_tensor("AX", [128, 8192], BF16))
    AB = es.enter_context(nc.sbuf_tensor("AB", [128, 14336], BF16))
    AFr = es.enter_context(nc.sbuf_tensor("AF", [128, 8192], F32))
    AFb = AFr.ap().bitcast(BF16)
    ABf = AB.ap().bitcast(F32)

    SML = es.enter_context(nc.sbuf_tensor("SML", [128, 128], F32))
    sm = {}
    sm["b_in"] = SML[:, 0:8]
    sm["bgx_hf"] = SML[:, 8:16]
    sm["bga_hf"] = SML[:, 16:24]
    sm["b_out"] = SML[:, 24:32]
    sm["aptmp"] = SML[:, 32:40]
    sm["selc"] = SML[:, 40:48]
    sm["gwtf"] = SML[:, 48:112]
    sm["ohm"] = SML[:, 112:120]
    for nm, shp, dt in [
        ("msp_h", [128, 8], F32),
        ("hcar", [128, 8], F32), ("apcar", [128, 8], F32),
        ("cin", [128, 8], F32), ("tmpc", [128, 8], F32), ("dmy", [128, 8], F32),
        ("aesum", [128, 128], F32), ("cpre", [128, 64], F32),
        ("idn", [128, 128], F32),
        ("onesc", [128, 1], BF16), ("onesf", [1, 128], F32),
        ("glt4", [128, 32], F32), ("mk4", [128, 32], F32),
        ("ms4", [128, 32], F32), ("comb4", [128, 32], F32),
        ("st12", [128, 24], F32), ("mv2", [128, 4], F32), ("rstd2", [128, 2], F32),
        ("cmtok0", [128, 32], BF16), ("cmquar", [128, 256], BF16),
        ("wsel", [128, 8], F32), ("sdmy", [128, 8], F32),
        ("epsc", [128, 1], F32), ("onec", [128, 1], F32), ("spt", [128, 8], F32),
    ]:
        sm[nm] = es.enter_context(nc.sbuf_tensor(nm, shp, dt))
    sm["asum"] = sm["aesum"][:, 0:64]
    sm["esum"] = sm["aesum"][:, 64:128]
    sm["c24"] = sm["mk4"]
    sm["m14"] = sm["st12"][:, 0:4]
    sm["m24"] = sm["st12"][:, 4:8]
    sm["dd4"] = sm["st12"][:, 8:12]
    sm["th4"] = sm["st12"][:, 12:16]
    sm["g14"] = sm["st12"][:, 16:20]
    sm["g24"] = sm["st12"][:, 20:24]

    PA0 = es.enter_context(nc.psum_tensor("PA0", [128, 512], F32))
    PA1 = es.enter_context(nc.psum_tensor("PA1", [128, 512], F32))
    PA2 = es.enter_context(nc.psum_tensor("PA2", [128, 512], F32))
    PSP = es.enter_context(nc.psum_tensor("PSP", [128, 512], F32))
    PY0 = es.enter_context(nc.psum_tensor("PY0", [128, 1024], F32))
    PY1 = es.enter_context(nc.psum_tensor("PY1", [128, 1024], F32))

    p = P(nc, es)
    T0 = ("sv", 0)

    def aw1_w(base, ki, ii):
        c = base + ki * 1024 + ii * 128
        return AW1[:, c:c + 128]

    def aw1_w1(ki, fi):
        c = ki * 4096 + fi * 128
        return AW1[:, c:c + 128]

    def hloc_sl(ii, c):
        a = ii * TOK + c * TC
        return AW2[:, a:a + TC]

    def apb_sl(ii, c):
        a = 16384 + ii * TOK + c * TC
        return AW2[:, a:a + TC]

    def w2_rhs(fk, half):
        a = fk * 1024 + half * 512
        return AW2[:, a:a + 512]

    def x0b_sl(ki):
        return AX[:, ki * TC:(ki + 1) * TC]

    def xinb_sl(ii):
        return AX[:, 4096 + ii * TC:4096 + (ii + 1) * TC]

    def xinf_sl(ii):
        return AFr[:, ii * TC:(ii + 1) * TC]

    def afull_sl(ii):
        return AFr[:, 4096 + ii * TC:4096 + (ii + 1) * TC]

    def slot(k):
        return ABf[:, 6144 + k * 512:6144 + (k + 1) * 512]

    def ufull_sl(ii):
        return AB[:, ii * TC:(ii + 1) * TC]

    def xfull_sl(ii):
        return AB[:, 8192 + ii * 512:8192 + (ii + 1) * 512]

    def hfix_sl(ii):
        return AB[:, ii * TC:(ii + 1) * TC]

    def x1c_sl(ii):
        return AB[:, 4096 + ii * TC:4096 + (ii + 1) * TC]

    def zb_slot(k):
        return AB[:, 8192 + k * 512:8192 + (k + 1) * 512]

    def zsq_slot(k):
        return AB[:, 9216 + k * 512:9216 + (k + 1) * 512]

    def z_sl(ii):
        return AFr[:, ii * TC:(ii + 1) * TC]

    def rows(k):
        return AFr[0:1, 4096 + k * 512:4096 + (k + 1) * 512]

    def x1tmp():
        return AFr[:, 6144:6656]

    def glog_sl():
        return ABf[0:8, 5120:5632]

    def combT_sl(c):
        return AB[0:8, 12288 + c * TC:12288 + (c + 1) * TC]

    def xg_sl(ki):
        return AX[:, ki * TB:(ki + 1) * TB]

    def xw_sl(ki):
        return AX[:, 4096 + ki * TB:4096 + (ki + 1) * TB]

    def h1s_sl(fk, off, ln):
        a = fk * TB + off
        return AFb[:, a:a + ln]

    def x1T_sl(par):
        return AB[:, par * 1024:(par + 1) * 1024]

    def moe_sl(par):
        return AB[:, 2048 + par * 1024:2048 + (par + 1) * 1024]

    def rsst_sl(par):
        return AB[:, 4096 + par * 1024:4096 + (par + 1) * 1024]

    def wb_sl(par):
        return AB[:, 6144 + par * 512:6144 + (par + 1) * 512]

    def cm8_sl(par):
        return AB[0:8, 7168 + par * 512:7168 + (par + 1) * 512]

    def cm8_full(par):
        return AB[:, 7168 + par * 512:7168 + (par + 1) * 512]

    def z2_sl(par):
        return ABf[:, 4096 + par * 1024:4096 + (par + 1) * 1024]

    def spacer():
        p.add("vector", lambda e: e.tensor_copy(sm["dmy"][:, :], sm["tmpc"][:, :]))

    def vgap():
        spacer()
        return p.add("vector", lambda e: e.tensor_copy(sm["dmy"][:, :], sm["tmpc"][:, :]),
                     inc="sv")

    t_w = {}
    for nm, src, base in [("in", w_in, 0), ("gx", w_gx, 8192),
                          ("ga", w_ga, 16384), ("out", w_out, 24576)]:
        src3 = src.rearrange("(ki p) c -> p ki c", p=128)
        dst3 = AW1[:, base:base + 8192].rearrange("p (ki c) -> p ki c", ki=8)
        t_w[nm] = p.add("gpsimd",
                        lambda e, o=dst3, i=src3: e.dma_start(out=o, in_=i),
                        inc="sg", n=16)


    t_small = p.add("sync", lambda e: e.dma_start(out=SML[:, :], in_=smalls_h),
                    inc="ss", n=16)
    t_idn = p.add("sync", lambda e: e.dma_start(out=sm["idn"][:, :], in_=idn_h),
                  waits=[t_small], inc="ss", n=16)
    t_gwtf = t_small
    small_loads = [t_small] * 6
    t_sml = t_small

    p.add("vector", lambda e: e.memset(sm["hcar"][:, :], 0.0))
    p.add("vector", lambda e: e.memset(sm["apcar"][:, :], 1.0))
    p.add("vector", lambda e: e.memset(sm["cin"][:, :], 0.0))
    p.add("vector", lambda e: e.memset(sm["tmpc"][:, :], 0.0))
    p.add("vector", lambda e: e.memset(sm["onesc"][:, :], 1.0))
    p.add("vector", lambda e: e.memset(sm["epsc"][:, :], EPS))
    p.add("vector", lambda e: e.memset(sm["onec"][:, :], 1.0))
    t_ones = p.add("vector", lambda e: e.memset(sm["onesf"][:, :], 1.0), inc="sv")
    def sgap(n=8):
        for _ in range(n):
            p.add("scalar", lambda e: e.activation(sm["sdmy"][:, :], sm["b_in"][:, :],
                                                   AF.Copy))
    ta1 = p.add("scalar", lambda e: e.activation(
        sm["spt"][:, :], sm["aptmp"][:, :], AF.Abs),
        waits=[small_loads[4]], inc="sc")
    t_r = p.add("scalar", lambda e: e.activation(
        sm["msp_h"][:, :], sm["aptmp"][:, :], AF.Relu), inc="sc")
    sgap()
    t_e = p.add("scalar", lambda e: e.activation(
        sm["aptmp"][:, :], sm["spt"][:, :], AF.Exp, scale=-1.0), inc="sc")
    sgap()
    t_l = p.add("scalar", lambda e: e.activation(
        sm["spt"][:, :], sm["aptmp"][:, :], AF.Ln, bias=sm["onec"][:, 0:1]),
        waits=[t_ones], inc="sc")
    sgap()
    tad = p.add("vector", lambda e: e.tensor_add(
        sm["msp_h"][:, :], sm["msp_h"][:, :], sm["spt"][:, :]),
        waits=[t_l, t_r], inc="sv")
    spacer()
    spacer()
    t_msp = p.add("vector", lambda e: e.tensor_scalar_mul(
        sm["msp_h"][:, :], sm["msp_h"][:, :], -4.0), waits=[tad], inc="sv")
    t_bgxh = p.add("vector", lambda e: e.tensor_scalar_mul(
        sm["bgx_hf"][:, :], sm["bgx_hf"][:, :], 0.5), waits=[small_loads[1]], inc="sv")
    t_bgah = p.add("vector", lambda e: e.tensor_scalar_mul(
        sm["bga_hf"][:, :], sm["bga_hf"][:, :], 0.5), waits=[small_loads[2]], inc="sv")

    PY0a, PY0b = PY0[:, 0:512], PY0[:, 512:1024]
    PY1a, PY1b = PY1[:, 0:512], PY1[:, 512:1024]
    rotors = [PA0[:, :], PA1[:, :], PA2[:, :], PSP[:, :], PY0a, PY0b, PY1a, PY1b]
    rot_evac = [T0] * 8
    rot_i = 0

    def next_rot():
        nonlocal rot_i
        r = rot_i % 8
        rot_i += 1
        return r

    t_x0 = [None] * NCHUNK
    t_xin_mm_last = T0
    slot_free = [T0, T0]
    t_u_done = {}
    t_x_done = {}
    t_sct = {}

    for c in range(NCHUNK):
        wx = [t_xin_mm_last] if c > 0 else []
        src3 = xT.rearrange("(ki p) t -> p ki t", p=128)[:, :, c * TC:(c + 1) * TC]
        dst3 = AX[:, 0:4096].rearrange("p (ki t) -> p ki t", ki=8)
        t_x0[c] = p.add("gpsimd", lambda e, o=dst3, i=src3: e.dma_start(out=o, in_=i),
                        waits=wx, inc="sg", n=16)

        t_evac_xin = []
        for ii in range(8):
            r = next_rot()
            tok = None
            for ki in range(8):
                tok = p.add("tensor", lambda e, ps=rotors[r], l=aw1_w(0, ki, ii),
                            rr=x0b_sl(ki), s=(ki == 0), t=(ki == 7):
                            e.matmul(ps, l, rr, start=s, stop=t),
                            waits=([t_x0[c], t_w["in"], rot_evac[r]] if ki == 0 else []),
                            inc=("st" if ki == 7 else None))
            p.add("vector", lambda e, ps=rotors[r], o=xinf_sl(ii), b=sm["b_in"][:, ii:ii + 1]:
                  e.tensor_scalar_add(o, ps, b), waits=[tok, t_sml])
            e2 = p.add("vector", lambda e, ps=rotors[r], o=xinb_sl(ii), b=sm["b_in"][:, ii:ii + 1]:
                       e.tensor_scalar_add(o, ps, b), inc="sv")
            rot_evac[r] = e2
            t_evac_xin.append(e2)
        t_evac_xin_g = vgap()
        t_xin_mm_last = p.tok("st")

        for ii in range(8):
            sA = slot(0)
            sB = slot(1)
            rA = next_rot()
            tokA = None
            for ki in range(8):
                tokA = p.add("tensor", lambda e, ps=rotors[rA], l=aw1_w(16384, ki, ii),
                             rr=xinb_sl(ki), s=(ki == 0), t=(ki == 7):
                             e.matmul(ps, l, rr, start=s, stop=t),
                             waits=([t_evac_xin_g, t_w["ga"], rot_evac[rA]] if ki == 0 else []),
                             inc=("st" if ki == 7 else None))
            rB = next_rot()
            tokB = None
            for ki in range(8):
                tokB = p.add("tensor", lambda e, ps=rotors[rB], l=aw1_w(8192, ki, ii),
                             rr=xinb_sl(ki), s=(ki == 0), t=(ki == 7):
                             e.matmul(ps, l, rr, start=s, stop=t),
                             waits=([t_w["gx"], rot_evac[rB]] if ki == 0 else []),
                             inc=("st" if ki == 7 else None))
            ta = p.add("scalar", lambda e, ps=rotors[rA], o=sB, b=sm["bga_hf"][:, ii:ii + 1]:
                       e.activation(o, ps, AF.Tanh, bias=b, scale=0.5),
                       waits=[tokA, t_bgah, slot_free[1]], inc="sc")
            rot_evac[rA] = ta
            tb = p.add("scalar", lambda e, ps=rotors[rB], o=sA, b=sm["bgx_hf"][:, ii:ii + 1]:
                       e.activation(o, ps, AF.Tanh, bias=b, scale=0.5),
                       waits=[tokB, t_bgxh, slot_free[0]], inc="sc")
            rot_evac[rB] = tb
            tl = p.add("vector", lambda e, s=sB, m=sm["msp_h"][:, ii:ii + 1]:
                       e.tensor_scalar(s, s, m, m, OP.mult, OP.add),
                       waits=[ta, t_msp], inc="sv")
            tu = p.add("vector", lambda e, s=sA, o=ufull_sl(ii), xi=xinf_sl(ii):
                       e.scalar_tensor_tensor(o, s, 1.0, xi, OP.add, OP.mult),
                       waits=[tb], inc="sv")
            t_u_done[(c, ii)] = tu
            p.add("scalar", lambda e, s=sB, o=afull_sl(ii):
                  e.activation(o, s, AF.Exp), waits=[tu], inc="sc")
            tq = p.add("scalar", lambda e, s=sB, o=sA:
                       e.activation(o, s, AF.Exp, scale=2.0), waits=[tu], inc="sc")
            slot_free[1] = tq
            p.add("vector", lambda e, s=sA: e.tensor_scalar(
                s, s, -1.0, 1.0, OP.mult, OP.add), waits=[tq])
            p.add("vector", lambda e, s=sA: e.tensor_scalar_max(s, s, 0.0))
            tx3 = p.add("vector", lambda e, s=sA, o=xfull_sl(ii): e.tensor_copy(o, s),
                        inc="sv")
            slot_free[0] = tx3
            t_x_done[(c, ii)] = tx3

        for ii in range(8):
            t_sct[(c, ii)] = p.add("scalar", lambda e, s=xfull_sl(ii):
                                   e.activation(s, s, AF.Sqrt, scale=0.25),
                                   waits=[t_x_done[(c, ii)]], inc="sc")

        for i0 in range(0, 8, 2):
            prs = [i0, i0 + 1]
            bsl = {ii: AFr[:, (ii % 2) * 512:(ii % 2) * 512 + 512] for ii in prs}
            hsl = {ii: AFr[:, 1024 + (ii % 2) * 512:1024 + (ii % 2) * 512 + 512] for ii in prs}
            asl = {ii: AFr[:, 2048 + (ii % 2) * 512:2048 + (ii % 2) * 512 + 512] for ii in prs}
            for ii in prs:
                p.add("vector", lambda e, o=bsl[ii], u=ufull_sl(ii), s=xfull_sl(ii):
                      e.tensor_mul(o, u, s),
                      waits=[t_sct[(c, ii)], t_u_done[(c, ii)]])
            for ii in prs:
                p.add("vector", lambda e, o=hsl[ii], a=afull_sl(ii), b=bsl[ii],
                      i0_=sm["hcar"][:, ii:ii + 1]:
                      e.tensor_tensor_scan(o, a, b, i0_, OP.mult, OP.add))
            for ii in prs:
                p.add("vector", lambda e, o=asl[ii], a=afull_sl(ii),
                      i0_=sm["apcar"][:, ii:ii + 1]:
                      e.tensor_tensor_scan(o, a, a, i0_, OP.mult, OP.bypass))
            for ii in prs:
                p.add("vector", lambda e, o=hloc_sl(ii, c), i=hsl[ii]: e.tensor_copy(o, i))
            for ii in prs:
                p.add("vector", lambda e, o=apb_sl(ii, c), i=asl[ii]: e.tensor_copy(o, i))
            for ii in prs:
                p.add("vector", lambda e, o=sm["hcar"][:, ii:ii + 1], i=hsl[ii][:, 511:512]:
                      e.tensor_copy(o, i))
            for ii in prs:
                p.add("vector", lambda e, o=sm["apcar"][:, ii:ii + 1], i=asl[ii][:, 511:512]:
                      e.tensor_copy(o, i), inc=("sv" if ii == i0 + 1 else None))

    t_gates_done = p.tok("st")
    t_lastscan = vgap()
    ag1i3 = ag1_in.ap().rearrange("r (p ii) -> r p ii", ii=8)
    tg1 = p.add("gpsimd", lambda e: e.dma_start(out=ag1i3[0], in_=sm["apcar"][:, :]),
                waits=[t_lastscan], inc="sg", n=16)
    tg2 = p.add("gpsimd", lambda e: e.dma_start(out=ag1i3[1], in_=sm["hcar"][:, :]),
                inc="sg", n=16)
    t_ag1 = p.add("gpsimd", lambda e: e.collective_compute(
        "AllGather", OP.bypass, replica_groups=[list(range(CORES))],
        ins=[ag1_in.ap().opt()], outs=[ag1_out.ap().opt()]),
        waits=[tg1, tg2], inc="scc")
    ag1o3 = ag1_out.ap().rearrange("(r two) (p ii) -> two p r ii", two=2, ii=8)
    tga = p.add("gpsimd", lambda e: e.dma_start(
        out=sm["asum"].rearrange("p (r ii) -> p r ii", ii=8), in_=ag1o3[0]),
        waits=[t_ag1], inc="sg", n=16)
    tge = p.add("gpsimd", lambda e: e.dma_start(
        out=sm["esum"].rearrange("p (r ii) -> p r ii", ii=8), in_=ag1o3[1]),
        waits=[tga], inc="sg", n=16)
    p.add("vector", lambda e: e.tensor_copy(sm["cpre"][:, 0:8], sm["esum"][:, 0:8]),
          waits=[tge])
    for k in range(1, 7):
        spacer()
        p.add("vector", lambda e, k=k: e.tensor_mul(
            sm["tmpc"][:, :], sm["asum"][:, k * 8:(k + 1) * 8],
            sm["cpre"][:, (k - 1) * 8:k * 8]))
        spacer()
        p.add("vector", lambda e, k=k: e.tensor_add(
            sm["cpre"][:, k * 8:(k + 1) * 8], sm["tmpc"][:, :],
            sm["esum"][:, k * 8:(k + 1) * 8]))
    for k in range(7):
        spacer()
        spacer()
        p.add("vector", lambda e, k=k: e.scalar_tensor_tensor(
            sm["cin"][:, :], sm["cpre"][:, k * 8:(k + 1) * 8],
            sm["selc"][:, k:k + 1], sm["cin"][:, :], OP.mult, OP.add))
    spacer()
    spacer()
    t_cin = p.add("vector", lambda e: e.tensor_copy(sm["dmy"][:, :], sm["cin"][:, :]),
                  inc="sv")

    rot3 = [PY0a, PY0b, PY1a, PY1b]
    rot3_evac = [rot_evac[4], rot_evac[5], rot_evac[6], rot_evac[7]]
    psp_free = rot_evac[3]
    pa2_free = rot_evac[2]
    pa0_free = rot_evac[0]
    pa1_free = rot_evac[1]
    t_glog_mm = T0
    t_agq = [None] * NQ
    t_x1dma_prev = ("sg", 0)
    t_zev = T0
    t_wout_mm = T0
    t_hf = []
    tct = T0

    for c in range(NCHUNK):
        src3 = xT.rearrange("(ki p) t -> p ki t", p=128)[:, :, c * TC:(c + 1) * TC]
        dst3 = AX[:, 0:4096].rearrange("p (ki t) -> p ki t", ki=8)
        t_x0r = p.add("gpsimd", lambda e, o=dst3, i=src3: e.dma_start(out=o, in_=i),
                      waits=[t_lastscan] + ([t_zev] if c > 0 else []), inc="sg", n=16)
        t_hf = []
        for ii in range(8):
            t_hf.append(p.add("vector", lambda e, o=hfix_sl(ii), a=apb_sl(ii, c),
                              s=sm["cin"][:, ii:ii + 1], h=hloc_sl(ii, c):
                              e.scalar_tensor_tensor(o, a, s, h, OP.mult, OP.add),
                              waits=[t_cin] + ([t_wout_mm] if c > 0 else []), inc="sv"))
        t_hf_g = vgap()
        t_zs = []
        for ii in range(8):
            r = ii % 4
            tok = None
            for ki in range(8):
                tok = p.add("tensor", lambda e, ps=rot3[r], l=aw1_w(24576, ki, ii),
                            rr=hfix_sl(ki), s=(ki == 0), t=(ki == 7):
                            e.matmul(ps, l, rr, start=s, stop=t),
                            waits=([t_hf_g, t_w["out"], rot3_evac[r]] if ki == 0 else []),
                            inc=("st" if ki == 7 else None))
            tz = p.add("vector", lambda e, ps=rot3[r], o=z_sl(ii),
                       b=sm["b_out"][:, ii:ii + 1], x0=x0b_sl(ii):
                       e.scalar_tensor_tensor(o, ps, b, x0, OP.add, OP.add),
                       waits=[tok, t_x0r], inc="sv")
            rot3_evac[r] = tz
            t_zs.append(tz)
        t_wout_mm = p.tok("st")
        t_zev = t_zs[-1]
        t_slot_mm = [T0, T0]
        tok_sum = tok_sq = None
        for ii in range(8):
            k = ii % 2
            p.add("vector", lambda e, o=zb_slot(k), z=z_sl(ii): e.tensor_copy(o, z),
                  waits=[t_zs[ii], t_slot_mm[k]])
            p.add("vector", lambda e, o=zsq_slot(k), z=z_sl(ii): e.tensor_mul(o, z, z))
            tzb = tzq = vgap()
            tok_sum = p.add("tensor", lambda e, rr=zb_slot(k), s=(ii == 0), t=(ii == 7):
                            e.matmul(PA0[0:1, :], sm["onesc"][:, :], rr, start=s, stop=t),
                            waits=[tzb, t_ones] + ([pa0_free] if ii == 0 else []),
                            inc="st")
            tok_sq = p.add("tensor", lambda e, rr=zsq_slot(k), s=(ii == 0), t=(ii == 7):
                           e.matmul(PA1[0:1, :], sm["onesc"][:, :], rr, start=s, stop=t),
                           waits=[tzq] + ([pa1_free] if ii == 0 else []),
                           inc="st")
            t_slot_mm[k] = tok_sq
        trm = p.add("vector", lambda e: e.tensor_scalar_mul(rows(0), PA0[0:1, :], 1.0 / D),
                    waits=[tok_sum], inc="sv")
        p.add("vector", lambda e: e.tensor_scalar_mul(rows(1), PA1[0:1, :], 1.0 / D),
              waits=[tok_sq])
        spacer()
        p.add("vector", lambda e: e.tensor_mul(rows(2), rows(0), rows(0)))
        spacer()
        p.add("vector", lambda e: e.tensor_sub(rows(1), rows(1), rows(2)))
        spacer()
        spacer()
        tvar = p.add("vector", lambda e: e.tensor_copy(sm["dmy"][:, :], sm["tmpc"][:, :]),
                     inc="sv")
        tsq = p.add("scalar", lambda e: e.activation(rows(2), rows(1), AF.Sqrt, bias=sm["epsc"][0:1, 0:1]),
                    waits=[tvar], inc="sc")
        trc = p.add("vector", lambda e: e.reciprocal(rows(3), rows(2)), waits=[tsq],
                    inc="sv")
        spacer()
        spacer()
        p.add("vector", lambda e: e.tensor_mul(rows(2), rows(0), rows(3)))
        tmr = vgap()
        tbr = p.add("tensor", lambda e: e.matmul(PA1[:, :], sm["onesf"][:, :], rows(3),
                                                 start=True, stop=True),
                    waits=[tmr, tvar, t_ones], inc="st")
        tbm = p.add("tensor", lambda e: e.matmul(PA0[:, :], sm["onesf"][:, :], rows(2),
                                                 start=True, stop=True),
                    waits=[tmr, trm], inc="st")
        tx1last = None
        for ii in range(8):
            p.add("vector", lambda e, z=z_sl(ii): e.tensor_mul(x1tmp(), z, PA1[:, :]),
                  waits=([tbr, tbm, t_glog_mm, t_x1dma_prev] if ii == 0 else []))
            p.add("vector", lambda e, z=z_sl(ii): e.tensor_sub(z, x1tmp(), PA0[:, :]))
            p.add("vector", lambda e, o=x1c_sl(ii), z=z_sl(ii): e.tensor_copy(o, z))
            tx1last = None
        tx1last = vgap()
        pa0_free = tx1last
        pa1_free = tx1last
        t_x1dma_prev = p.add("gpsimd", lambda e, c=c: e.dma_start(
            out=agq_in[c].ap()[0:D, :].rearrange("(ki p) t -> p ki t", p=128),
            in_=AB[:, 4096:8192].rearrange("p (ki t) -> p ki t", ki=8)),
            waits=[tx1last], inc="sg", n=16)
        tok = None
        for ki in range(8):
            tok = p.add("tensor", lambda e, l=sm["gwtf"][:, ki * 8:(ki + 1) * 8],
                        rr=z_sl(ki), s=(ki == 0), t=(ki == 7):
                        e.matmul(PSP[0:8, :], l, rr, start=s, stop=t),
                        waits=([tx1last, t_gwtf, psp_free] if ki == 0 else []),
                        inc=("st" if ki == 7 else None))
        t_glog_mm = tok
        tgl = p.add("vector", lambda e: e.tensor_copy(glog_sl(), PSP[0:8, :]),
                    waits=[tok], inc="sv")
        psp_free = tgl
        tprev = pa2_free
        for j in range(4):
            ttr = p.add("tensor", lambda e, j=j: e.transpose(
                PA2[:, 0:8], glog_sl()[0:8, j * 128:(j + 1) * 128],
                sm["idn"][0:8, 0:8]),
                waits=[tgl, t_idn, tprev], inc="st")
            tprev = p.add("vector", lambda e, j=j: e.tensor_copy(
                sm["glt4"][:, j * 8:(j + 1) * 8], PA2[:, 0:8]),
                waits=[ttr], inc="sv")
        pa2_free = tprev
        for j in range(4):
            p.add("vector", lambda e, j=j: e.reduce_max(
                sm["m14"][:, j:j + 1], sm["glt4"][:, j * 8:(j + 1) * 8], axis=AX_X))
        spacer()
        spacer()
        for j in range(4):
            p.add("vector", lambda e, j=j: e.tensor_scalar(
                sm["mk4"][:, j * 8:(j + 1) * 8], sm["glt4"][:, j * 8:(j + 1) * 8],
                sm["m14"][:, j:j + 1], 1.0, OP.is_equal, OP.mult))
        spacer()
        for j in range(4):
            p.add("vector", lambda e, j=j: e.scalar_tensor_tensor(
                sm["ms4"][:, j * 8:(j + 1) * 8], sm["mk4"][:, j * 8:(j + 1) * 8],
                -1e30, sm["glt4"][:, j * 8:(j + 1) * 8], OP.mult, OP.add))
        spacer()
        for j in range(4):
            p.add("vector", lambda e, j=j: e.reduce_max(
                sm["m24"][:, j:j + 1], sm["ms4"][:, j * 8:(j + 1) * 8], axis=AX_X))
        spacer()
        spacer()
        p.add("vector", lambda e: e.tensor_sub(sm["dd4"], sm["m24"],
                                               sm["m14"]))
        spacer()
        spacer()
        tdd = p.add("vector", lambda e: e.tensor_copy(sm["dmy"][:, :], sm["tmpc"][:, :]),
                    inc="sv")
        tth = p.add("scalar", lambda e: e.activation(sm["th4"], sm["dd4"],
                                                     AF.Tanh, scale=0.5),
                    waits=[tdd], inc="sc")
        p.add("vector", lambda e: e.tensor_scalar(sm["g24"], sm["th4"],
                                                  0.5, 0.5, OP.mult, OP.add),
              waits=[tth])
        p.add("vector", lambda e: e.tensor_scalar(sm["g14"], sm["th4"],
                                                  -0.5, 0.5, OP.mult, OP.add))
        spacer()
        spacer()
        for j in range(4):
            p.add("vector", lambda e, j=j: e.tensor_scalar(
                sm["comb4"][:, j * 8:(j + 1) * 8], sm["glt4"][:, j * 8:(j + 1) * 8],
                sm["m14"][:, j:j + 1], sm["g14"][:, j:j + 1], OP.is_equal, OP.mult))
        spacer()
        for j in range(4):
            p.add("vector", lambda e, j=j: e.tensor_scalar(
                sm["c24"][:, j * 8:(j + 1) * 8], sm["ms4"][:, j * 8:(j + 1) * 8],
                sm["m24"][:, j:j + 1], sm["g24"][:, j:j + 1], OP.is_equal, OP.mult))
        spacer()
        p.add("vector", lambda e: e.tensor_add(sm["comb4"][:, :], sm["comb4"][:, :],
                                               sm["c24"][:, :]))
        spacer()
        spacer()
        tcb = p.add("vector", lambda e: e.tensor_copy(sm["cmtok0"][:, :], sm["comb4"][:, :]),
                    inc="sv")
        tct = vgap()
        tcdma = p.add("gpsimd", lambda e, c=c: e.dma_start(
            out=agq_in[c].ap()[D:D + 8, :].rearrange(
                "r t -> (r t)").rearrange("(p c) -> p c", p=128),
            in_=sm["cmtok0"][:, :]),
            waits=[tct], inc="sg", n=16)
        t_agq[c] = p.add("gpsimd", lambda e, c=c: e.collective_compute(
            "AllGather", OP.bypass, replica_groups=[list(range(CORES))],
            ins=[agq_in[c].ap().opt()], outs=[agq_out[c].ap().opt()]),
            waits=[tcdma, t_x1dma_prev], inc="scc")

    t_tensorA_done = p.tok("st")

    tw1a = p.add("gpsimd", lambda e: e.dma_start(
        out=AW1[:, 0:24576].rearrange("p (ki f) -> p ki f", ki=6),
        in_=w1T[0:768, :].rearrange("(ki p) f -> p ki f", p=128)),
        waits=[t_gates_done], inc="sg", n=16)
    tw1 = p.add("gpsimd", lambda e: e.dma_start(
        out=AW1[:, 24576:32768].rearrange("p (ki f) -> p ki f", ki=2),
        in_=w1T[768:1024, :].rearrange("(ki p) f -> p ki f", p=128)),
        waits=[t_tensorA_done, tw1a], inc="sg", n=16)
    tw2 = p.add("gpsimd", lambda e: e.dma_start(
        out=AW2.ap().rearrange("p (fk d) -> p fk d", fk=32),
        in_=w2T.rearrange("(fk p) d -> p fk d", p=128)),
        waits=[t_hf[-1]], inc="sg", n=16)

    h1_rot = [PA0, PA1, PA2, PSP]
    h1_evac = [pa0_free, pa1_free, pa2_free, psp_free]
    h1_mm = [T0, T0, T0, T0]
    pyb = [PY0, PY1]
    py_evac = [rot3_evac[1], rot3_evac[3]]
    py_mm_init = [rot3_evac[0], rot3_evac[2]]
    py_first = [True, True]
    t_xg = {}
    t_c8 = {}
    t_xw = {}
    wsel_done = {}
    t_wsel_read = {}
    t_cmtok_free = tw1
    h1_mm_last = {-1: T0, -2: T0}
    last_rs_dma = [("sg", 0), ("sg", 0)]
    t_out_count = 0
    hro = 0
    pyo = 0
    last_ln_dma = [("so", 0), ("so", 0)]
    last_ln_read = [T0, T0]
    t_rs_q = {}
    NCI = NQ * CORES

    def xbuf(ci):
        par = ci % 2
        return lambda ki: AX[:, par * 4096 + ki * TB:par * 4096 + (ki + 1) * TB]

    def issue_loads(ci):
        lc, r = ci // CORES, ci % CORES
        par = ci % 2
        xs = xbuf(ci)
        t_xg[ci] = p.add("sync", lambda e, r=r, lc=lc, par=par: e.dma_start(
            out=AX[:, par * 4096:par * 4096 + 4096].rearrange("p (ki t) -> p ki t", ki=8),
            in_=agq_out[lc].ap()[r * (D + 8):r * (D + 8) + D, :].rearrange(
                "(ki p) t -> p ki t", p=128)),
            waits=[t_agq[lc], t_zev, h1_mm_last.get(ci - 2, T0)] +
                  ([t_tensorA_done] if ci < 2 else []), inc="ss", n=16)
        if r == 0:
            src3 = agq_out[lc].ap().rearrange(
                "(rk row) t -> rk row t", row=D + 8)[:, D:D + 8, :].rearrange(
                "rk row t -> rk (row t)").rearrange(
                "rk (p c) -> p rk c", p=128)
            t_c8[ci] = p.add("sync", lambda e, s3=src3: e.dma_start(
                out=sm["cmquar"].ap().rearrange("p (rk c) -> p rk c", c=32),
                in_=s3),
                waits=[t_agq[lc], wsel_done.get(ci - 1, T0)] +
                      ([t_cmtok_free] if ci < 2 else []), inc="ss", n=16)
        else:
            t_c8[ci] = t_c8[ci - 1]

    def issue_wb_xw(ci):
        par = ci % 2
        r_ = ci % CORES
        for tt in range(4):
            p.add("vector", lambda e, tt=tt, r_=r_: e.tensor_mul(
                sm["mk4"][:, tt * 8:(tt + 1) * 8],
                sm["cmquar"][:, r_ * 32 + tt * 8:r_ * 32 + (tt + 1) * 8],
                sm["ohm"]),
                waits=([t_c8[ci], t_wsel_read.get(ci - 2, T0)] if tt == 0 else []))
        for tt in range(4):
            p.add("vector", lambda e, tt=tt, par=par: e.reduce_sum(
                sm["wsel"][:, par * 4 + tt:par * 4 + tt + 1],
                sm["mk4"][:, tt * 8:(tt + 1) * 8], axis=AX_X))
        wsel_done[ci] = vgap()
        t_xw[ci] = t_xg[ci]

    def do_ln2(lc):
        nonlocal t_out_count
        for tp_ in range(2):
            tin = {}
            tz2 = {}
            tmv = {}
            trc2 = {}
            for u in range(2):
                tt = tp_ * 2 + u
                tx1t = p.add("scalar", lambda e, lc=lc, tt=tt, u=u: e.dma_start(
                    out=x1T_sl(u), in_=agq_in[lc].ap()[0:D, tt * 128:(tt + 1) * 128],
                    transpose=True),
                    waits=[t_agq[lc], last_ln_dma[u], last_ln_read[u]], inc="ss", n=16)
                tmo = p.add("scalar", lambda e, lc=lc, tt=tt, u=u: e.dma_start(
                    out=moe_sl(u), in_=rs_out[lc].ap()[tt * 128:(tt + 1) * 128, :]),
                    waits=[t_rs_q[lc], last_ln_dma[u], last_ln_read[u]], inc="ss", n=16)
                tin[u] = (tx1t, tmo)
            for u in range(2):
                tz2[u] = p.add("vector", lambda e, u=u: e.tensor_add(
                    z2_sl(u), x1T_sl(u), moe_sl(u)),
                    waits=list(tin[u]), inc="sv")
                last_ln_read[u] = tz2[u]
            for u in range(2):
                p.add("vector", lambda e, u=u: e.bn_stats(
                    sm["st12"][:, u * 12:u * 12 + 6], z2_sl(u)[:, 0:512]),
                    waits=[tz2[u]])
                p.add("vector", lambda e, u=u: e.bn_stats(
                    sm["st12"][:, u * 12 + 6:u * 12 + 12], z2_sl(u)[:, 512:1024]))
            for u in range(2):
                tmv[u] = p.add("vector", lambda e, u=u: e.bn_aggr(
                    sm["mv2"][:, u * 2:u * 2 + 2], sm["st12"][:, u * 12:u * 12 + 12]),
                    inc="sv")
            spacer()
            spacer()
            tmvg = p.add("vector", lambda e: e.tensor_copy(sm["dmy"][:, :], sm["tmpc"][:, :]),
                         inc="sv")
            tmv = {0: tmvg, 1: tmvg}
            for u in range(2):
                tsq2 = p.add("scalar", lambda e, u=u: e.activation(
                    sm["rstd2"][:, u:u + 1], sm["mv2"][:, u * 2 + 1:u * 2 + 2],
                    AF.Sqrt, bias=sm["epsc"][:, 0:1]), waits=[tmv[u]], inc="sc")
                trc2[u] = tsq2
            for u in range(2):
                trc2[u] = p.add("vector", lambda e, u=u: e.reciprocal(
                    sm["rstd2"][:, u:u + 1], sm["rstd2"][:, u:u + 1]),
                    waits=[trc2[u]], inc="sv")
            spacer()
            spacer()
            touts = {}
            for u in range(2):
                p.add("vector", lambda e, u=u: e.tensor_scalar(
                    z2_sl(u), z2_sl(u), sm["mv2"][:, u * 2:u * 2 + 1],
                    sm["rstd2"][:, u:u + 1], OP.subtract, OP.mult),
                    waits=[trc2[u]])
            tog = vgap()
            for u in range(2):
                touts[u] = tog
            for u in range(2):
                tt = tp_ * 2 + u
                last_ln_dma[u] = p.add("scalar", lambda e, lc=lc, tt=tt, u=u: e.dma_start(
                    out=out_d[lc * TC + tt * 128:(lc * TC + (tt + 1) * 128), :],
                    in_=z2_sl(u)), waits=[touts[u]], inc="so", n=16)
                t_out_count += 16

    issue_loads(0)
    issue_loads(1)
    issue_wb_xw(0)
    for ci in range(NCI):
        lc, r = ci // CORES, ci % CORES
        xs = xbuf(ci)
        for fi in range(FI):
            rr_ = hro % 4
            hro += 1
            tok = None
            for ki in range(8):
                tok = p.add("tensor", lambda e, ps=h1_rot[rr_], l=aw1_w1(ki, fi),
                            rhs=xs(ki), s=(ki == 0), t=(ki == 7):
                            e.matmul(ps[:, :], l, rhs, start=s, stop=t),
                            waits=([t_xw[ci], tw1, h1_evac[rr_]] if ki == 0 else []),
                            inc=("st" if ki == 7 else None))
            h1_mm[rr_] = tok
            if fi % 2 == 0:
                t_ev = p.add("vector", lambda e, ps=h1_rot[rr_], o=h1s_sl(fi, 0, TB):
                             e.tensor_scalar(o, ps[:, :], 0.0, None, OP.max),
                             waits=[tok], inc="sv")
            else:
                t_ev = p.add("scalar", lambda e, ps=h1_rot[rr_], o=h1s_sl(fi, 0, TB):
                             e.activation(o, ps[:, :], AF.Relu),
                             waits=[tok], inc="sc")
            h1_evac[rr_] = t_ev
        h1_mm_last[ci] = h1_mm[(hro - 1) % 4]
        h1s_done_v = vgap()
        h1s_done_c = p.add("scalar", lambda e: e.activation(
            sm["sdmy"][:, :], sm["b_in"][:, :], AF.Copy), inc="sc")
        if ci + 2 < NCI:
            issue_loads(ci + 2)
        if ci + 1 < NCI:
            issue_wb_xw(ci + 1)
        for tt in range(4):
            pb = pyo % 2
            pyo += 1
            tok = None
            for half in range(2):
                for fk in range(FI):
                    is_first = (half == 0 and fk == 0)
                    is_last = (half == 1 and fk == FI - 1)
                    tok = p.add("tensor", lambda e, ps=pyb[pb], h=half,
                                l=h1s_sl(fk, tt * 128, 128),
                                rhs=w2_rhs(fk, half), s=(fk == 0), t=(fk == FI - 1):
                                e.matmul(ps[:, h * 512:(h + 1) * 512], l, rhs,
                                         start=s, stop=t),
                                waits=(([h1s_done_v, h1s_done_c, tw2, py_evac[pb]] +
                                        ([py_mm_init[pb]] if py_first[pb] else []))
                                       if is_first else []),
                                inc=("st" if is_last else None))
                    if is_first:
                        py_first[pb] = False
            spar = tt % 2
            par = ci % 2
            p.add("vector", lambda e, ps=pyb[pb], o=rsst_sl(spar), tt=tt, par=par:
                  e.tensor_scalar(o, ps[:, :], sm["wsel"][:, par * 4 + tt:par * 4 + tt + 1],
                                  None, OP.mult),
                  waits=[tok, last_rs_dma[spar], wsel_done[ci]])
            tev = vgap()
            py_evac[pb] = tev
            t_wsel_read[ci] = tev
            last_rs_dma[spar] = p.add(
                "gpsimd", lambda e, q=lc, r=r, tt=tt, spar=spar:
                e.dma_start(out=rs_in[q].ap()[r * TC + tt * 128:
                                              r * TC + (tt + 1) * 128, :],
                            in_=rsst_sl(spar)),
                waits=[tev], inc="sg", n=16)
        if r == CORES - 1:
            t_rs_q[lc] = p.add("gpsimd", lambda e, q=lc: e.collective_compute(
                "ReduceScatter", OP.add, replica_groups=[list(range(CORES))],
                ins=[rs_in[q].ap().opt()], outs=[rs_out[q].ap().opt()]),
                waits=[last_rs_dma[0], last_rs_dma[1]], inc="scc")
        if r == 1 and lc > 0:
            do_ln2(lc - 1)
    do_ln2(NQ - 1)

    p.add("vector", lambda e: e.tensor_copy(sm["dmy"][:, :], sm["tmpc"][:, :]),
          waits=[("so", t_out_count)])

    with nc.Block() as block:
        p.emit(block)
    es.close()
    return nc


_NC_CACHE = None


def kernel(L0_attn_x, L0_attn_W_in, L0_attn_b_in, L0_attn_W_gx, L0_attn_b_gx,
           L0_attn_W_ga, L0_attn_b_ga, L0_attn_a_param, L0_attn_W_out,
           L0_attn_b_out, L0_ffn_gate_w, L0_ffn_w1, L0_ffn_w2):
    global _NC_CACHE
    f = np.float32
    x = np.asarray(L0_attn_x, f)
    xT = np.ascontiguousarray(x[0].T)

    def colmajor8(v):
        return np.ascontiguousarray(np.asarray(v, f).reshape(8, 128).T)

    W_in = np.ascontiguousarray(np.asarray(L0_attn_W_in, f))
    W_gx = np.ascontiguousarray(np.asarray(L0_attn_W_gx, f))
    W_ga = np.ascontiguousarray(np.asarray(L0_attn_W_ga, f))
    W_out = np.ascontiguousarray(np.asarray(L0_attn_W_out, f))
    gw = np.asarray(L0_ffn_gate_w, f)
    w1 = np.asarray(L0_ffn_w1, f)
    w2 = np.asarray(L0_ffn_w2, f)
    bins = colmajor8(L0_attn_b_in)
    bgxs = colmajor8(L0_attn_b_gx)
    bgas = colmajor8(L0_attn_b_ga)
    bouts = colmajor8(L0_attn_b_out)
    apars = colmajor8(L0_attn_a_param)
    idn = np.eye(128, dtype=f)
    gwp = np.ascontiguousarray(gw.T.reshape(8, 128, 8).transpose(1, 0, 2).reshape(128, 64))

    if _NC_CACHE is None:
        _NC_CACHE = build_nc()
    nc = _NC_CACHE

    in_maps = []
    for i in range(CORES):
        selc = np.zeros((128, 8), f)
        if i > 0:
            selc[:, i - 1] = 1.0
        ohm = np.tile((np.arange(8) == i).astype(f), (128, 1))
        smalls = np.concatenate([bins, bgxs, bgas, bouts, apars, selc, gwp, ohm,
                                 np.zeros((128, 8), f)], axis=1).astype(f)
        in_maps.append({
            "xT": np.ascontiguousarray(xT[:, i * TOK:(i + 1) * TOK]),
            "w_in": W_in, "w_gx": W_gx, "w_ga": W_ga, "w_out": W_out,
            "smalls_h": np.ascontiguousarray(smalls),
            "w1T": np.ascontiguousarray(w1[i].T),
            "w2T": np.ascontiguousarray(w2[i].T),
            "idn_h": idn,
        })

    res = run_bass_kernel_spmd(nc, in_maps, core_ids=list(range(CORES)))
    out = np.concatenate([res.results[i]["out"] for i in range(CORES)], axis=0)
    return out.reshape(1, S, D).astype(np.float32)


# version 10
# speedup vs baseline: 1.0742x; 1.0057x over previous
import numpy as np
from contextlib import ExitStack

import concourse.bass as bass
import concourse.mybir as mybir
from concourse.bass_utils import run_bass_kernel_spmd

F32 = mybir.dt.float32
BF16 = mybir.dt.bfloat16
AF = mybir.ActivationFunctionType
OP = mybir.AluOpType
AX_X = mybir.AxisListType.X

CORES = 8
D = 1024
S = 16384
TOK = S // CORES
TC = 512
NCHUNK = TOK // TC
FF = 4096
FI = FF // 128
TB = 512
NQ = 4
EPS = 1e-5


class P:

    ENGS = ("sync", "gpsimd", "vector", "scalar", "tensor")

    def __init__(self, nc, es):
        self.nc = nc
        self.ops = {e: [] for e in self.ENGS}
        self.sems = {}
        self.cnt = {}
        for name in ("sg", "ss", "sv", "sc", "st", "scc", "so"):
            self.sems[name] = es.enter_context(nc.semaphore(name))
            self.cnt[name] = 0

    def add(self, eng, fn, waits=(), inc=None, n=1):
        ws = tuple(w for w in waits if w is not None and w[1] > 0)
        self.ops[eng].append((fn, ws, inc, n))
        if inc is not None:
            self.cnt[inc] += n
            return (inc, self.cnt[inc])
        return None

    def tok(self, name):
        return (name, self.cnt[name])

    def _run(self, name, eng):
        seen = {}
        for fn, waits, inc, n in self.ops[name]:
            for (s, v) in waits:
                if seen.get(s, 0) < v:
                    eng.wait_ge(self.sems[s], v)
                    seen[s] = v
            instr = fn(eng)
            if inc is not None:
                instr.then_inc(self.sems[inc], n)

    def emit(self, block):
        runner = self._run

        @block.sync
        def _(e):
            runner("sync", e)

        @block.gpsimd
        def _(e):
            runner("gpsimd", e)

        @block.vector
        def _(e):
            runner("vector", e)

        @block.scalar
        def _(e):
            runner("scalar", e)

        @block.tensor
        def _(e):
            runner("tensor", e)


def build_nc():
    nc = bass.Bass("TRN2", target_bir_lowering=False, detect_race_conditions=False)

    xT = nc.dram_tensor("xT", [D, TOK], F32, kind="ExternalInput").ap()
    w_in = nc.dram_tensor("w_in", [D, D], F32, kind="ExternalInput").ap()
    w_gx = nc.dram_tensor("w_gx", [D, D], F32, kind="ExternalInput").ap()
    w_ga = nc.dram_tensor("w_ga", [D, D], F32, kind="ExternalInput").ap()
    w_out = nc.dram_tensor("w_out", [D, D], F32, kind="ExternalInput").ap()
    smalls_h = nc.dram_tensor("smalls_h", [128, 128], F32, kind="ExternalInput").ap()
    w1T = nc.dram_tensor("w1T", [D, FF], F32, kind="ExternalInput").ap()
    w2T = nc.dram_tensor("w2T", [FF, D], F32, kind="ExternalInput").ap()
    idn_h = nc.dram_tensor("idn_h", [128, 128], F32, kind="ExternalInput").ap()
    out_d = nc.dram_tensor("out", [TOK, D], F32, kind="ExternalOutput").ap()

    ag1_in = nc.dram_tensor("ag1_in", [2, D], F32)
    ag1_out = nc.dram_tensor("ag1_out", [2 * CORES, D], F32)
    agq_in = [nc.dram_tensor(f"agq_in_{q}", [D + 8, TC], BF16) for q in range(NQ)]
    agq_out = [nc.dram_tensor(f"agq_out_{q}", [(D + 8) * CORES, TC], BF16) for q in range(NQ)]
    rs_in = [nc.dram_tensor(f"rs_in_{q}", [CORES * TC, D], BF16) for q in range(NQ)]
    rs_out = [nc.dram_tensor(f"rs_out_{q}", [TC, D], BF16) for q in range(NQ)]

    es = ExitStack()

    AW1 = es.enter_context(nc.sbuf_tensor("AW1", [128, 32768], BF16))
    AW2 = es.enter_context(nc.sbuf_tensor("AW2", [128, 32768], BF16))
    AX = es.enter_context(nc.sbuf_tensor("AX", [128, 8192], BF16))
    AB = es.enter_context(nc.sbuf_tensor("AB", [128, 14336], BF16))
    AFr = es.enter_context(nc.sbuf_tensor("AF", [128, 8192], F32))
    AFb = AFr.ap().bitcast(BF16)
    ABf = AB.ap().bitcast(F32)

    SML = es.enter_context(nc.sbuf_tensor("SML", [128, 128], F32))
    sm = {}
    sm["b_in"] = SML[:, 0:8]
    sm["bgx_hf"] = SML[:, 8:16]
    sm["bga_hf"] = SML[:, 16:24]
    sm["b_out"] = SML[:, 24:32]
    sm["aptmp"] = SML[:, 32:40]
    sm["selc"] = SML[:, 40:48]
    sm["gwtf"] = SML[:, 48:112]
    sm["ohm"] = SML[:, 112:120]
    for nm, shp, dt in [
        ("msp_h", [128, 8], F32),
        ("hcar", [128, 8], F32), ("apcar", [128, 8], F32),
        ("cin", [128, 8], F32), ("tmpc", [128, 8], F32), ("dmy", [128, 8], F32),
        ("aesum", [128, 128], F32), ("cpre", [128, 64], F32),
        ("idn", [128, 128], F32),
        ("onesc", [128, 1], BF16), ("onesf", [1, 128], F32),
        ("glt4", [128, 32], F32), ("mk4", [128, 32], F32),
        ("ms4", [128, 32], F32), ("comb4", [128, 32], F32),
        ("st12", [128, 24], F32), ("mv2", [128, 4], F32), ("rstd2", [128, 2], F32),
        ("cmtok0", [128, 32], BF16), ("cmquar", [128, 256], BF16),
        ("wsel", [128, 8], F32), ("sdmy", [128, 8], F32),
        ("epsc", [128, 1], F32), ("onec", [128, 1], F32), ("spt", [128, 8], F32),
    ]:
        sm[nm] = es.enter_context(nc.sbuf_tensor(nm, shp, dt))
    sm["asum"] = sm["aesum"][:, 0:64]
    sm["esum"] = sm["aesum"][:, 64:128]
    sm["c24"] = sm["mk4"]
    sm["m14"] = sm["st12"][:, 0:4]
    sm["m24"] = sm["st12"][:, 4:8]
    sm["dd4"] = sm["st12"][:, 8:12]
    sm["th4"] = sm["st12"][:, 12:16]
    sm["g14"] = sm["st12"][:, 16:20]
    sm["g24"] = sm["st12"][:, 20:24]

    PA0 = es.enter_context(nc.psum_tensor("PA0", [128, 512], F32))
    PA1 = es.enter_context(nc.psum_tensor("PA1", [128, 512], F32))
    PA2 = es.enter_context(nc.psum_tensor("PA2", [128, 512], F32))
    PSP = es.enter_context(nc.psum_tensor("PSP", [128, 512], F32))
    PY0 = es.enter_context(nc.psum_tensor("PY0", [128, 1024], F32))
    PY1 = es.enter_context(nc.psum_tensor("PY1", [128, 1024], F32))

    p = P(nc, es)
    T0 = ("sv", 0)

    def aw1_w(base, ki, ii):
        c = base + ki * 1024 + ii * 128
        return AW1[:, c:c + 128]

    def aw1_w1(ki, fi):
        c = ki * 4096 + fi * 128
        return AW1[:, c:c + 128]

    def hloc_sl(ii, c):
        a = ii * TOK + c * TC
        return AW2[:, a:a + TC]

    def apb_sl(ii, c):
        a = 16384 + ii * TOK + c * TC
        return AW2[:, a:a + TC]

    def w2_rhs(fk, half):
        a = fk * 1024 + half * 512
        return AW2[:, a:a + 512]

    def x0b_sl(ki):
        return AX[:, ki * TC:(ki + 1) * TC]

    def xinb_sl(ii):
        return AX[:, 4096 + ii * TC:4096 + (ii + 1) * TC]

    def xinf_sl(ii):
        return AFr[:, ii * TC:(ii + 1) * TC]

    def afull_sl(ii):
        return AFr[:, 4096 + ii * TC:4096 + (ii + 1) * TC]

    def slot(k):
        return ABf[:, 6144 + k * 512:6144 + (k + 1) * 512]

    def ufull_sl(ii):
        return AB[:, ii * TC:(ii + 1) * TC]

    def xfull_sl(ii):
        return AB[:, 8192 + ii * 512:8192 + (ii + 1) * 512]

    def hfix_sl(ii):
        return AB[:, ii * TC:(ii + 1) * TC]

    def x1c_sl(ii):
        return AB[:, 4096 + ii * TC:4096 + (ii + 1) * TC]

    def zb_slot(k):
        return AB[:, 8192 + k * 512:8192 + (k + 1) * 512]

    def zsq_slot(k):
        return AB[:, 9216 + k * 512:9216 + (k + 1) * 512]

    def z_sl(ii):
        return AFr[:, ii * TC:(ii + 1) * TC]

    def rows(k):
        return AFr[0:1, 4096 + k * 512:4096 + (k + 1) * 512]

    def x1tmp():
        return AFr[:, 6144:6656]

    def glog_sl():
        return ABf[0:8, 5120:5632]

    def combT_sl(c):
        return AB[0:8, 12288 + c * TC:12288 + (c + 1) * TC]

    def xg_sl(ki):
        return AX[:, ki * TB:(ki + 1) * TB]

    def xw_sl(ki):
        return AX[:, 4096 + ki * TB:4096 + (ki + 1) * TB]

    def h1s_sl(fk, off, ln):
        a = fk * TB + off
        return AFb[:, a:a + ln]

    def x1T_sl(par):
        return AB[:, par * 1024:(par + 1) * 1024]

    def moe_sl(par):
        return AB[:, 2048 + par * 1024:2048 + (par + 1) * 1024]

    def rsst_sl(par):
        return AB[:, 4096 + par * 1024:4096 + (par + 1) * 1024]

    def wb_sl(par):
        return AB[:, 6144 + par * 512:6144 + (par + 1) * 512]

    def cm8_sl(par):
        return AB[0:8, 7168 + par * 512:7168 + (par + 1) * 512]

    def cm8_full(par):
        return AB[:, 7168 + par * 512:7168 + (par + 1) * 512]

    def z2_sl(par):
        return ABf[:, 4096 + par * 1024:4096 + (par + 1) * 1024]

    def spacer():
        p.add("vector", lambda e: e.tensor_copy(sm["dmy"][:, :], sm["tmpc"][:, :]))

    def vgap():
        spacer()
        return p.add("vector", lambda e: e.tensor_copy(sm["dmy"][:, :], sm["tmpc"][:, :]),
                     inc="sv")

    src3_0 = xT.rearrange("(ki p) t -> p ki t", p=128)[:, :, 0:TC]
    dst3_0 = AX[:, 0:4096].rearrange("p (ki t) -> p ki t", ki=8)
    t_x0_first = p.add("gpsimd", lambda e: e.dma_start(out=dst3_0, in_=src3_0),
                       inc="sg", n=16)
    t_w = {}
    for nm, src, base in [("in", w_in, 0), ("gx", w_gx, 8192),
                          ("ga", w_ga, 16384), ("out", w_out, 24576)]:
        src3 = src.rearrange("(ki p) c -> p ki c", p=128)
        dst3 = AW1[:, base:base + 8192].rearrange("p (ki c) -> p ki c", ki=8)
        t_w[nm] = p.add("gpsimd",
                        lambda e, o=dst3, i=src3: e.dma_start(out=o, in_=i),
                        inc="sg", n=16)


    t_small = p.add("sync", lambda e: e.dma_start(out=SML[:, :], in_=smalls_h),
                    inc="ss", n=16)
    t_idn = p.add("sync", lambda e: e.dma_start(out=sm["idn"][:, :], in_=idn_h),
                  waits=[t_small], inc="ss", n=16)
    t_gwtf = t_small
    small_loads = [t_small] * 6
    t_sml = t_small

    p.add("vector", lambda e: e.memset(sm["hcar"][:, :], 0.0))
    p.add("vector", lambda e: e.memset(sm["apcar"][:, :], 1.0))
    p.add("vector", lambda e: e.memset(sm["cin"][:, :], 0.0))
    p.add("vector", lambda e: e.memset(sm["tmpc"][:, :], 0.0))
    p.add("vector", lambda e: e.memset(sm["onesc"][:, :], 1.0))
    p.add("vector", lambda e: e.memset(sm["epsc"][:, :], EPS))
    p.add("vector", lambda e: e.memset(sm["onec"][:, :], 1.0))
    t_ones = p.add("vector", lambda e: e.memset(sm["onesf"][:, :], 1.0), inc="sv")
    def sgap(n=8):
        for _ in range(n):
            p.add("scalar", lambda e: e.activation(sm["sdmy"][:, :], sm["b_in"][:, :],
                                                   AF.Copy))
    ta1 = p.add("scalar", lambda e: e.activation(
        sm["spt"][:, :], sm["aptmp"][:, :], AF.Abs),
        waits=[small_loads[4]], inc="sc")
    t_r = p.add("scalar", lambda e: e.activation(
        sm["msp_h"][:, :], sm["aptmp"][:, :], AF.Relu), inc="sc")
    sgap()
    t_e = p.add("scalar", lambda e: e.activation(
        sm["aptmp"][:, :], sm["spt"][:, :], AF.Exp, scale=-1.0), inc="sc")
    sgap()
    t_l = p.add("scalar", lambda e: e.activation(
        sm["spt"][:, :], sm["aptmp"][:, :], AF.Ln, bias=sm["onec"][:, 0:1]),
        waits=[t_ones], inc="sc")
    sgap()
    tad = p.add("vector", lambda e: e.tensor_add(
        sm["msp_h"][:, :], sm["msp_h"][:, :], sm["spt"][:, :]),
        waits=[t_l, t_r], inc="sv")
    spacer()
    spacer()
    t_msp = p.add("vector", lambda e: e.tensor_scalar_mul(
        sm["msp_h"][:, :], sm["msp_h"][:, :], -4.0), waits=[tad], inc="sv")
    t_bgxh = p.add("vector", lambda e: e.tensor_scalar_mul(
        sm["bgx_hf"][:, :], sm["bgx_hf"][:, :], 0.5), waits=[small_loads[1]], inc="sv")
    t_bgah = p.add("vector", lambda e: e.tensor_scalar_mul(
        sm["bga_hf"][:, :], sm["bga_hf"][:, :], 0.5), waits=[small_loads[2]], inc="sv")

    PY0a, PY0b = PY0[:, 0:512], PY0[:, 512:1024]
    PY1a, PY1b = PY1[:, 0:512], PY1[:, 512:1024]
    rotors = [PA0[:, :], PA1[:, :], PA2[:, :], PSP[:, :], PY0a, PY0b, PY1a, PY1b]
    rot_evac = [T0] * 8
    rot_i = 0

    def next_rot():
        nonlocal rot_i
        r = rot_i % 8
        rot_i += 1
        return r

    t_x0 = [None] * NCHUNK
    t_xin_mm_last = T0
    slot_free = [T0, T0]
    t_u_done = {}
    t_x_done = {}
    t_sct = {}

    for c in range(NCHUNK):
        if c == 0:
            t_x0[0] = t_x0_first
        else:
            src3 = xT.rearrange("(ki p) t -> p ki t", p=128)[:, :, c * TC:(c + 1) * TC]
            dst3 = AX[:, 0:4096].rearrange("p (ki t) -> p ki t", ki=8)
            t_x0[c] = p.add("gpsimd", lambda e, o=dst3, i=src3: e.dma_start(out=o, in_=i),
                            waits=[t_xin_mm_last], inc="sg", n=16)

        t_evac_xin = []
        for ii in range(8):
            r = next_rot()
            tok = None
            for ki in range(8):
                tok = p.add("tensor", lambda e, ps=rotors[r], l=aw1_w(0, ki, ii),
                            rr=x0b_sl(ki), s=(ki == 0), t=(ki == 7):
                            e.matmul(ps, l, rr, start=s, stop=t),
                            waits=([t_x0[c], t_w["in"], rot_evac[r]] if ki == 0 else []),
                            inc=("st" if ki == 7 else None))
            p.add("vector", lambda e, ps=rotors[r], o=xinf_sl(ii), b=sm["b_in"][:, ii:ii + 1]:
                  e.tensor_scalar_add(o, ps, b), waits=[tok, t_sml])
            e2 = p.add("vector", lambda e, ps=rotors[r], o=xinb_sl(ii), b=sm["b_in"][:, ii:ii + 1]:
                       e.tensor_scalar_add(o, ps, b), inc="sv")
            rot_evac[r] = e2
            t_evac_xin.append(e2)
        t_evac_xin_g = vgap()
        t_xin_mm_last = p.tok("st")

        for ii in range(8):
            sA = slot(0)
            sB = slot(1)
            rA = next_rot()
            tokA = None
            for ki in range(8):
                tokA = p.add("tensor", lambda e, ps=rotors[rA], l=aw1_w(16384, ki, ii),
                             rr=xinb_sl(ki), s=(ki == 0), t=(ki == 7):
                             e.matmul(ps, l, rr, start=s, stop=t),
                             waits=([t_evac_xin_g, t_w["ga"], rot_evac[rA]] if ki == 0 else []),
                             inc=("st" if ki == 7 else None))
            rB = next_rot()
            tokB = None
            for ki in range(8):
                tokB = p.add("tensor", lambda e, ps=rotors[rB], l=aw1_w(8192, ki, ii),
                             rr=xinb_sl(ki), s=(ki == 0), t=(ki == 7):
                             e.matmul(ps, l, rr, start=s, stop=t),
                             waits=([t_w["gx"], rot_evac[rB]] if ki == 0 else []),
                             inc=("st" if ki == 7 else None))
            ta = p.add("scalar", lambda e, ps=rotors[rA], o=sB, b=sm["bga_hf"][:, ii:ii + 1]:
                       e.activation(o, ps, AF.Tanh, bias=b, scale=0.5),
                       waits=[tokA, t_bgah, slot_free[1]], inc="sc")
            rot_evac[rA] = ta
            tb = p.add("scalar", lambda e, ps=rotors[rB], o=sA, b=sm["bgx_hf"][:, ii:ii + 1]:
                       e.activation(o, ps, AF.Tanh, bias=b, scale=0.5),
                       waits=[tokB, t_bgxh, slot_free[0]], inc="sc")
            rot_evac[rB] = tb
            tl = p.add("vector", lambda e, s=sB, m=sm["msp_h"][:, ii:ii + 1]:
                       e.tensor_scalar(s, s, m, m, OP.mult, OP.add),
                       waits=[ta, t_msp], inc="sv")
            tu = p.add("vector", lambda e, s=sA, o=ufull_sl(ii), xi=xinf_sl(ii):
                       e.scalar_tensor_tensor(o, s, 1.0, xi, OP.add, OP.mult),
                       waits=[tb], inc="sv")
            t_u_done[(c, ii)] = tu
            p.add("scalar", lambda e, s=sB, o=afull_sl(ii):
                  e.activation(o, s, AF.Exp), waits=[tu], inc="sc")
            tq = p.add("scalar", lambda e, s=sB, o=sA:
                       e.activation(o, s, AF.Exp, scale=2.0), waits=[tu], inc="sc")
            slot_free[1] = tq
            p.add("vector", lambda e, s=sA: e.tensor_scalar(
                s, s, -1.0, 1.0, OP.mult, OP.add), waits=[tq])
            p.add("vector", lambda e, s=sA: e.tensor_scalar_max(s, s, 0.0))
            tx3 = p.add("vector", lambda e, s=sA, o=xfull_sl(ii): e.tensor_copy(o, s),
                        inc="sv")
            slot_free[0] = tx3
            t_x_done[(c, ii)] = tx3

        for ii in range(8):
            t_sct[(c, ii)] = p.add("scalar", lambda e, s=xfull_sl(ii):
                                   e.activation(s, s, AF.Sqrt, scale=0.25),
                                   waits=[t_x_done[(c, ii)]], inc="sc")

        for i0 in range(0, 8, 2):
            prs = [i0, i0 + 1]
            bsl = {ii: AFr[:, (ii % 2) * 512:(ii % 2) * 512 + 512] for ii in prs}
            hsl = {ii: AFr[:, 1024 + (ii % 2) * 512:1024 + (ii % 2) * 512 + 512] for ii in prs}
            asl = {ii: AFr[:, 2048 + (ii % 2) * 512:2048 + (ii % 2) * 512 + 512] for ii in prs}
            for ii in prs:
                p.add("vector", lambda e, o=bsl[ii], u=ufull_sl(ii), s=xfull_sl(ii):
                      e.tensor_mul(o, u, s),
                      waits=[t_sct[(c, ii)], t_u_done[(c, ii)]])
            for ii in prs:
                p.add("vector", lambda e, o=hsl[ii], a=afull_sl(ii), b=bsl[ii],
                      i0_=sm["hcar"][:, ii:ii + 1]:
                      e.tensor_tensor_scan(o, a, b, i0_, OP.mult, OP.add))
            for ii in prs:
                p.add("vector", lambda e, o=asl[ii], a=afull_sl(ii),
                      i0_=sm["apcar"][:, ii:ii + 1]:
                      e.tensor_tensor_scan(o, a, a, i0_, OP.mult, OP.bypass))
            for ii in prs:
                p.add("vector", lambda e, o=hloc_sl(ii, c), i=hsl[ii]: e.tensor_copy(o, i))
            for ii in prs:
                p.add("vector", lambda e, o=apb_sl(ii, c), i=asl[ii]: e.tensor_copy(o, i))
            for ii in prs:
                p.add("vector", lambda e, o=sm["hcar"][:, ii:ii + 1], i=hsl[ii][:, 511:512]:
                      e.tensor_copy(o, i))
            for ii in prs:
                p.add("vector", lambda e, o=sm["apcar"][:, ii:ii + 1], i=asl[ii][:, 511:512]:
                      e.tensor_copy(o, i), inc=("sv" if ii == i0 + 1 else None))

    t_gates_done = p.tok("st")
    t_lastscan = vgap()
    ag1i3 = ag1_in.ap().rearrange("r (p ii) -> r p ii", ii=8)
    tg1 = p.add("gpsimd", lambda e: e.dma_start(out=ag1i3[0], in_=sm["apcar"][:, :]),
                waits=[t_lastscan], inc="sg", n=16)
    tg2 = p.add("gpsimd", lambda e: e.dma_start(out=ag1i3[1], in_=sm["hcar"][:, :]),
                inc="sg", n=16)
    t_ag1 = p.add("gpsimd", lambda e: e.collective_compute(
        "AllGather", OP.bypass, replica_groups=[list(range(CORES))],
        ins=[ag1_in.ap().opt()], outs=[ag1_out.ap().opt()]),
        waits=[tg1, tg2], inc="scc")
    ag1o3 = ag1_out.ap().rearrange("(r two) (p ii) -> two p r ii", two=2, ii=8)
    tga = p.add("gpsimd", lambda e: e.dma_start(
        out=sm["asum"].rearrange("p (r ii) -> p r ii", ii=8), in_=ag1o3[0]),
        waits=[t_ag1], inc="sg", n=16)
    tge = p.add("gpsimd", lambda e: e.dma_start(
        out=sm["esum"].rearrange("p (r ii) -> p r ii", ii=8), in_=ag1o3[1]),
        waits=[tga], inc="sg", n=16)
    p.add("vector", lambda e: e.tensor_copy(sm["cpre"][:, 0:8], sm["esum"][:, 0:8]),
          waits=[tge])
    for k in range(1, 7):
        spacer()
        p.add("vector", lambda e, k=k: e.tensor_mul(
            sm["tmpc"][:, :], sm["asum"][:, k * 8:(k + 1) * 8],
            sm["cpre"][:, (k - 1) * 8:k * 8]))
        spacer()
        p.add("vector", lambda e, k=k: e.tensor_add(
            sm["cpre"][:, k * 8:(k + 1) * 8], sm["tmpc"][:, :],
            sm["esum"][:, k * 8:(k + 1) * 8]))
    for k in range(7):
        spacer()
        spacer()
        p.add("vector", lambda e, k=k: e.scalar_tensor_tensor(
            sm["cin"][:, :], sm["cpre"][:, k * 8:(k + 1) * 8],
            sm["selc"][:, k:k + 1], sm["cin"][:, :], OP.mult, OP.add))
    spacer()
    spacer()
    t_cin = p.add("vector", lambda e: e.tensor_copy(sm["dmy"][:, :], sm["cin"][:, :]),
                  inc="sv")

    rot3 = [PY0a, PY0b, PY1a, PY1b]
    rot3_evac = [rot_evac[4], rot_evac[5], rot_evac[6], rot_evac[7]]
    psp_free = rot_evac[3]
    pa2_free = rot_evac[2]
    pa0_free = rot_evac[0]
    pa1_free = rot_evac[1]
    t_glog_mm = T0
    t_agq = [None] * NQ
    t_x1dma_prev = ("sg", 0)
    t_zev = T0
    t_wout_mm = T0
    t_hf = []
    tct = T0

    for c in range(NCHUNK):
        src3 = xT.rearrange("(ki p) t -> p ki t", p=128)[:, :, c * TC:(c + 1) * TC]
        dst3 = AX[:, 0:4096].rearrange("p (ki t) -> p ki t", ki=8)
        t_x0r = p.add("gpsimd", lambda e, o=dst3, i=src3: e.dma_start(out=o, in_=i),
                      waits=[t_lastscan] + ([t_zev] if c > 0 else []), inc="sg", n=16)
        t_hf = []
        for ii in range(8):
            t_hf.append(p.add("vector", lambda e, o=hfix_sl(ii), a=apb_sl(ii, c),
                              s=sm["cin"][:, ii:ii + 1], h=hloc_sl(ii, c):
                              e.scalar_tensor_tensor(o, a, s, h, OP.mult, OP.add),
                              waits=[t_cin] + ([t_wout_mm] if c > 0 else []), inc="sv"))
        t_hf_g = vgap()
        t_zs = []
        for ii in range(8):
            r = ii % 4
            tok = None
            for ki in range(8):
                tok = p.add("tensor", lambda e, ps=rot3[r], l=aw1_w(24576, ki, ii),
                            rr=hfix_sl(ki), s=(ki == 0), t=(ki == 7):
                            e.matmul(ps, l, rr, start=s, stop=t),
                            waits=([t_hf_g, t_w["out"], rot3_evac[r]] if ki == 0 else []),
                            inc=("st" if ki == 7 else None))
            tz = p.add("vector", lambda e, ps=rot3[r], o=z_sl(ii),
                       b=sm["b_out"][:, ii:ii + 1], x0=x0b_sl(ii):
                       e.scalar_tensor_tensor(o, ps, b, x0, OP.add, OP.add),
                       waits=[tok, t_x0r], inc="sv")
            rot3_evac[r] = tz
            t_zs.append(tz)
        t_wout_mm = p.tok("st")
        t_zev = t_zs[-1]
        t_slot_mm = [T0, T0]
        tok_sum = tok_sq = None
        for ii in range(8):
            k = ii % 2
            p.add("vector", lambda e, o=zb_slot(k), z=z_sl(ii): e.tensor_copy(o, z),
                  waits=[t_zs[ii], t_slot_mm[k]])
            p.add("vector", lambda e, o=zsq_slot(k), z=z_sl(ii): e.tensor_mul(o, z, z))
            tzb = tzq = vgap()
            tok_sum = p.add("tensor", lambda e, rr=zb_slot(k), s=(ii == 0), t=(ii == 7):
                            e.matmul(PA0[0:1, :], sm["onesc"][:, :], rr, start=s, stop=t),
                            waits=[tzb, t_ones] + ([pa0_free] if ii == 0 else []),
                            inc="st")
            tok_sq = p.add("tensor", lambda e, rr=zsq_slot(k), s=(ii == 0), t=(ii == 7):
                           e.matmul(PA1[0:1, :], sm["onesc"][:, :], rr, start=s, stop=t),
                           waits=[tzq] + ([pa1_free] if ii == 0 else []),
                           inc="st")
            t_slot_mm[k] = tok_sq
        trm = p.add("vector", lambda e: e.tensor_scalar_mul(rows(0), PA0[0:1, :], 1.0 / D),
                    waits=[tok_sum], inc="sv")
        p.add("vector", lambda e: e.tensor_scalar_mul(rows(1), PA1[0:1, :], 1.0 / D),
              waits=[tok_sq])
        spacer()
        p.add("vector", lambda e: e.tensor_mul(rows(2), rows(0), rows(0)))
        spacer()
        p.add("vector", lambda e: e.tensor_sub(rows(1), rows(1), rows(2)))
        spacer()
        spacer()
        tvar = p.add("vector", lambda e: e.tensor_copy(sm["dmy"][:, :], sm["tmpc"][:, :]),
                     inc="sv")
        tsq = p.add("scalar", lambda e: e.activation(rows(2), rows(1), AF.Sqrt, bias=sm["epsc"][0:1, 0:1]),
                    waits=[tvar], inc="sc")
        trc = p.add("vector", lambda e: e.reciprocal(rows(3), rows(2)), waits=[tsq],
                    inc="sv")
        spacer()
        spacer()
        p.add("vector", lambda e: e.tensor_mul(rows(2), rows(0), rows(3)))
        tmr = vgap()
        tbr = p.add("tensor", lambda e: e.matmul(PA1[:, :], sm["onesf"][:, :], rows(3),
                                                 start=True, stop=True),
                    waits=[tmr, tvar, t_ones], inc="st")
        tbm = p.add("tensor", lambda e: e.matmul(PA0[:, :], sm["onesf"][:, :], rows(2),
                                                 start=True, stop=True),
                    waits=[tmr, trm], inc="st")
        tx1last = None
        for ii in range(8):
            p.add("vector", lambda e, z=z_sl(ii): e.tensor_mul(x1tmp(), z, PA1[:, :]),
                  waits=([tbr, tbm, t_glog_mm, t_x1dma_prev] if ii == 0 else []))
            p.add("vector", lambda e, z=z_sl(ii): e.tensor_sub(z, x1tmp(), PA0[:, :]))
            p.add("vector", lambda e, o=x1c_sl(ii), z=z_sl(ii): e.tensor_copy(o, z))
            tx1last = None
        tx1last = vgap()
        pa0_free = tx1last
        pa1_free = tx1last
        t_x1dma_prev = p.add("gpsimd", lambda e, c=c: e.dma_start(
            out=agq_in[c].ap()[0:D, :].rearrange("(ki p) t -> p ki t", p=128),
            in_=AB[:, 4096:8192].rearrange("p (ki t) -> p ki t", ki=8)),
            waits=[tx1last], inc="sg", n=16)
        tok = None
        for ki in range(8):
            tok = p.add("tensor", lambda e, l=sm["gwtf"][:, ki * 8:(ki + 1) * 8],
                        rr=z_sl(ki), s=(ki == 0), t=(ki == 7):
                        e.matmul(PSP[0:8, :], l, rr, start=s, stop=t),
                        waits=([tx1last, t_gwtf, psp_free] if ki == 0 else []),
                        inc=("st" if ki == 7 else None))
        t_glog_mm = tok
        tgl = p.add("vector", lambda e: e.tensor_copy(glog_sl(), PSP[0:8, :]),
                    waits=[tok], inc="sv")
        psp_free = tgl
        tprev = pa2_free
        for j in range(4):
            ttr = p.add("tensor", lambda e, j=j: e.transpose(
                PA2[:, 0:8], glog_sl()[0:8, j * 128:(j + 1) * 128],
                sm["idn"][0:8, 0:8]),
                waits=[tgl, t_idn, tprev], inc="st")
            tprev = p.add("vector", lambda e, j=j: e.tensor_copy(
                sm["glt4"][:, j * 8:(j + 1) * 8], PA2[:, 0:8]),
                waits=[ttr], inc="sv")
        pa2_free = tprev
        for j in range(4):
            p.add("vector", lambda e, j=j: e.reduce_max(
                sm["m14"][:, j:j + 1], sm["glt4"][:, j * 8:(j + 1) * 8], axis=AX_X))
        spacer()
        spacer()
        for j in range(4):
            p.add("vector", lambda e, j=j: e.tensor_scalar(
                sm["mk4"][:, j * 8:(j + 1) * 8], sm["glt4"][:, j * 8:(j + 1) * 8],
                sm["m14"][:, j:j + 1], 1.0, OP.is_equal, OP.mult))
        spacer()
        for j in range(4):
            p.add("vector", lambda e, j=j: e.scalar_tensor_tensor(
                sm["ms4"][:, j * 8:(j + 1) * 8], sm["mk4"][:, j * 8:(j + 1) * 8],
                -1e30, sm["glt4"][:, j * 8:(j + 1) * 8], OP.mult, OP.add))
        spacer()
        for j in range(4):
            p.add("vector", lambda e, j=j: e.reduce_max(
                sm["m24"][:, j:j + 1], sm["ms4"][:, j * 8:(j + 1) * 8], axis=AX_X))
        spacer()
        spacer()
        p.add("vector", lambda e: e.tensor_sub(sm["dd4"], sm["m24"],
                                               sm["m14"]))
        spacer()
        spacer()
        tdd = p.add("vector", lambda e: e.tensor_copy(sm["dmy"][:, :], sm["tmpc"][:, :]),
                    inc="sv")
        tth = p.add("scalar", lambda e: e.activation(sm["th4"], sm["dd4"],
                                                     AF.Tanh, scale=0.5),
                    waits=[tdd], inc="sc")
        p.add("vector", lambda e: e.tensor_scalar(sm["g24"], sm["th4"],
                                                  0.5, 0.5, OP.mult, OP.add),
              waits=[tth])
        p.add("vector", lambda e: e.tensor_scalar(sm["g14"], sm["th4"],
                                                  -0.5, 0.5, OP.mult, OP.add))
        spacer()
        spacer()
        for j in range(4):
            p.add("vector", lambda e, j=j: e.tensor_scalar(
                sm["comb4"][:, j * 8:(j + 1) * 8], sm["glt4"][:, j * 8:(j + 1) * 8],
                sm["m14"][:, j:j + 1], sm["g14"][:, j:j + 1], OP.is_equal, OP.mult))
        spacer()
        for j in range(4):
            p.add("vector", lambda e, j=j: e.tensor_scalar(
                sm["c24"][:, j * 8:(j + 1) * 8], sm["ms4"][:, j * 8:(j + 1) * 8],
                sm["m24"][:, j:j + 1], sm["g24"][:, j:j + 1], OP.is_equal, OP.mult))
        spacer()
        p.add("vector", lambda e: e.tensor_add(sm["comb4"][:, :], sm["comb4"][:, :],
                                               sm["c24"][:, :]))
        spacer()
        spacer()
        tcb = p.add("vector", lambda e: e.tensor_copy(sm["cmtok0"][:, :], sm["comb4"][:, :]),
                    inc="sv")
        tct = vgap()
        tcdma = p.add("gpsimd", lambda e, c=c: e.dma_start(
            out=agq_in[c].ap()[D:D + 8, :].rearrange(
                "r t -> (r t)").rearrange("(p c) -> p c", p=128),
            in_=sm["cmtok0"][:, :]),
            waits=[tct], inc="sg", n=16)
        t_agq[c] = p.add("gpsimd", lambda e, c=c: e.collective_compute(
            "AllGather", OP.bypass, replica_groups=[list(range(CORES))],
            ins=[agq_in[c].ap().opt()], outs=[agq_out[c].ap().opt()]),
            waits=[tcdma, t_x1dma_prev], inc="scc")

    t_tensorA_done = p.tok("st")

    tw1a = p.add("gpsimd", lambda e: e.dma_start(
        out=AW1[:, 0:24576].rearrange("p (ki f) -> p ki f", ki=6),
        in_=w1T[0:768, :].rearrange("(ki p) f -> p ki f", p=128)),
        waits=[t_gates_done], inc="sg", n=16)
    tw1 = p.add("gpsimd", lambda e: e.dma_start(
        out=AW1[:, 24576:32768].rearrange("p (ki f) -> p ki f", ki=2),
        in_=w1T[768:1024, :].rearrange("(ki p) f -> p ki f", p=128)),
        waits=[t_tensorA_done, tw1a], inc="sg", n=16)
    tw2 = p.add("gpsimd", lambda e: e.dma_start(
        out=AW2.ap().rearrange("p (fk d) -> p fk d", fk=32),
        in_=w2T.rearrange("(fk p) d -> p fk d", p=128)),
        waits=[t_hf[-1]], inc="sg", n=16)

    h1_rot = [PA0, PA1, PA2, PSP]
    h1_evac = [pa0_free, pa1_free, pa2_free, psp_free]
    h1_mm = [T0, T0, T0, T0]
    pyb = [PY0, PY1]
    py_evac = [rot3_evac[1], rot3_evac[3]]
    py_mm_init = [rot3_evac[0], rot3_evac[2]]
    py_first = [True, True]
    t_xg = {}
    t_c8 = {}
    t_xw = {}
    wsel_done = {}
    t_wsel_read = {}
    t_cmtok_free = tw1
    h1_mm_last = {-1: T0, -2: T0}
    last_rs_dma = [("sg", 0), ("sg", 0)]
    t_out_count = 0
    hro = 0
    pyo = 0
    last_ln_dma = [("so", 0), ("so", 0)]
    last_ln_read = [T0, T0]
    t_rs_q = {}
    NCI = NQ * CORES

    def xbuf(ci):
        par = ci % 2
        return lambda ki: AX[:, par * 4096 + ki * TB:par * 4096 + (ki + 1) * TB]

    def issue_loads(ci):
        lc, r = ci // CORES, ci % CORES
        par = ci % 2
        xs = xbuf(ci)
        t_xg[ci] = p.add("sync", lambda e, r=r, lc=lc, par=par: e.dma_start(
            out=AX[:, par * 4096:par * 4096 + 4096].rearrange("p (ki t) -> p ki t", ki=8),
            in_=agq_out[lc].ap()[r * (D + 8):r * (D + 8) + D, :].rearrange(
                "(ki p) t -> p ki t", p=128)),
            waits=[t_agq[lc], t_zev, h1_mm_last.get(ci - 2, T0)] +
                  ([t_tensorA_done] if ci < 2 else []), inc="ss", n=16)
        if r == 0:
            src3 = agq_out[lc].ap().rearrange(
                "(rk row) t -> rk row t", row=D + 8)[:, D:D + 8, :].rearrange(
                "rk row t -> rk (row t)").rearrange(
                "rk (p c) -> p rk c", p=128)
            t_c8[ci] = p.add("sync", lambda e, s3=src3: e.dma_start(
                out=sm["cmquar"].ap().rearrange("p (rk c) -> p rk c", c=32),
                in_=s3),
                waits=[t_agq[lc], wsel_done.get(ci - 1, T0)] +
                      ([t_cmtok_free] if ci < 2 else []), inc="ss", n=16)
        else:
            t_c8[ci] = t_c8[ci - 1]

    def issue_wb_xw(ci):
        par = ci % 2
        r_ = ci % CORES
        for tt in range(4):
            p.add("vector", lambda e, tt=tt, r_=r_: e.tensor_mul(
                sm["mk4"][:, tt * 8:(tt + 1) * 8],
                sm["cmquar"][:, r_ * 32 + tt * 8:r_ * 32 + (tt + 1) * 8],
                sm["ohm"]),
                waits=([t_c8[ci], t_wsel_read.get(ci - 2, T0)] if tt == 0 else []))
        for tt in range(4):
            p.add("vector", lambda e, tt=tt, par=par: e.reduce_sum(
                sm["wsel"][:, par * 4 + tt:par * 4 + tt + 1],
                sm["mk4"][:, tt * 8:(tt + 1) * 8], axis=AX_X))
        wsel_done[ci] = vgap()
        t_xw[ci] = t_xg[ci]

    def do_ln2(lc):
        nonlocal t_out_count
        for tp_ in range(2):
            tin = {}
            tz2 = {}
            tmv = {}
            trc2 = {}
            for u in range(2):
                tt = tp_ * 2 + u
                tx1t = p.add("scalar", lambda e, lc=lc, tt=tt, u=u: e.dma_start(
                    out=x1T_sl(u), in_=agq_in[lc].ap()[0:D, tt * 128:(tt + 1) * 128],
                    transpose=True),
                    waits=[t_agq[lc], last_ln_dma[u], last_ln_read[u]], inc="ss", n=16)
                tmo = p.add("scalar", lambda e, lc=lc, tt=tt, u=u: e.dma_start(
                    out=moe_sl(u), in_=rs_out[lc].ap()[tt * 128:(tt + 1) * 128, :]),
                    waits=[t_rs_q[lc], last_ln_dma[u], last_ln_read[u]], inc="ss", n=16)
                tin[u] = (tx1t, tmo)
            for u in range(2):
                tz2[u] = p.add("vector", lambda e, u=u: e.tensor_add(
                    z2_sl(u), x1T_sl(u), moe_sl(u)),
                    waits=list(tin[u]), inc="sv")
                last_ln_read[u] = tz2[u]
            for u in range(2):
                p.add("vector", lambda e, u=u: e.bn_stats(
                    sm["st12"][:, u * 12:u * 12 + 6], z2_sl(u)[:, 0:512]),
                    waits=[tz2[u]])
                p.add("vector", lambda e, u=u: e.bn_stats(
                    sm["st12"][:, u * 12 + 6:u * 12 + 12], z2_sl(u)[:, 512:1024]))
            for u in range(2):
                tmv[u] = p.add("vector", lambda e, u=u: e.bn_aggr(
                    sm["mv2"][:, u * 2:u * 2 + 2], sm["st12"][:, u * 12:u * 12 + 12]),
                    inc="sv")
            spacer()
            spacer()
            tmvg = p.add("vector", lambda e: e.tensor_copy(sm["dmy"][:, :], sm["tmpc"][:, :]),
                         inc="sv")
            tmv = {0: tmvg, 1: tmvg}
            for u in range(2):
                tsq2 = p.add("scalar", lambda e, u=u: e.activation(
                    sm["rstd2"][:, u:u + 1], sm["mv2"][:, u * 2 + 1:u * 2 + 2],
                    AF.Sqrt, bias=sm["epsc"][:, 0:1]), waits=[tmv[u]], inc="sc")
                trc2[u] = tsq2
            for u in range(2):
                trc2[u] = p.add("vector", lambda e, u=u: e.reciprocal(
                    sm["rstd2"][:, u:u + 1], sm["rstd2"][:, u:u + 1]),
                    waits=[trc2[u]], inc="sv")
            spacer()
            spacer()
            touts = {}
            for u in range(2):
                p.add("vector", lambda e, u=u: e.tensor_scalar(
                    z2_sl(u), z2_sl(u), sm["mv2"][:, u * 2:u * 2 + 1],
                    sm["rstd2"][:, u:u + 1], OP.subtract, OP.mult),
                    waits=[trc2[u]])
            tog = vgap()
            for u in range(2):
                touts[u] = tog
            for u in range(2):
                tt = tp_ * 2 + u
                last_ln_dma[u] = p.add("scalar", lambda e, lc=lc, tt=tt, u=u: e.dma_start(
                    out=out_d[lc * TC + tt * 128:(lc * TC + (tt + 1) * 128), :],
                    in_=z2_sl(u)), waits=[touts[u]], inc="so", n=16)
                t_out_count += 16

    issue_loads(0)
    issue_loads(1)
    issue_wb_xw(0)
    for ci in range(NCI):
        lc, r = ci // CORES, ci % CORES
        xs = xbuf(ci)
        for fi in range(FI):
            rr_ = hro % 4
            hro += 1
            tok = None
            for ki in range(8):
                tok = p.add("tensor", lambda e, ps=h1_rot[rr_], l=aw1_w1(ki, fi),
                            rhs=xs(ki), s=(ki == 0), t=(ki == 7):
                            e.matmul(ps[:, :], l, rhs, start=s, stop=t),
                            waits=([t_xw[ci], tw1, h1_evac[rr_]] if ki == 0 else []),
                            inc=("st" if ki == 7 else None))
            h1_mm[rr_] = tok
            if fi % 2 == 0:
                t_ev = p.add("vector", lambda e, ps=h1_rot[rr_], o=h1s_sl(fi, 0, TB):
                             e.tensor_scalar(o, ps[:, :], 0.0, None, OP.max),
                             waits=[tok], inc="sv")
            else:
                t_ev = p.add("scalar", lambda e, ps=h1_rot[rr_], o=h1s_sl(fi, 0, TB):
                             e.activation(o, ps[:, :], AF.Relu),
                             waits=[tok], inc="sc")
            h1_evac[rr_] = t_ev
        h1_mm_last[ci] = h1_mm[(hro - 1) % 4]
        h1s_done_v = vgap()
        h1s_done_c = p.add("scalar", lambda e: e.activation(
            sm["sdmy"][:, :], sm["b_in"][:, :], AF.Copy), inc="sc")
        if ci + 2 < NCI:
            issue_loads(ci + 2)
        if ci + 1 < NCI:
            issue_wb_xw(ci + 1)
        for tt in range(4):
            pb = pyo % 2
            pyo += 1
            tok = None
            for half in range(2):
                for fk in range(FI):
                    is_first = (half == 0 and fk == 0)
                    is_last = (half == 1 and fk == FI - 1)
                    tok = p.add("tensor", lambda e, ps=pyb[pb], h=half,
                                l=h1s_sl(fk, tt * 128, 128),
                                rhs=w2_rhs(fk, half), s=(fk == 0), t=(fk == FI - 1):
                                e.matmul(ps[:, h * 512:(h + 1) * 512], l, rhs,
                                         start=s, stop=t),
                                waits=(([h1s_done_v, h1s_done_c, tw2, py_evac[pb]] +
                                        ([py_mm_init[pb]] if py_first[pb] else []))
                                       if is_first else []),
                                inc=("st" if is_last else None))
                    if is_first:
                        py_first[pb] = False
            spar = tt % 2
            par = ci % 2
            p.add("vector", lambda e, ps=pyb[pb], o=rsst_sl(spar), tt=tt, par=par:
                  e.tensor_scalar(o, ps[:, :], sm["wsel"][:, par * 4 + tt:par * 4 + tt + 1],
                                  None, OP.mult),
                  waits=[tok, last_rs_dma[spar], wsel_done[ci]])
            tev = vgap()
            py_evac[pb] = tev
            t_wsel_read[ci] = tev
            last_rs_dma[spar] = p.add(
                "gpsimd", lambda e, q=lc, r=r, tt=tt, spar=spar:
                e.dma_start(out=rs_in[q].ap()[r * TC + tt * 128:
                                              r * TC + (tt + 1) * 128, :],
                            in_=rsst_sl(spar)),
                waits=[tev], inc="sg", n=16)
        if r == CORES - 1:
            t_rs_q[lc] = p.add("gpsimd", lambda e, q=lc: e.collective_compute(
                "ReduceScatter", OP.add, replica_groups=[list(range(CORES))],
                ins=[rs_in[q].ap().opt()], outs=[rs_out[q].ap().opt()]),
                waits=[last_rs_dma[0], last_rs_dma[1]], inc="scc")
        if r == 1 and lc > 0:
            do_ln2(lc - 1)
    do_ln2(NQ - 1)

    p.add("vector", lambda e: e.tensor_copy(sm["dmy"][:, :], sm["tmpc"][:, :]),
          waits=[("so", t_out_count)])

    with nc.Block() as block:
        p.emit(block)
    es.close()
    return nc


_NC_CACHE = None


def kernel(L0_attn_x, L0_attn_W_in, L0_attn_b_in, L0_attn_W_gx, L0_attn_b_gx,
           L0_attn_W_ga, L0_attn_b_ga, L0_attn_a_param, L0_attn_W_out,
           L0_attn_b_out, L0_ffn_gate_w, L0_ffn_w1, L0_ffn_w2):
    global _NC_CACHE
    f = np.float32
    x = np.asarray(L0_attn_x, f)
    xT = np.ascontiguousarray(x[0].T)

    def colmajor8(v):
        return np.ascontiguousarray(np.asarray(v, f).reshape(8, 128).T)

    W_in = np.ascontiguousarray(np.asarray(L0_attn_W_in, f))
    W_gx = np.ascontiguousarray(np.asarray(L0_attn_W_gx, f))
    W_ga = np.ascontiguousarray(np.asarray(L0_attn_W_ga, f))
    W_out = np.ascontiguousarray(np.asarray(L0_attn_W_out, f))
    gw = np.asarray(L0_ffn_gate_w, f)
    w1 = np.asarray(L0_ffn_w1, f)
    w2 = np.asarray(L0_ffn_w2, f)
    bins = colmajor8(L0_attn_b_in)
    bgxs = colmajor8(L0_attn_b_gx)
    bgas = colmajor8(L0_attn_b_ga)
    bouts = colmajor8(L0_attn_b_out)
    apars = colmajor8(L0_attn_a_param)
    idn = np.eye(128, dtype=f)
    gwp = np.ascontiguousarray(gw.T.reshape(8, 128, 8).transpose(1, 0, 2).reshape(128, 64))

    if _NC_CACHE is None:
        _NC_CACHE = build_nc()
    nc = _NC_CACHE

    in_maps = []
    for i in range(CORES):
        selc = np.zeros((128, 8), f)
        if i > 0:
            selc[:, i - 1] = 1.0
        ohm = np.tile((np.arange(8) == i).astype(f), (128, 1))
        smalls = np.concatenate([bins, bgxs, bgas, bouts, apars, selc, gwp, ohm,
                                 np.zeros((128, 8), f)], axis=1).astype(f)
        in_maps.append({
            "xT": np.ascontiguousarray(xT[:, i * TOK:(i + 1) * TOK]),
            "w_in": W_in, "w_gx": W_gx, "w_ga": W_ga, "w_out": W_out,
            "smalls_h": np.ascontiguousarray(smalls),
            "w1T": np.ascontiguousarray(w1[i].T),
            "w2T": np.ascontiguousarray(w2[i].T),
            "idn_h": idn,
        })

    res = run_bass_kernel_spmd(nc, in_maps, core_ids=list(range(CORES)))
    out = np.concatenate([res.results[i]["out"] for i in range(CORES)], axis=0)
    return out.reshape(1, S, D).astype(np.float32)


# version 11
# speedup vs baseline: 1.0765x; 1.0021x over previous
import numpy as np
from contextlib import ExitStack

import concourse.bass as bass
import concourse.mybir as mybir
from concourse.bass_utils import run_bass_kernel_spmd

F32 = mybir.dt.float32
BF16 = mybir.dt.bfloat16
AF = mybir.ActivationFunctionType
OP = mybir.AluOpType
AX_X = mybir.AxisListType.X

CORES = 8
D = 1024
S = 16384
TOK = S // CORES
TC = 512
NCHUNK = TOK // TC
FF = 4096
FI = FF // 128
TB = 512
NQ = 4
EPS = 1e-5


class P:

    ENGS = ("sync", "gpsimd", "vector", "scalar", "tensor")

    def __init__(self, nc, es):
        self.nc = nc
        self.ops = {e: [] for e in self.ENGS}
        self.sems = {}
        self.cnt = {}
        for name in ("sg", "ss", "sv", "sc", "st", "scc", "so"):
            self.sems[name] = es.enter_context(nc.semaphore(name))
            self.cnt[name] = 0

    def add(self, eng, fn, waits=(), inc=None, n=1):
        ws = tuple(w for w in waits if w is not None and w[1] > 0)
        self.ops[eng].append((fn, ws, inc, n))
        if inc is not None:
            self.cnt[inc] += n
            return (inc, self.cnt[inc])
        return None

    def tok(self, name):
        return (name, self.cnt[name])

    def _run(self, name, eng):
        seen = {}
        for fn, waits, inc, n in self.ops[name]:
            for (s, v) in waits:
                if seen.get(s, 0) < v:
                    eng.wait_ge(self.sems[s], v)
                    seen[s] = v
            instr = fn(eng)
            if inc is not None:
                instr.then_inc(self.sems[inc], n)

    def emit(self, block):
        runner = self._run

        @block.sync
        def _(e):
            runner("sync", e)

        @block.gpsimd
        def _(e):
            runner("gpsimd", e)

        @block.vector
        def _(e):
            runner("vector", e)

        @block.scalar
        def _(e):
            runner("scalar", e)

        @block.tensor
        def _(e):
            runner("tensor", e)


def build_nc():
    nc = bass.Bass("TRN2", target_bir_lowering=False, detect_race_conditions=False)

    xT = nc.dram_tensor("xT", [D, TOK], F32, kind="ExternalInput").ap()
    w_in = nc.dram_tensor("w_in", [D, D], F32, kind="ExternalInput").ap()
    w_gx = nc.dram_tensor("w_gx", [D, D], F32, kind="ExternalInput").ap()
    w_ga = nc.dram_tensor("w_ga", [D, D], F32, kind="ExternalInput").ap()
    w_out = nc.dram_tensor("w_out", [D, D], F32, kind="ExternalInput").ap()
    smalls_h = nc.dram_tensor("smalls_h", [128, 128], F32, kind="ExternalInput").ap()
    w1T = nc.dram_tensor("w1T", [D, FF], F32, kind="ExternalInput").ap()
    w2T = nc.dram_tensor("w2T", [FF, D], F32, kind="ExternalInput").ap()
    idn_h = nc.dram_tensor("idn_h", [128, 128], F32, kind="ExternalInput").ap()
    out_d = nc.dram_tensor("out", [TOK, D], F32, kind="ExternalOutput").ap()

    ag1_in = nc.dram_tensor("ag1_in", [2, D], F32)
    ag1_out = nc.dram_tensor("ag1_out", [2 * CORES, D], F32)
    agq_in = [nc.dram_tensor(f"agq_in_{q}", [D + 8, TC], BF16) for q in range(NQ)]
    agq_out = [nc.dram_tensor(f"agq_out_{q}", [(D + 8) * CORES, TC], BF16) for q in range(NQ)]
    rs_in = [nc.dram_tensor(f"rs_in_{q}", [CORES * TC, D], BF16) for q in range(NQ)]
    rs_out = [nc.dram_tensor(f"rs_out_{q}", [TC, D], BF16) for q in range(NQ)]

    es = ExitStack()

    AW1 = es.enter_context(nc.sbuf_tensor("AW1", [128, 32768], BF16))
    AW2 = es.enter_context(nc.sbuf_tensor("AW2", [128, 32768], BF16))
    AX = es.enter_context(nc.sbuf_tensor("AX", [128, 8192], BF16))
    AB = es.enter_context(nc.sbuf_tensor("AB", [128, 14336], BF16))
    AFr = es.enter_context(nc.sbuf_tensor("AF", [128, 8192], F32))
    AFb = AFr.ap().bitcast(BF16)
    ABf = AB.ap().bitcast(F32)

    SML = es.enter_context(nc.sbuf_tensor("SML", [128, 128], F32))
    sm = {}
    sm["b_in"] = SML[:, 0:8]
    sm["bgx_hf"] = SML[:, 8:16]
    sm["bga_hf"] = SML[:, 16:24]
    sm["b_out"] = SML[:, 24:32]
    sm["aptmp"] = SML[:, 32:40]
    sm["selc"] = SML[:, 40:48]
    sm["gwtf"] = SML[:, 48:112]
    sm["ohm"] = SML[:, 112:120]
    for nm, shp, dt in [
        ("msp_h", [128, 8], F32),
        ("hcar", [128, 8], F32), ("apcar", [128, 8], F32),
        ("cin", [128, 8], F32), ("tmpc", [128, 8], F32), ("dmy", [128, 8], F32),
        ("aesum", [128, 128], F32), ("cpre", [128, 64], F32),
        ("idn", [128, 128], F32),
        ("onesc", [128, 1], BF16), ("onesf", [1, 128], F32),
        ("glt4", [128, 32], F32), ("mk4", [128, 32], F32),
        ("ms4", [128, 32], F32), ("comb4", [128, 32], F32),
        ("st12", [128, 24], F32), ("mv2", [128, 4], F32), ("rstd2", [128, 2], F32),
        ("cmtok0", [128, 32], BF16), ("cmquar", [128, 256], BF16),
        ("wsel", [128, 8], F32), ("sdmy", [128, 8], F32),
        ("epsc", [128, 1], F32), ("onec", [128, 1], F32), ("spt", [128, 8], F32),
    ]:
        sm[nm] = es.enter_context(nc.sbuf_tensor(nm, shp, dt))
    sm["asum"] = sm["aesum"][:, 0:64]
    sm["esum"] = sm["aesum"][:, 64:128]
    sm["c24"] = sm["mk4"]
    sm["m14"] = sm["st12"][:, 0:4]
    sm["m24"] = sm["st12"][:, 4:8]
    sm["dd4"] = sm["st12"][:, 8:12]
    sm["th4"] = sm["st12"][:, 12:16]
    sm["g14"] = sm["st12"][:, 16:20]
    sm["g24"] = sm["st12"][:, 20:24]

    PA0 = es.enter_context(nc.psum_tensor("PA0", [128, 512], F32))
    PA1 = es.enter_context(nc.psum_tensor("PA1", [128, 512], F32))
    PA2 = es.enter_context(nc.psum_tensor("PA2", [128, 512], F32))
    PSP = es.enter_context(nc.psum_tensor("PSP", [128, 512], F32))
    PY0 = es.enter_context(nc.psum_tensor("PY0", [128, 1024], F32))
    PY1 = es.enter_context(nc.psum_tensor("PY1", [128, 1024], F32))

    p = P(nc, es)
    T0 = ("sv", 0)

    def aw1_w(base, ki, ii):
        c = base + ki * 1024 + ii * 128
        return AW1[:, c:c + 128]

    def aw1_w1(ki, fi):
        c = ki * 4096 + fi * 128
        return AW1[:, c:c + 128]

    def hloc_sl(ii, c):
        a = ii * TOK + c * TC
        return AW2[:, a:a + TC]

    def apb_sl(ii, c):
        a = 16384 + ii * TOK + c * TC
        return AW2[:, a:a + TC]

    def w2_rhs(fk, half):
        a = fk * 1024 + half * 512
        return AW2[:, a:a + 512]

    def x0b_sl(ki):
        return AX[:, ki * TC:(ki + 1) * TC]

    def xinb_sl(ii):
        return AX[:, 4096 + ii * TC:4096 + (ii + 1) * TC]

    def xinf_sl(ii):
        return AFr[:, ii * TC:(ii + 1) * TC]

    def afull_sl(ii):
        return AFr[:, 4096 + ii * TC:4096 + (ii + 1) * TC]

    def slot(k):
        return ABf[:, 6144 + k * 512:6144 + (k + 1) * 512]

    def ufull_sl(ii):
        return AB[:, ii * TC:(ii + 1) * TC]

    def xfull_sl(ii):
        return AB[:, 8192 + ii * 512:8192 + (ii + 1) * 512]

    def hfix_sl(ii):
        return AB[:, ii * TC:(ii + 1) * TC]

    def x1c_sl(ii):
        return AB[:, 4096 + ii * TC:4096 + (ii + 1) * TC]

    def zb_slot(k):
        return AB[:, 8192 + k * 512:8192 + (k + 1) * 512]

    def zsq_slot(k):
        return AB[:, 9216 + k * 512:9216 + (k + 1) * 512]

    def z_sl(ii):
        return AFr[:, ii * TC:(ii + 1) * TC]

    def rows(k):
        return AFr[0:1, 4096 + k * 512:4096 + (k + 1) * 512]

    def x1tmp():
        return AFr[:, 6144:6656]

    def glog_sl():
        return ABf[0:8, 5120:5632]

    def combT_sl(c):
        return AB[0:8, 12288 + c * TC:12288 + (c + 1) * TC]

    def xg_sl(ki):
        return AX[:, ki * TB:(ki + 1) * TB]

    def xw_sl(ki):
        return AX[:, 4096 + ki * TB:4096 + (ki + 1) * TB]

    def h1s_sl(fk, off, ln):
        a = fk * TB + off
        return AFb[:, a:a + ln]

    def x1T_sl(par):
        return AB[:, par * 1024:(par + 1) * 1024]

    def moe_sl(par):
        return AB[:, 2048 + par * 1024:2048 + (par + 1) * 1024]

    def rsst_sl(par):
        return AB[:, 4096 + par * 1024:4096 + (par + 1) * 1024]

    def wb_sl(par):
        return AB[:, 6144 + par * 512:6144 + (par + 1) * 512]

    def cm8_sl(par):
        return AB[0:8, 7168 + par * 512:7168 + (par + 1) * 512]

    def cm8_full(par):
        return AB[:, 7168 + par * 512:7168 + (par + 1) * 512]

    def z2_sl(par):
        return ABf[:, 4096 + par * 1024:4096 + (par + 1) * 1024]

    def spacer():
        p.add("vector", lambda e: e.tensor_copy(sm["dmy"][:, :], sm["tmpc"][:, :]))

    def vgap():
        spacer()
        return p.add("vector", lambda e: e.tensor_copy(sm["dmy"][:, :], sm["tmpc"][:, :]),
                     inc="sv")

    src3_0 = xT.rearrange("(ki p) t -> p ki t", p=128)[:, :, 0:TC]
    dst3_0 = AX[:, 0:4096].rearrange("p (ki t) -> p ki t", ki=8)
    t_x0_first = p.add("gpsimd", lambda e: e.dma_start(out=dst3_0, in_=src3_0),
                       inc="sg", n=16)
    t_w = {}
    for nm, src, base in [("in", w_in, 0), ("gx", w_gx, 8192),
                          ("ga", w_ga, 16384), ("out", w_out, 24576)]:
        src3 = src.rearrange("(ki p) c -> p ki c", p=128)
        dst3 = AW1[:, base:base + 8192].rearrange("p (ki c) -> p ki c", ki=8)
        t_w[nm] = p.add("gpsimd",
                        lambda e, o=dst3, i=src3: e.dma_start(out=o, in_=i),
                        inc="sg", n=16)


    t_small = p.add("sync", lambda e: e.dma_start(out=SML[:, :], in_=smalls_h),
                    inc="ss", n=16)
    t_idn = p.add("sync", lambda e: e.dma_start(out=sm["idn"][:, :], in_=idn_h),
                  waits=[t_small], inc="ss", n=16)
    t_gwtf = t_small
    small_loads = [t_small] * 6
    t_sml = t_small

    p.add("vector", lambda e: e.memset(sm["hcar"][:, :], 0.0))
    p.add("vector", lambda e: e.memset(sm["apcar"][:, :], 1.0))
    p.add("vector", lambda e: e.memset(sm["cin"][:, :], 0.0))
    p.add("vector", lambda e: e.memset(sm["tmpc"][:, :], 0.0))
    p.add("vector", lambda e: e.memset(sm["onesc"][:, :], 1.0))
    p.add("vector", lambda e: e.memset(sm["epsc"][:, :], EPS))
    p.add("vector", lambda e: e.memset(sm["onec"][:, :], 1.0))
    t_ones = p.add("vector", lambda e: e.memset(sm["onesf"][:, :], 1.0), inc="sv")
    def sgap(n=8):
        for _ in range(n):
            p.add("scalar", lambda e: e.activation(sm["sdmy"][:, :], sm["b_in"][:, :],
                                                   AF.Copy))
    ta1 = p.add("scalar", lambda e: e.activation(
        sm["spt"][:, :], sm["aptmp"][:, :], AF.Abs),
        waits=[small_loads[4]], inc="sc")
    t_r = p.add("scalar", lambda e: e.activation(
        sm["msp_h"][:, :], sm["aptmp"][:, :], AF.Relu), inc="sc")
    sgap()
    t_e = p.add("scalar", lambda e: e.activation(
        sm["aptmp"][:, :], sm["spt"][:, :], AF.Exp, scale=-1.0), inc="sc")
    sgap()
    t_l = p.add("scalar", lambda e: e.activation(
        sm["spt"][:, :], sm["aptmp"][:, :], AF.Ln, bias=sm["onec"][:, 0:1]),
        waits=[t_ones], inc="sc")
    sgap()
    tad = p.add("vector", lambda e: e.tensor_add(
        sm["msp_h"][:, :], sm["msp_h"][:, :], sm["spt"][:, :]),
        waits=[t_l, t_r], inc="sv")
    spacer()
    spacer()
    t_msp = p.add("vector", lambda e: e.tensor_scalar_mul(
        sm["msp_h"][:, :], sm["msp_h"][:, :], -4.0), waits=[tad], inc="sv")
    t_bgxh = p.add("vector", lambda e: e.tensor_scalar_mul(
        sm["bgx_hf"][:, :], sm["bgx_hf"][:, :], 0.5), waits=[small_loads[1]], inc="sv")
    t_bgah = p.add("vector", lambda e: e.tensor_scalar_mul(
        sm["bga_hf"][:, :], sm["bga_hf"][:, :], 0.5), waits=[small_loads[2]], inc="sv")

    PY0a, PY0b = PY0[:, 0:512], PY0[:, 512:1024]
    PY1a, PY1b = PY1[:, 0:512], PY1[:, 512:1024]
    rotors = [PA0[:, :], PA1[:, :], PA2[:, :], PSP[:, :], PY0a, PY0b, PY1a, PY1b]
    rot_evac = [T0] * 8
    rot_i = 0

    def next_rot():
        nonlocal rot_i
        r = rot_i % 8
        rot_i += 1
        return r

    t_x0 = [None] * NCHUNK
    t_xin_mm_last = T0
    slot_free = [T0, T0]
    t_u_done = {}
    t_x_done = {}
    t_sct = {}

    for c in range(NCHUNK):
        if c == 0:
            t_x0[0] = t_x0_first
        else:
            src3 = xT.rearrange("(ki p) t -> p ki t", p=128)[:, :, c * TC:(c + 1) * TC]
            dst3 = AX[:, 0:4096].rearrange("p (ki t) -> p ki t", ki=8)
            t_x0[c] = p.add("gpsimd", lambda e, o=dst3, i=src3: e.dma_start(out=o, in_=i),
                            waits=[t_xin_mm_last], inc="sg", n=16)

        t_evac_xin = []
        for ii in range(8):
            r = next_rot()
            tok = None
            for ki in range(8):
                tok = p.add("tensor", lambda e, ps=rotors[r], l=aw1_w(0, ki, ii),
                            rr=x0b_sl(ki), s=(ki == 0), t=(ki == 7):
                            e.matmul(ps, l, rr, start=s, stop=t),
                            waits=([t_x0[c], t_w["in"], rot_evac[r]] if ki == 0 else []),
                            inc=("st" if ki == 7 else None))
            p.add("vector", lambda e, ps=rotors[r], o=xinf_sl(ii), b=sm["b_in"][:, ii:ii + 1]:
                  e.tensor_scalar_add(o, ps, b), waits=[tok, t_sml])
            e2 = p.add("vector", lambda e, ps=rotors[r], o=xinb_sl(ii), b=sm["b_in"][:, ii:ii + 1]:
                       e.tensor_scalar_add(o, ps, b), inc="sv")
            rot_evac[r] = e2
            t_evac_xin.append(e2)
        t_evac_xin_g = vgap()
        t_xin_mm_last = p.tok("st")

        for ii in range(8):
            sA = slot(0)
            sB = slot(1)
            rA = next_rot()
            tokA = None
            for ki in range(8):
                tokA = p.add("tensor", lambda e, ps=rotors[rA], l=aw1_w(16384, ki, ii),
                             rr=xinb_sl(ki), s=(ki == 0), t=(ki == 7):
                             e.matmul(ps, l, rr, start=s, stop=t),
                             waits=([t_evac_xin_g, t_w["ga"], rot_evac[rA]] if ki == 0 else []),
                             inc=("st" if ki == 7 else None))
            rB = next_rot()
            tokB = None
            for ki in range(8):
                tokB = p.add("tensor", lambda e, ps=rotors[rB], l=aw1_w(8192, ki, ii),
                             rr=xinb_sl(ki), s=(ki == 0), t=(ki == 7):
                             e.matmul(ps, l, rr, start=s, stop=t),
                             waits=([t_w["gx"], rot_evac[rB]] if ki == 0 else []),
                             inc=("st" if ki == 7 else None))
            ta = p.add("scalar", lambda e, ps=rotors[rA], o=sB, b=sm["bga_hf"][:, ii:ii + 1]:
                       e.activation(o, ps, AF.Tanh, bias=b, scale=0.5),
                       waits=[tokA, t_bgah, slot_free[1]], inc="sc")
            rot_evac[rA] = ta
            tb = p.add("scalar", lambda e, ps=rotors[rB], o=sA, b=sm["bgx_hf"][:, ii:ii + 1]:
                       e.activation(o, ps, AF.Tanh, bias=b, scale=0.5),
                       waits=[tokB, t_bgxh, slot_free[0]], inc="sc")
            rot_evac[rB] = tb
            tl = p.add("vector", lambda e, s=sB, m=sm["msp_h"][:, ii:ii + 1]:
                       e.tensor_scalar(s, s, m, m, OP.mult, OP.add),
                       waits=[ta, t_msp], inc="sv")
            tu = p.add("vector", lambda e, s=sA, o=ufull_sl(ii), xi=xinf_sl(ii):
                       e.scalar_tensor_tensor(o, s, 1.0, xi, OP.add, OP.mult),
                       waits=[tb], inc="sv")
            t_u_done[(c, ii)] = tu
            p.add("scalar", lambda e, s=sB, o=afull_sl(ii):
                  e.activation(o, s, AF.Exp), waits=[tu], inc="sc")
            tq = p.add("scalar", lambda e, s=sB, o=sA:
                       e.activation(o, s, AF.Exp, scale=2.0), waits=[tu], inc="sc")
            slot_free[1] = tq
            p.add("vector", lambda e, s=sA: e.tensor_scalar(
                s, s, -1.0, 1.0, OP.mult, OP.add), waits=[tq])
            p.add("vector", lambda e, s=sA: e.tensor_scalar_max(s, s, 0.0))
            tx3 = p.add("vector", lambda e, s=sA, o=xfull_sl(ii): e.tensor_copy(o, s),
                        inc="sv")
            slot_free[0] = tx3
            t_x_done[(c, ii)] = tx3

        for ii in range(8):
            t_sct[(c, ii)] = p.add("scalar", lambda e, s=xfull_sl(ii):
                                   e.activation(s, s, AF.Sqrt, scale=0.25),
                                   waits=[t_x_done[(c, ii)]], inc="sc")

        for i0 in range(0, 8, 2):
            prs = [i0, i0 + 1]
            bsl = {ii: AFr[:, (ii % 2) * 512:(ii % 2) * 512 + 512] for ii in prs}
            hsl = {ii: AFr[:, 1024 + (ii % 2) * 512:1024 + (ii % 2) * 512 + 512] for ii in prs}
            asl = {ii: AFr[:, 2048 + (ii % 2) * 512:2048 + (ii % 2) * 512 + 512] for ii in prs}
            for ii in prs:
                p.add("vector", lambda e, o=bsl[ii], u=ufull_sl(ii), s=xfull_sl(ii):
                      e.tensor_mul(o, u, s),
                      waits=[t_sct[(c, ii)], t_u_done[(c, ii)]])
            for ii in prs:
                p.add("vector", lambda e, o=hsl[ii], a=afull_sl(ii), b=bsl[ii],
                      i0_=sm["hcar"][:, ii:ii + 1]:
                      e.tensor_tensor_scan(o, a, b, i0_, OP.mult, OP.add))
            for ii in prs:
                p.add("vector", lambda e, o=asl[ii], a=afull_sl(ii),
                      i0_=sm["apcar"][:, ii:ii + 1]:
                      e.tensor_tensor_scan(o, a, a, i0_, OP.mult, OP.bypass))
            for ii in prs:
                p.add("vector", lambda e, o=hloc_sl(ii, c), i=hsl[ii]: e.tensor_copy(o, i))
            for ii in prs:
                p.add("vector", lambda e, o=apb_sl(ii, c), i=asl[ii]: e.tensor_copy(o, i))
            for ii in prs:
                p.add("vector", lambda e, o=sm["hcar"][:, ii:ii + 1], i=hsl[ii][:, 511:512]:
                      e.tensor_copy(o, i))
            for ii in prs:
                p.add("vector", lambda e, o=sm["apcar"][:, ii:ii + 1], i=asl[ii][:, 511:512]:
                      e.tensor_copy(o, i), inc=("sv" if ii == i0 + 1 else None))

    t_gates_done = p.tok("st")
    t_lastscan = vgap()
    ag1i3 = ag1_in.ap().rearrange("r (p ii) -> r p ii", ii=8)
    tg1 = p.add("gpsimd", lambda e: e.dma_start(out=ag1i3[0], in_=sm["apcar"][:, :]),
                waits=[t_lastscan], inc="sg", n=16)
    tg2 = p.add("gpsimd", lambda e: e.dma_start(out=ag1i3[1], in_=sm["hcar"][:, :]),
                inc="sg", n=16)
    t_ag1 = p.add("gpsimd", lambda e: e.collective_compute(
        "AllGather", OP.bypass, replica_groups=[list(range(CORES))],
        ins=[ag1_in.ap().opt()], outs=[ag1_out.ap().opt()]),
        waits=[tg1, tg2], inc="scc")
    ag1o3 = ag1_out.ap().rearrange("(r two) (p ii) -> two p r ii", two=2, ii=8)
    tga = p.add("gpsimd", lambda e: e.dma_start(
        out=sm["asum"].rearrange("p (r ii) -> p r ii", ii=8), in_=ag1o3[0]),
        waits=[t_ag1], inc="sg", n=16)
    tge = p.add("gpsimd", lambda e: e.dma_start(
        out=sm["esum"].rearrange("p (r ii) -> p r ii", ii=8), in_=ag1o3[1]),
        waits=[tga], inc="sg", n=16)
    p.add("vector", lambda e: e.tensor_copy(sm["cpre"][:, 0:8], sm["esum"][:, 0:8]),
          waits=[tge])
    for k in range(1, 7):
        spacer()
        p.add("vector", lambda e, k=k: e.tensor_mul(
            sm["tmpc"][:, :], sm["asum"][:, k * 8:(k + 1) * 8],
            sm["cpre"][:, (k - 1) * 8:k * 8]))
        spacer()
        p.add("vector", lambda e, k=k: e.tensor_add(
            sm["cpre"][:, k * 8:(k + 1) * 8], sm["tmpc"][:, :],
            sm["esum"][:, k * 8:(k + 1) * 8]))
    for k in range(7):
        spacer()
        spacer()
        p.add("vector", lambda e, k=k: e.scalar_tensor_tensor(
            sm["cin"][:, :], sm["cpre"][:, k * 8:(k + 1) * 8],
            sm["selc"][:, k:k + 1], sm["cin"][:, :], OP.mult, OP.add))
    spacer()
    spacer()
    t_cin = p.add("vector", lambda e: e.tensor_copy(sm["dmy"][:, :], sm["cin"][:, :]),
                  inc="sv")

    rot3 = [PY0a, PY0b, PY1a, PY1b]
    rot3_evac = [rot_evac[4], rot_evac[5], rot_evac[6], rot_evac[7]]
    psp_free = rot_evac[3]
    pa2_free = rot_evac[2]
    pa0_free = rot_evac[0]
    pa1_free = rot_evac[1]
    t_glog_mm = T0
    t_agq = [None] * NQ
    t_x1dma_prev = ("sg", 0)
    t_zev = T0
    t_wout_mm = T0
    t_hf = []
    tct = T0

    for c in range(NCHUNK):
        src3 = xT.rearrange("(ki p) t -> p ki t", p=128)[:, :, c * TC:(c + 1) * TC]
        dst3 = AX[:, 0:4096].rearrange("p (ki t) -> p ki t", ki=8)
        t_x0r = p.add("gpsimd", lambda e, o=dst3, i=src3: e.dma_start(out=o, in_=i),
                      waits=[t_lastscan] + ([t_zev] if c > 0 else []), inc="sg", n=16)
        t_hf = []
        for ii in range(8):
            t_hf.append(p.add("vector", lambda e, o=hfix_sl(ii), a=apb_sl(ii, c),
                              s=sm["cin"][:, ii:ii + 1], h=hloc_sl(ii, c):
                              e.scalar_tensor_tensor(o, a, s, h, OP.mult, OP.add),
                              waits=[t_cin] + ([t_wout_mm] if c > 0 else []), inc="sv"))
        t_hf_g = vgap()
        t_zs = []
        for ii in range(8):
            r = ii % 4
            tok = None
            for ki in range(8):
                tok = p.add("tensor", lambda e, ps=rot3[r], l=aw1_w(24576, ki, ii),
                            rr=hfix_sl(ki), s=(ki == 0), t=(ki == 7):
                            e.matmul(ps, l, rr, start=s, stop=t),
                            waits=([t_hf_g, t_w["out"], rot3_evac[r]] if ki == 0 else []),
                            inc=("st" if ki == 7 else None))
            tz = p.add("vector", lambda e, ps=rot3[r], o=z_sl(ii),
                       b=sm["b_out"][:, ii:ii + 1], x0=x0b_sl(ii):
                       e.scalar_tensor_tensor(o, ps, b, x0, OP.add, OP.add),
                       waits=[tok, t_x0r], inc="sv")
            rot3_evac[r] = tz
            t_zs.append(tz)
        t_wout_mm = p.tok("st")
        t_zev = t_zs[-1]
        t_slot_mm = [T0, T0]
        tok_sum = tok_sq = None
        for ii in range(8):
            k = ii % 2
            p.add("vector", lambda e, o=zb_slot(k), z=z_sl(ii): e.tensor_copy(o, z),
                  waits=[t_zs[ii], t_slot_mm[k]])
            p.add("vector", lambda e, o=zsq_slot(k), z=z_sl(ii): e.tensor_mul(o, z, z))
            tzb = tzq = vgap()
            tok_sum = p.add("tensor", lambda e, rr=zb_slot(k), s=(ii == 0), t=(ii == 7):
                            e.matmul(PA0[0:1, :], sm["onesc"][:, :], rr, start=s, stop=t),
                            waits=[tzb, t_ones] + ([pa0_free] if ii == 0 else []),
                            inc="st")
            tok_sq = p.add("tensor", lambda e, rr=zsq_slot(k), s=(ii == 0), t=(ii == 7):
                           e.matmul(PA1[0:1, :], sm["onesc"][:, :], rr, start=s, stop=t),
                           waits=[tzq] + ([pa1_free] if ii == 0 else []),
                           inc="st")
            t_slot_mm[k] = tok_sq
        trm = p.add("vector", lambda e: e.tensor_scalar_mul(rows(0), PA0[0:1, :], 1.0 / D),
                    waits=[tok_sum], inc="sv")
        p.add("vector", lambda e: e.tensor_scalar_mul(rows(1), PA1[0:1, :], 1.0 / D),
              waits=[tok_sq])
        spacer()
        p.add("vector", lambda e: e.tensor_mul(rows(2), rows(0), rows(0)))
        spacer()
        p.add("vector", lambda e: e.tensor_sub(rows(1), rows(1), rows(2)))
        spacer()
        spacer()
        tvar = p.add("vector", lambda e: e.tensor_copy(sm["dmy"][:, :], sm["tmpc"][:, :]),
                     inc="sv")
        tsq = p.add("scalar", lambda e: e.activation(rows(2), rows(1), AF.Sqrt, bias=sm["epsc"][0:1, 0:1]),
                    waits=[tvar], inc="sc")
        trc = p.add("vector", lambda e: e.reciprocal(rows(3), rows(2)), waits=[tsq],
                    inc="sv")
        spacer()
        spacer()
        p.add("vector", lambda e: e.tensor_mul(rows(2), rows(0), rows(3)))
        tmr = vgap()
        tbr = p.add("tensor", lambda e: e.matmul(PA1[:, :], sm["onesf"][:, :], rows(3),
                                                 start=True, stop=True),
                    waits=[tmr, tvar, t_ones], inc="st")
        tbm = p.add("tensor", lambda e: e.matmul(PA0[:, :], sm["onesf"][:, :], rows(2),
                                                 start=True, stop=True),
                    waits=[tmr, trm], inc="st")
        tx1last = None
        for ii in range(8):
            p.add("vector", lambda e, z=z_sl(ii): e.tensor_mul(x1tmp(), z, PA1[:, :]),
                  waits=([tbr, tbm, t_glog_mm, t_x1dma_prev] if ii == 0 else []))
            p.add("vector", lambda e, z=z_sl(ii): e.tensor_sub(z, x1tmp(), PA0[:, :]))
            p.add("vector", lambda e, o=x1c_sl(ii), z=z_sl(ii): e.tensor_copy(o, z))
            tx1last = None
        tx1last = vgap()
        pa0_free = tx1last
        pa1_free = tx1last
        t_x1dma_prev = p.add("gpsimd", lambda e, c=c: e.dma_start(
            out=agq_in[c].ap()[0:D, :].rearrange("(ki p) t -> p ki t", p=128),
            in_=AB[:, 4096:8192].rearrange("p (ki t) -> p ki t", ki=8)),
            waits=[tx1last], inc="sg", n=16)
        tok = None
        for ki in range(8):
            tok = p.add("tensor", lambda e, l=sm["gwtf"][:, ki * 8:(ki + 1) * 8],
                        rr=z_sl(ki), s=(ki == 0), t=(ki == 7):
                        e.matmul(PSP[0:8, :], l, rr, start=s, stop=t),
                        waits=([tx1last, t_gwtf, psp_free] if ki == 0 else []),
                        inc=("st" if ki == 7 else None))
        t_glog_mm = tok
        tgl = p.add("vector", lambda e: e.tensor_copy(glog_sl(), PSP[0:8, :]),
                    waits=[tok], inc="sv")
        psp_free = tgl
        tprev = pa2_free
        for j in range(4):
            ttr = p.add("tensor", lambda e, j=j: e.transpose(
                PA2[:, 0:8], glog_sl()[0:8, j * 128:(j + 1) * 128],
                sm["idn"][0:8, 0:8]),
                waits=[tgl, t_idn, tprev], inc="st")
            tprev = p.add("vector", lambda e, j=j: e.tensor_copy(
                sm["glt4"][:, j * 8:(j + 1) * 8], PA2[:, 0:8]),
                waits=[ttr], inc="sv")
        pa2_free = tprev
        for j in range(4):
            p.add("vector", lambda e, j=j: e.reduce_max(
                sm["m14"][:, j:j + 1], sm["glt4"][:, j * 8:(j + 1) * 8], axis=AX_X))
        spacer()
        spacer()
        for j in range(4):
            p.add("vector", lambda e, j=j: e.tensor_scalar(
                sm["mk4"][:, j * 8:(j + 1) * 8], sm["glt4"][:, j * 8:(j + 1) * 8],
                sm["m14"][:, j:j + 1], 1.0, OP.is_equal, OP.mult))
        spacer()
        for j in range(4):
            p.add("vector", lambda e, j=j: e.scalar_tensor_tensor(
                sm["ms4"][:, j * 8:(j + 1) * 8], sm["mk4"][:, j * 8:(j + 1) * 8],
                -1e30, sm["glt4"][:, j * 8:(j + 1) * 8], OP.mult, OP.add))
        spacer()
        for j in range(4):
            p.add("vector", lambda e, j=j: e.reduce_max(
                sm["m24"][:, j:j + 1], sm["ms4"][:, j * 8:(j + 1) * 8], axis=AX_X))
        spacer()
        spacer()
        p.add("vector", lambda e: e.tensor_sub(sm["dd4"], sm["m24"],
                                               sm["m14"]))
        spacer()
        spacer()
        tdd = p.add("vector", lambda e: e.tensor_copy(sm["dmy"][:, :], sm["tmpc"][:, :]),
                    inc="sv")
        tth = p.add("scalar", lambda e: e.activation(sm["th4"], sm["dd4"],
                                                     AF.Tanh, scale=0.5),
                    waits=[tdd], inc="sc")
        p.add("vector", lambda e: e.tensor_scalar(sm["g24"], sm["th4"],
                                                  0.5, 0.5, OP.mult, OP.add),
              waits=[tth])
        p.add("vector", lambda e: e.tensor_scalar(sm["g14"], sm["th4"],
                                                  -0.5, 0.5, OP.mult, OP.add))
        spacer()
        spacer()
        for j in range(4):
            p.add("vector", lambda e, j=j: e.tensor_scalar(
                sm["comb4"][:, j * 8:(j + 1) * 8], sm["glt4"][:, j * 8:(j + 1) * 8],
                sm["m14"][:, j:j + 1], sm["g14"][:, j:j + 1], OP.is_equal, OP.mult))
        spacer()
        for j in range(4):
            p.add("vector", lambda e, j=j: e.tensor_scalar(
                sm["c24"][:, j * 8:(j + 1) * 8], sm["ms4"][:, j * 8:(j + 1) * 8],
                sm["m24"][:, j:j + 1], sm["g24"][:, j:j + 1], OP.is_equal, OP.mult))
        spacer()
        p.add("vector", lambda e: e.tensor_add(sm["comb4"][:, :], sm["comb4"][:, :],
                                               sm["c24"][:, :]))
        spacer()
        spacer()
        tcb = p.add("vector", lambda e: e.tensor_copy(sm["cmtok0"][:, :], sm["comb4"][:, :]),
                    inc="sv")
        tct = vgap()
        tcdma = p.add("gpsimd", lambda e, c=c: e.dma_start(
            out=agq_in[c].ap()[D:D + 8, :].rearrange(
                "r t -> (r t)").rearrange("(p c) -> p c", p=128),
            in_=sm["cmtok0"][:, :]),
            waits=[tct], inc="sg", n=16)
        t_agq[c] = p.add("gpsimd", lambda e, c=c: e.collective_compute(
            "AllGather", OP.bypass, replica_groups=[list(range(CORES))],
            ins=[agq_in[c].ap().opt()], outs=[agq_out[c].ap().opt()]),
            waits=[tcdma, t_x1dma_prev], inc="scc")

    t_tensorA_done = p.tok("st")

    tw1a = p.add("gpsimd", lambda e: e.dma_start(
        out=AW1[:, 0:24576].rearrange("p (ki f) -> p ki f", ki=6),
        in_=w1T[0:768, :].rearrange("(ki p) f -> p ki f", p=128)),
        waits=[t_gates_done], inc="sg", n=16)
    tw1 = p.add("gpsimd", lambda e: e.dma_start(
        out=AW1[:, 24576:32768].rearrange("p (ki f) -> p ki f", ki=2),
        in_=w1T[768:1024, :].rearrange("(ki p) f -> p ki f", p=128)),
        waits=[t_tensorA_done, tw1a], inc="sg", n=16)
    tw2 = p.add("gpsimd", lambda e: e.dma_start(
        out=AW2.ap().rearrange("p (fk d) -> p fk d", fk=32),
        in_=w2T.rearrange("(fk p) d -> p fk d", p=128)),
        waits=[t_hf[-1]], inc="sg", n=16)

    h1_rot = [PA0, PA1, PA2, PSP]
    h1_evac = [pa0_free, pa1_free, pa2_free, psp_free]
    h1_mm = [T0, T0, T0, T0]
    pyb = [PY0, PY1]
    py_evac = [rot3_evac[1], rot3_evac[3]]
    py_mm_init = [rot3_evac[0], rot3_evac[2]]
    py_first = [True, True]
    t_xg = {}
    t_c8 = {}
    t_xw = {}
    wsel_done = {}
    t_wsel_read = {}
    t_cmtok_free = tw1
    h1_mm_last = {-1: T0, -2: T0}
    last_rs_dma = [("sg", 0), ("sg", 0)]
    t_out_count = 0
    hro = 0
    pyo = 0
    last_ln_dma = [("so", 0), ("so", 0)]
    last_ln_read = [T0, T0]
    t_rs_q = {}
    NCI = NQ * CORES

    def xbuf(ci):
        par = ci % 2
        return lambda ki: AX[:, par * 4096 + ki * TB:par * 4096 + (ki + 1) * TB]

    def issue_loads(ci):
        lc, r = ci // CORES, ci % CORES
        par = ci % 2
        xs = xbuf(ci)
        t_xg[ci] = p.add("sync", lambda e, r=r, lc=lc, par=par: e.dma_start(
            out=AX[:, par * 4096:par * 4096 + 4096].rearrange("p (ki t) -> p ki t", ki=8),
            in_=agq_out[lc].ap()[r * (D + 8):r * (D + 8) + D, :].rearrange(
                "(ki p) t -> p ki t", p=128)),
            waits=[t_agq[lc], t_zev, h1_mm_last.get(ci - 2, T0)] +
                  ([t_tensorA_done] if ci < 2 else []), inc="ss", n=16)
        if r == 0:
            src3 = agq_out[lc].ap().rearrange(
                "(rk row) t -> rk row t", row=D + 8)[:, D:D + 8, :].rearrange(
                "rk row t -> rk (row t)").rearrange(
                "rk (p c) -> p rk c", p=128)
            t_c8[ci] = p.add("sync", lambda e, s3=src3: e.dma_start(
                out=sm["cmquar"].ap().rearrange("p (rk c) -> p rk c", c=32),
                in_=s3),
                waits=[t_agq[lc], wsel_done.get(ci - 1, T0)] +
                      ([t_cmtok_free] if ci < 2 else []), inc="ss", n=16)
        else:
            t_c8[ci] = t_c8[ci - 1]

    def issue_wb_xw(ci):
        par = ci % 2
        r_ = ci % CORES
        for tt in range(4):
            p.add("vector", lambda e, tt=tt, r_=r_: e.tensor_mul(
                sm["mk4"][:, tt * 8:(tt + 1) * 8],
                sm["cmquar"][:, r_ * 32 + tt * 8:r_ * 32 + (tt + 1) * 8],
                sm["ohm"]),
                waits=([t_c8[ci], t_wsel_read.get(ci - 2, T0)] if tt == 0 else []))
        for tt in range(4):
            p.add("vector", lambda e, tt=tt, par=par: e.reduce_sum(
                sm["wsel"][:, par * 4 + tt:par * 4 + tt + 1],
                sm["mk4"][:, tt * 8:(tt + 1) * 8], axis=AX_X))
        wsel_done[ci] = vgap()
        t_xw[ci] = t_xg[ci]

    def do_ln2(lc):
        nonlocal t_out_count
        for tp_ in range(2):
            tin = {}
            tz2 = {}
            tmv = {}
            trc2 = {}
            for u in range(2):
                tt = tp_ * 2 + u
                tx1t = p.add("scalar", lambda e, lc=lc, tt=tt, u=u: e.dma_start(
                    out=x1T_sl(u), in_=agq_in[lc].ap()[0:D, tt * 128:(tt + 1) * 128],
                    transpose=True),
                    waits=[t_agq[lc], last_ln_dma[u], last_ln_read[u]], inc="ss", n=16)
                tmo = p.add("scalar", lambda e, lc=lc, tt=tt, u=u: e.dma_start(
                    out=moe_sl(u), in_=rs_out[lc].ap()[tt * 128:(tt + 1) * 128, :]),
                    waits=[t_rs_q[lc], last_ln_dma[u], last_ln_read[u]], inc="ss", n=16)
                tin[u] = (tx1t, tmo)
            for u in range(2):
                tz2[u] = p.add("vector", lambda e, u=u: e.tensor_add(
                    z2_sl(u), x1T_sl(u), moe_sl(u)),
                    waits=list(tin[u]), inc="sv")
                last_ln_read[u] = tz2[u]
            for u in range(2):
                p.add("vector", lambda e, u=u: e.bn_stats(
                    sm["st12"][:, u * 12:u * 12 + 6], z2_sl(u)[:, 0:512]),
                    waits=[tz2[u]])
                p.add("vector", lambda e, u=u: e.bn_stats(
                    sm["st12"][:, u * 12 + 6:u * 12 + 12], z2_sl(u)[:, 512:1024]))
            for u in range(2):
                tmv[u] = p.add("vector", lambda e, u=u: e.bn_aggr(
                    sm["mv2"][:, u * 2:u * 2 + 2], sm["st12"][:, u * 12:u * 12 + 12]),
                    inc="sv")
            spacer()
            spacer()
            tmvg = p.add("vector", lambda e: e.tensor_copy(sm["dmy"][:, :], sm["tmpc"][:, :]),
                         inc="sv")
            tmv = {0: tmvg, 1: tmvg}
            for u in range(2):
                tsq2 = p.add("scalar", lambda e, u=u: e.activation(
                    sm["rstd2"][:, u:u + 1], sm["mv2"][:, u * 2 + 1:u * 2 + 2],
                    AF.Sqrt, bias=sm["epsc"][:, 0:1]), waits=[tmv[u]], inc="sc")
                trc2[u] = tsq2
            for u in range(2):
                trc2[u] = p.add("vector", lambda e, u=u: e.reciprocal(
                    sm["rstd2"][:, u:u + 1], sm["rstd2"][:, u:u + 1]),
                    waits=[trc2[u]], inc="sv")
            spacer()
            spacer()
            touts = {}
            for u in range(2):
                p.add("vector", lambda e, u=u: e.tensor_scalar(
                    z2_sl(u), z2_sl(u), sm["mv2"][:, u * 2:u * 2 + 1],
                    sm["rstd2"][:, u:u + 1], OP.subtract, OP.mult),
                    waits=[trc2[u]])
            tog = vgap()
            for u in range(2):
                touts[u] = tog
            for u in range(2):
                tt = tp_ * 2 + u
                last_ln_dma[u] = p.add("scalar", lambda e, lc=lc, tt=tt, u=u: e.dma_start(
                    out=out_d[lc * TC + tt * 128:(lc * TC + (tt + 1) * 128), :],
                    in_=z2_sl(u)), waits=[touts[u]], inc="so", n=16)
                t_out_count += 16

    issue_loads(0)
    issue_loads(1)
    issue_wb_xw(0)
    for ci in range(NCI):
        lc, r = ci // CORES, ci % CORES
        xs = xbuf(ci)
        for fi in range(FI):
            rr_ = hro % 4
            hro += 1
            tok = None
            for ki in range(8):
                tok = p.add("tensor", lambda e, ps=h1_rot[rr_], l=aw1_w1(ki, fi),
                            rhs=xs(ki), s=(ki == 0), t=(ki == 7):
                            e.matmul(ps[:, :], l, rhs, start=s, stop=t),
                            waits=([t_xw[ci], tw1, h1_evac[rr_]] if ki == 0 else []),
                            inc=("st" if ki == 7 else None))
            h1_mm[rr_] = tok
            if fi % 2 == 0:
                t_ev = p.add("vector", lambda e, ps=h1_rot[rr_], o=h1s_sl(fi, 0, TB):
                             e.tensor_scalar(o, ps[:, :], 0.0, None, OP.max),
                             waits=[tok], inc="sv")
            else:
                t_ev = p.add("scalar", lambda e, ps=h1_rot[rr_], o=h1s_sl(fi, 0, TB):
                             e.activation(o, ps[:, :], AF.Relu),
                             waits=[tok], inc="sc")
            h1_evac[rr_] = t_ev
        h1_mm_last[ci] = h1_mm[(hro - 1) % 4]
        h1s_done_v = vgap()
        h1s_done_c = p.add("scalar", lambda e: e.activation(
            sm["sdmy"][:, :], sm["b_in"][:, :], AF.Copy), inc="sc")
        if ci + 2 < NCI:
            issue_loads(ci + 2)
        if ci + 1 < NCI:
            issue_wb_xw(ci + 1)
        for tt in range(4):
            pb = pyo % 2
            pyo += 1
            tok = None
            for half in range(2):
                for fk in range(FI):
                    is_first = (half == 0 and fk == 0)
                    is_last = (half == 1 and fk == FI - 1)
                    tok = p.add("tensor", lambda e, ps=pyb[pb], h=half,
                                l=h1s_sl(fk, tt * 128, 128),
                                rhs=w2_rhs(fk, half), s=(fk == 0), t=(fk == FI - 1):
                                e.matmul(ps[:, h * 512:(h + 1) * 512], l, rhs,
                                         start=s, stop=t),
                                waits=(([h1s_done_v, h1s_done_c, tw2, py_evac[pb]] +
                                        ([py_mm_init[pb]] if py_first[pb] else []))
                                       if is_first else []),
                                inc=("st" if is_last else None))
                    if is_first:
                        py_first[pb] = False
            spar = tt % 2
            par = ci % 2
            p.add("vector", lambda e, ps=pyb[pb], o=rsst_sl(spar), tt=tt, par=par:
                  e.tensor_scalar(o, ps[:, :], sm["wsel"][:, par * 4 + tt:par * 4 + tt + 1],
                                  None, OP.mult),
                  waits=[tok, last_rs_dma[spar], wsel_done[ci]])
            tev = vgap()
            py_evac[pb] = tev
            t_wsel_read[ci] = tev
            last_rs_dma[spar] = p.add(
                "sync", lambda e, q=lc, r=r, tt=tt, spar=spar:
                e.dma_start(out=rs_in[q].ap()[r * TC + tt * 128:
                                              r * TC + (tt + 1) * 128, :],
                            in_=rsst_sl(spar)),
                waits=[tev], inc="ss", n=16)
        if r == CORES - 1:
            t_rs_q[lc] = p.add("gpsimd", lambda e, q=lc: e.collective_compute(
                "ReduceScatter", OP.add, replica_groups=[list(range(CORES))],
                ins=[rs_in[q].ap().opt()], outs=[rs_out[q].ap().opt()]),
                waits=[last_rs_dma[0], last_rs_dma[1]], inc="scc")
        if r == 1 and lc > 0:
            do_ln2(lc - 1)
    do_ln2(NQ - 1)

    p.add("vector", lambda e: e.tensor_copy(sm["dmy"][:, :], sm["tmpc"][:, :]),
          waits=[("so", t_out_count)])

    with nc.Block() as block:
        p.emit(block)
    es.close()
    return nc


_NC_CACHE = None


def kernel(L0_attn_x, L0_attn_W_in, L0_attn_b_in, L0_attn_W_gx, L0_attn_b_gx,
           L0_attn_W_ga, L0_attn_b_ga, L0_attn_a_param, L0_attn_W_out,
           L0_attn_b_out, L0_ffn_gate_w, L0_ffn_w1, L0_ffn_w2):
    global _NC_CACHE
    f = np.float32
    x = np.asarray(L0_attn_x, f)
    xT = np.ascontiguousarray(x[0].T)

    def colmajor8(v):
        return np.ascontiguousarray(np.asarray(v, f).reshape(8, 128).T)

    W_in = np.ascontiguousarray(np.asarray(L0_attn_W_in, f))
    W_gx = np.ascontiguousarray(np.asarray(L0_attn_W_gx, f))
    W_ga = np.ascontiguousarray(np.asarray(L0_attn_W_ga, f))
    W_out = np.ascontiguousarray(np.asarray(L0_attn_W_out, f))
    gw = np.asarray(L0_ffn_gate_w, f)
    w1 = np.asarray(L0_ffn_w1, f)
    w2 = np.asarray(L0_ffn_w2, f)
    bins = colmajor8(L0_attn_b_in)
    bgxs = colmajor8(L0_attn_b_gx)
    bgas = colmajor8(L0_attn_b_ga)
    bouts = colmajor8(L0_attn_b_out)
    apars = colmajor8(L0_attn_a_param)
    idn = np.eye(128, dtype=f)
    gwp = np.ascontiguousarray(gw.T.reshape(8, 128, 8).transpose(1, 0, 2).reshape(128, 64))

    if _NC_CACHE is None:
        _NC_CACHE = build_nc()
    nc = _NC_CACHE

    in_maps = []
    for i in range(CORES):
        selc = np.zeros((128, 8), f)
        if i > 0:
            selc[:, i - 1] = 1.0
        ohm = np.tile((np.arange(8) == i).astype(f), (128, 1))
        smalls = np.concatenate([bins, bgxs, bgas, bouts, apars, selc, gwp, ohm,
                                 np.zeros((128, 8), f)], axis=1).astype(f)
        in_maps.append({
            "xT": np.ascontiguousarray(xT[:, i * TOK:(i + 1) * TOK]),
            "w_in": W_in, "w_gx": W_gx, "w_ga": W_ga, "w_out": W_out,
            "smalls_h": np.ascontiguousarray(smalls),
            "w1T": np.ascontiguousarray(w1[i].T),
            "w2T": np.ascontiguousarray(w2[i].T),
            "idn_h": idn,
        })

    res = run_bass_kernel_spmd(nc, in_maps, core_ids=list(range(CORES)))
    out = np.concatenate([res.results[i]["out"] for i in range(CORES)], axis=0)
    return out.reshape(1, S, D).astype(np.float32)


# version 12 (final)
# speedup vs baseline: 1.0807x; 1.0039x over previous
import numpy as np
from contextlib import ExitStack

import concourse.bass as bass
import concourse.mybir as mybir
from concourse.bass_utils import run_bass_kernel_spmd

F32 = mybir.dt.float32
BF16 = mybir.dt.bfloat16
AF = mybir.ActivationFunctionType
OP = mybir.AluOpType
AX_X = mybir.AxisListType.X

CORES = 8
D = 1024
S = 16384
TOK = S // CORES
TC = 512
NCHUNK = TOK // TC
FF = 4096
FI = FF // 128
TB = 512
NQ = 4
EPS = 1e-5


class P:

    ENGS = ("sync", "gpsimd", "vector", "scalar", "tensor")

    def __init__(self, nc, es):
        self.nc = nc
        self.ops = {e: [] for e in self.ENGS}
        self.sems = {}
        self.cnt = {}
        for name in ("sg", "ss", "sv", "sc", "st", "scc", "so"):
            self.sems[name] = es.enter_context(nc.semaphore(name))
            self.cnt[name] = 0

    def add(self, eng, fn, waits=(), inc=None, n=1):
        ws = tuple(w for w in waits if w is not None and w[1] > 0)
        self.ops[eng].append((fn, ws, inc, n))
        if inc is not None:
            self.cnt[inc] += n
            return (inc, self.cnt[inc])
        return None

    def tok(self, name):
        return (name, self.cnt[name])

    def _run(self, name, eng):
        seen = {}
        for fn, waits, inc, n in self.ops[name]:
            for (s, v) in waits:
                if seen.get(s, 0) < v:
                    eng.wait_ge(self.sems[s], v)
                    seen[s] = v
            instr = fn(eng)
            if inc is not None:
                instr.then_inc(self.sems[inc], n)

    def emit(self, block):
        runner = self._run

        @block.sync
        def _(e):
            runner("sync", e)

        @block.gpsimd
        def _(e):
            runner("gpsimd", e)

        @block.vector
        def _(e):
            runner("vector", e)

        @block.scalar
        def _(e):
            runner("scalar", e)

        @block.tensor
        def _(e):
            runner("tensor", e)


def build_nc():
    nc = bass.Bass("TRN2", target_bir_lowering=False, detect_race_conditions=False)

    xT = nc.dram_tensor("xT", [D, TOK], F32, kind="ExternalInput").ap()
    w_in = nc.dram_tensor("w_in", [D, D], F32, kind="ExternalInput").ap()
    w_gx = nc.dram_tensor("w_gx", [D, D], F32, kind="ExternalInput").ap()
    w_ga = nc.dram_tensor("w_ga", [D, D], F32, kind="ExternalInput").ap()
    w_out = nc.dram_tensor("w_out", [D, D], F32, kind="ExternalInput").ap()
    smalls_h = nc.dram_tensor("smalls_h", [128, 128], F32, kind="ExternalInput").ap()
    w1T = nc.dram_tensor("w1T", [D, FF], F32, kind="ExternalInput").ap()
    w2T = nc.dram_tensor("w2T", [FF, D], F32, kind="ExternalInput").ap()
    idn_h = nc.dram_tensor("idn_h", [128, 128], F32, kind="ExternalInput").ap()
    out_d = nc.dram_tensor("out", [TOK, D], F32, kind="ExternalOutput").ap()

    ag1_in = nc.dram_tensor("ag1_in", [2, D], F32)
    ag1_out = nc.dram_tensor("ag1_out", [2 * CORES, D], F32)
    agq_in = [nc.dram_tensor(f"agq_in_{q}", [D + 8, TC], BF16) for q in range(NQ)]
    agq_out = [nc.dram_tensor(f"agq_out_{q}", [(D + 8) * CORES, TC], BF16) for q in range(NQ)]
    rs_in = [nc.dram_tensor(f"rs_in_{q}", [CORES * TC, D], BF16) for q in range(NQ)]
    rs_out = [nc.dram_tensor(f"rs_out_{q}", [TC, D], BF16) for q in range(NQ)]

    es = ExitStack()

    AW1 = es.enter_context(nc.sbuf_tensor("AW1", [128, 32768], BF16))
    AW2 = es.enter_context(nc.sbuf_tensor("AW2", [128, 32768], BF16))
    AX = es.enter_context(nc.sbuf_tensor("AX", [128, 8192], BF16))
    AB = es.enter_context(nc.sbuf_tensor("AB", [128, 14336], BF16))
    AFr = es.enter_context(nc.sbuf_tensor("AF", [128, 8192], F32))
    AFb = AFr.ap().bitcast(BF16)
    ABf = AB.ap().bitcast(F32)

    SML = es.enter_context(nc.sbuf_tensor("SML", [128, 128], F32))
    sm = {}
    sm["b_in"] = SML[:, 0:8]
    sm["bgx_hf"] = SML[:, 8:16]
    sm["bga_hf"] = SML[:, 16:24]
    sm["b_out"] = SML[:, 24:32]
    sm["aptmp"] = SML[:, 32:40]
    sm["selc"] = SML[:, 40:48]
    sm["gwtf"] = SML[:, 48:112]
    sm["ohm"] = SML[:, 112:120]
    for nm, shp, dt in [
        ("msp_h", [128, 8], F32),
        ("hcar", [128, 8], F32), ("apcar", [128, 8], F32),
        ("cin", [128, 8], F32), ("tmpc", [128, 8], F32), ("dmy", [128, 8], F32),
        ("aesum", [128, 128], F32), ("cpre", [128, 64], F32),
        ("idn", [128, 128], F32),
        ("onesc", [128, 1], BF16), ("onesf", [1, 128], F32),
        ("glt4", [128, 32], F32), ("mk4", [128, 32], F32),
        ("ms4", [128, 32], F32), ("comb4", [128, 32], F32),
        ("st12", [128, 24], F32), ("mv2", [128, 4], F32), ("rstd2", [128, 2], F32),
        ("cmtok0", [128, 32], BF16), ("cmquar", [128, 256], BF16),
        ("wsel", [128, 8], F32), ("sdmy", [128, 8], F32),
        ("epsc", [128, 1], F32), ("onec", [128, 1], F32), ("spt", [128, 8], F32),
    ]:
        sm[nm] = es.enter_context(nc.sbuf_tensor(nm, shp, dt))
    sm["asum"] = sm["aesum"][:, 0:64]
    sm["esum"] = sm["aesum"][:, 64:128]
    sm["c24"] = sm["mk4"]
    sm["m14"] = sm["st12"][:, 0:4]
    sm["m24"] = sm["st12"][:, 4:8]
    sm["dd4"] = sm["st12"][:, 8:12]
    sm["th4"] = sm["st12"][:, 12:16]
    sm["g14"] = sm["st12"][:, 16:20]
    sm["g24"] = sm["st12"][:, 20:24]

    PA0 = es.enter_context(nc.psum_tensor("PA0", [128, 512], F32))
    PA1 = es.enter_context(nc.psum_tensor("PA1", [128, 512], F32))
    PA2 = es.enter_context(nc.psum_tensor("PA2", [128, 512], F32))
    PSP = es.enter_context(nc.psum_tensor("PSP", [128, 512], F32))
    PY0 = es.enter_context(nc.psum_tensor("PY0", [128, 1024], F32))
    PY1 = es.enter_context(nc.psum_tensor("PY1", [128, 1024], F32))

    p = P(nc, es)
    T0 = ("sv", 0)

    def aw1_w(base, ki, ii):
        c = base + ki * 1024 + ii * 128
        return AW1[:, c:c + 128]

    def aw1_w1(ki, fi):
        c = ki * 4096 + fi * 128
        return AW1[:, c:c + 128]

    def hloc_sl(ii, c):
        a = ii * TOK + c * TC
        return AW2[:, a:a + TC]

    def apb_sl(ii, c):
        a = 16384 + ii * TOK + c * TC
        return AW2[:, a:a + TC]

    def w2_rhs(fk, half):
        a = fk * 1024 + half * 512
        return AW2[:, a:a + 512]

    def x0b_sl(ki):
        return AX[:, ki * TC:(ki + 1) * TC]

    def xinb_sl(ii):
        return AX[:, 4096 + ii * TC:4096 + (ii + 1) * TC]

    def xinf_sl(ii):
        return AFr[:, ii * TC:(ii + 1) * TC]

    def afull_sl(ii):
        return AFr[:, 4096 + ii * TC:4096 + (ii + 1) * TC]

    def slot(k):
        return ABf[:, 6144 + k * 512:6144 + (k + 1) * 512]

    def ufull_sl(ii):
        return AB[:, ii * TC:(ii + 1) * TC]

    def xfull_sl(ii):
        return AB[:, 8192 + ii * 512:8192 + (ii + 1) * 512]

    def hfix_sl(ii):
        return AB[:, ii * TC:(ii + 1) * TC]

    def x1c_sl(ii):
        return AB[:, 4096 + ii * TC:4096 + (ii + 1) * TC]

    def zb_slot(k):
        return AB[:, 8192 + k * 512:8192 + (k + 1) * 512]

    def zsq_slot(k):
        return AB[:, 9216 + k * 512:9216 + (k + 1) * 512]

    def z_sl(ii):
        return AFr[:, ii * TC:(ii + 1) * TC]

    def rows(k):
        return AFr[0:1, 4096 + k * 512:4096 + (k + 1) * 512]

    def x1tmp():
        return AFr[:, 6144:6656]

    def glog_sl():
        return ABf[0:8, 5120:5632]

    def combT_sl(c):
        return AB[0:8, 12288 + c * TC:12288 + (c + 1) * TC]

    def xg_sl(ki):
        return AX[:, ki * TB:(ki + 1) * TB]

    def xw_sl(ki):
        return AX[:, 4096 + ki * TB:4096 + (ki + 1) * TB]

    def h1s_sl(fk, off, ln):
        a = fk * TB + off
        return AFb[:, a:a + ln]

    def x1T_sl(par):
        return AB[:, par * 1024:(par + 1) * 1024]

    def moe_sl(par):
        return AB[:, 2048 + par * 1024:2048 + (par + 1) * 1024]

    def rsst_sl(par):
        return AB[:, 4096 + par * 1024:4096 + (par + 1) * 1024]

    def wb_sl(par):
        return AB[:, 6144 + par * 512:6144 + (par + 1) * 512]

    def cm8_sl(par):
        return AB[0:8, 7168 + par * 512:7168 + (par + 1) * 512]

    def cm8_full(par):
        return AB[:, 7168 + par * 512:7168 + (par + 1) * 512]

    def z2_sl(par):
        return ABf[:, 4096 + par * 1024:4096 + (par + 1) * 1024]

    def spacer():
        p.add("vector", lambda e: e.tensor_copy(sm["dmy"][:, :], sm["tmpc"][:, :]))

    def vgap():
        spacer()
        return p.add("vector", lambda e: e.tensor_copy(sm["dmy"][:, :], sm["tmpc"][:, :]),
                     inc="sv")

    src3_0 = xT.rearrange("(ki p) t -> p ki t", p=128)[:, :, 0:TC]
    dst3_0 = AX[:, 0:4096].rearrange("p (ki t) -> p ki t", ki=8)
    t_x0_first = p.add("gpsimd", lambda e: e.dma_start(out=dst3_0, in_=src3_0),
                       inc="sg", n=16)
    t_w = {}
    for nm, src, base in [("in", w_in, 0), ("gx", w_gx, 8192),
                          ("ga", w_ga, 16384), ("out", w_out, 24576)]:
        src3 = src.rearrange("(ki p) c -> p ki c", p=128)
        dst3 = AW1[:, base:base + 8192].rearrange("p (ki c) -> p ki c", ki=8)
        t_w[nm] = p.add("gpsimd",
                        lambda e, o=dst3, i=src3: e.dma_start(out=o, in_=i),
                        inc="sg", n=16)


    t_small = p.add("sync", lambda e: e.dma_start(out=SML[:, :], in_=smalls_h),
                    inc="ss", n=16)
    t_idn = p.add("sync", lambda e: e.dma_start(out=sm["idn"][:, :], in_=idn_h),
                  waits=[t_small], inc="ss", n=16)
    t_gwtf = t_small
    small_loads = [t_small] * 6
    t_sml = t_small

    p.add("vector", lambda e: e.memset(sm["hcar"][:, :], 0.0))
    p.add("vector", lambda e: e.memset(sm["apcar"][:, :], 1.0))
    p.add("vector", lambda e: e.memset(sm["cin"][:, :], 0.0))
    p.add("vector", lambda e: e.memset(sm["tmpc"][:, :], 0.0))
    p.add("vector", lambda e: e.memset(sm["onesc"][:, :], 1.0))
    p.add("vector", lambda e: e.memset(sm["epsc"][:, :], EPS))
    p.add("vector", lambda e: e.memset(sm["onec"][:, :], 1.0))
    t_ones = p.add("vector", lambda e: e.memset(sm["onesf"][:, :], 1.0), inc="sv")
    def sgap(n=8):
        for _ in range(n):
            p.add("scalar", lambda e: e.activation(sm["sdmy"][:, :], sm["b_in"][:, :],
                                                   AF.Copy))
    ta1 = p.add("scalar", lambda e: e.activation(
        sm["spt"][:, :], sm["aptmp"][:, :], AF.Abs),
        waits=[small_loads[4]], inc="sc")
    t_r = p.add("scalar", lambda e: e.activation(
        sm["msp_h"][:, :], sm["aptmp"][:, :], AF.Relu), inc="sc")
    sgap()
    t_e = p.add("scalar", lambda e: e.activation(
        sm["aptmp"][:, :], sm["spt"][:, :], AF.Exp, scale=-1.0), inc="sc")
    sgap()
    t_l = p.add("scalar", lambda e: e.activation(
        sm["spt"][:, :], sm["aptmp"][:, :], AF.Ln, bias=sm["onec"][:, 0:1]),
        waits=[t_ones], inc="sc")
    sgap()
    tad = p.add("vector", lambda e: e.tensor_add(
        sm["msp_h"][:, :], sm["msp_h"][:, :], sm["spt"][:, :]),
        waits=[t_l, t_r], inc="sv")
    spacer()
    spacer()
    t_msp = p.add("vector", lambda e: e.tensor_scalar_mul(
        sm["msp_h"][:, :], sm["msp_h"][:, :], -4.0), waits=[tad], inc="sv")
    t_bgxh = p.add("vector", lambda e: e.tensor_scalar_mul(
        sm["bgx_hf"][:, :], sm["bgx_hf"][:, :], 0.5), waits=[small_loads[1]], inc="sv")
    t_bgah = p.add("vector", lambda e: e.tensor_scalar_mul(
        sm["bga_hf"][:, :], sm["bga_hf"][:, :], 0.5), waits=[small_loads[2]], inc="sv")

    PY0a, PY0b = PY0[:, 0:512], PY0[:, 512:1024]
    PY1a, PY1b = PY1[:, 0:512], PY1[:, 512:1024]
    rotors = [PA0[:, :], PA1[:, :], PA2[:, :], PSP[:, :], PY0a, PY0b, PY1a, PY1b]
    rot_evac = [T0] * 8
    rot_i = 0

    def next_rot():
        nonlocal rot_i
        r = rot_i % 8
        rot_i += 1
        return r

    t_x0 = [None] * NCHUNK
    t_xin_mm_last = T0
    slot_free = [T0, T0]
    t_u_done = {}
    t_x_done = {}
    t_sct = {}

    for c in range(NCHUNK):
        if c == 0:
            t_x0[0] = t_x0_first
        else:
            src3 = xT.rearrange("(ki p) t -> p ki t", p=128)[:, :, c * TC:(c + 1) * TC]
            dst3 = AX[:, 0:4096].rearrange("p (ki t) -> p ki t", ki=8)
            t_x0[c] = p.add("gpsimd", lambda e, o=dst3, i=src3: e.dma_start(out=o, in_=i),
                            waits=[t_xin_mm_last], inc="sg", n=16)

        t_evac_xin = []
        for ii in range(8):
            r = next_rot()
            tok = None
            for ki in range(8):
                tok = p.add("tensor", lambda e, ps=rotors[r], l=aw1_w(0, ki, ii),
                            rr=x0b_sl(ki), s=(ki == 0), t=(ki == 7):
                            e.matmul(ps, l, rr, start=s, stop=t),
                            waits=([t_x0[c], t_w["in"], rot_evac[r]] if ki == 0 else []),
                            inc=("st" if ki == 7 else None))
            p.add("vector", lambda e, ps=rotors[r], o=xinf_sl(ii), b=sm["b_in"][:, ii:ii + 1]:
                  e.tensor_scalar_add(o, ps, b), waits=[tok, t_sml])
            e2 = p.add("vector", lambda e, ps=rotors[r], o=xinb_sl(ii), b=sm["b_in"][:, ii:ii + 1]:
                       e.tensor_scalar_add(o, ps, b), inc="sv")
            rot_evac[r] = e2
            t_evac_xin.append(e2)
        t_evac_xin_g = vgap()
        t_xin_mm_last = p.tok("st")

        for ii in range(8):
            sA = slot(0)
            sB = slot(1)
            rA = next_rot()
            tokA = None
            for ki in range(8):
                tokA = p.add("tensor", lambda e, ps=rotors[rA], l=aw1_w(16384, ki, ii),
                             rr=xinb_sl(ki), s=(ki == 0), t=(ki == 7):
                             e.matmul(ps, l, rr, start=s, stop=t),
                             waits=([t_evac_xin_g, t_w["ga"], rot_evac[rA]] if ki == 0 else []),
                             inc=("st" if ki == 7 else None))
            rB = next_rot()
            tokB = None
            for ki in range(8):
                tokB = p.add("tensor", lambda e, ps=rotors[rB], l=aw1_w(8192, ki, ii),
                             rr=xinb_sl(ki), s=(ki == 0), t=(ki == 7):
                             e.matmul(ps, l, rr, start=s, stop=t),
                             waits=([t_w["gx"], rot_evac[rB]] if ki == 0 else []),
                             inc=("st" if ki == 7 else None))
            ta = p.add("scalar", lambda e, ps=rotors[rA], o=sB, b=sm["bga_hf"][:, ii:ii + 1]:
                       e.activation(o, ps, AF.Tanh, bias=b, scale=0.5),
                       waits=[tokA, t_bgah, slot_free[1]], inc="sc")
            rot_evac[rA] = ta
            tb = p.add("scalar", lambda e, ps=rotors[rB], o=sA, b=sm["bgx_hf"][:, ii:ii + 1]:
                       e.activation(o, ps, AF.Tanh, bias=b, scale=0.5),
                       waits=[tokB, t_bgxh, slot_free[0]], inc="sc")
            rot_evac[rB] = tb
            tl = p.add("vector", lambda e, s=sB, m=sm["msp_h"][:, ii:ii + 1]:
                       e.tensor_scalar(s, s, m, m, OP.mult, OP.add),
                       waits=[ta, t_msp], inc="sv")
            tu = p.add("vector", lambda e, s=sA, o=ufull_sl(ii), xi=xinf_sl(ii):
                       e.scalar_tensor_tensor(o, s, 1.0, xi, OP.add, OP.mult),
                       waits=[tb], inc="sv")
            t_u_done[(c, ii)] = tu
            p.add("scalar", lambda e, s=sB, o=afull_sl(ii):
                  e.activation(o, s, AF.Exp), waits=[tu], inc="sc")
            tq = p.add("scalar", lambda e, s=sB, o=sA:
                       e.activation(o, s, AF.Exp, scale=2.0), waits=[tu], inc="sc")
            slot_free[1] = tq
            p.add("vector", lambda e, s=sA: e.tensor_scalar(
                s, s, -1.0, 1.0, OP.mult, OP.add), waits=[tq])
            p.add("vector", lambda e, s=sA: e.tensor_scalar_max(s, s, 0.0))
            tx3 = p.add("vector", lambda e, s=sA, o=xfull_sl(ii): e.tensor_copy(o, s),
                        inc="sv")
            slot_free[0] = tx3
            t_x_done[(c, ii)] = tx3

        for ii in range(8):
            t_sct[(c, ii)] = p.add("scalar", lambda e, s=xfull_sl(ii):
                                   e.activation(s, s, AF.Sqrt, scale=0.25),
                                   waits=[t_x_done[(c, ii)]], inc="sc")

        for i0 in range(0, 8, 2):
            prs = [i0, i0 + 1]
            bsl = {ii: AFr[:, (ii % 2) * 512:(ii % 2) * 512 + 512] for ii in prs}
            hsl = {ii: AFr[:, 1024 + (ii % 2) * 512:1024 + (ii % 2) * 512 + 512] for ii in prs}
            asl = {ii: AFr[:, 2048 + (ii % 2) * 512:2048 + (ii % 2) * 512 + 512] for ii in prs}
            for ii in prs:
                p.add("vector", lambda e, o=bsl[ii], u=ufull_sl(ii), s=xfull_sl(ii):
                      e.tensor_mul(o, u, s),
                      waits=[t_sct[(c, ii)], t_u_done[(c, ii)]])
            for ii in prs:
                p.add("vector", lambda e, o=hsl[ii], a=afull_sl(ii), b=bsl[ii],
                      i0_=sm["hcar"][:, ii:ii + 1]:
                      e.tensor_tensor_scan(o, a, b, i0_, OP.mult, OP.add))
            for ii in prs:
                p.add("vector", lambda e, o=asl[ii], a=afull_sl(ii),
                      i0_=sm["apcar"][:, ii:ii + 1]:
                      e.tensor_tensor_scan(o, a, a, i0_, OP.mult, OP.bypass))
            for ii in prs:
                p.add("vector", lambda e, o=hloc_sl(ii, c), i=hsl[ii]: e.tensor_copy(o, i))
            for ii in prs:
                p.add("vector", lambda e, o=apb_sl(ii, c), i=asl[ii]: e.tensor_copy(o, i))
            for ii in prs:
                p.add("vector", lambda e, o=sm["hcar"][:, ii:ii + 1], i=hsl[ii][:, 511:512]:
                      e.tensor_copy(o, i))
            for ii in prs:
                p.add("vector", lambda e, o=sm["apcar"][:, ii:ii + 1], i=asl[ii][:, 511:512]:
                      e.tensor_copy(o, i), inc=("sv" if ii == i0 + 1 else None))

    t_gates_done = p.tok("st")
    t_lastscan = vgap()
    ag1i3 = ag1_in.ap().rearrange("r (p ii) -> r p ii", ii=8)
    tg1 = p.add("gpsimd", lambda e: e.dma_start(out=ag1i3[0], in_=sm["apcar"][:, :]),
                waits=[t_lastscan], inc="sg", n=16)
    tg2 = p.add("gpsimd", lambda e: e.dma_start(out=ag1i3[1], in_=sm["hcar"][:, :]),
                inc="sg", n=16)
    t_ag1 = p.add("gpsimd", lambda e: e.collective_compute(
        "AllGather", OP.bypass, replica_groups=[list(range(CORES))],
        ins=[ag1_in.ap().opt()], outs=[ag1_out.ap().opt()]),
        waits=[tg1, tg2], inc="scc")
    ag1o3 = ag1_out.ap().rearrange("(r two) (p ii) -> two p r ii", two=2, ii=8)
    tga = p.add("gpsimd", lambda e: e.dma_start(
        out=sm["asum"].rearrange("p (r ii) -> p r ii", ii=8), in_=ag1o3[0]),
        waits=[t_ag1], inc="sg", n=16)
    tge = p.add("gpsimd", lambda e: e.dma_start(
        out=sm["esum"].rearrange("p (r ii) -> p r ii", ii=8), in_=ag1o3[1]),
        waits=[tga], inc="sg", n=16)
    p.add("vector", lambda e: e.tensor_copy(sm["cpre"][:, 0:8], sm["esum"][:, 0:8]),
          waits=[tge])
    for k in range(1, 7):
        spacer()
        p.add("vector", lambda e, k=k: e.tensor_mul(
            sm["tmpc"][:, :], sm["asum"][:, k * 8:(k + 1) * 8],
            sm["cpre"][:, (k - 1) * 8:k * 8]))
        spacer()
        p.add("vector", lambda e, k=k: e.tensor_add(
            sm["cpre"][:, k * 8:(k + 1) * 8], sm["tmpc"][:, :],
            sm["esum"][:, k * 8:(k + 1) * 8]))
    for k in range(7):
        spacer()
        spacer()
        p.add("vector", lambda e, k=k: e.scalar_tensor_tensor(
            sm["cin"][:, :], sm["cpre"][:, k * 8:(k + 1) * 8],
            sm["selc"][:, k:k + 1], sm["cin"][:, :], OP.mult, OP.add))
    spacer()
    spacer()
    t_cin = p.add("vector", lambda e: e.tensor_copy(sm["dmy"][:, :], sm["cin"][:, :]),
                  inc="sv")

    rot3 = [PY0a, PY0b, PY1a, PY1b]
    rot3_evac = [rot_evac[4], rot_evac[5], rot_evac[6], rot_evac[7]]
    psp_free = rot_evac[3]
    pa2_free = rot_evac[2]
    pa0_free = rot_evac[0]
    pa1_free = rot_evac[1]
    t_glog_mm = T0
    t_agq = [None] * NQ
    t_x1dma_prev = ("sg", 0)
    t_zev = T0
    t_wout_mm = T0
    t_hf = []
    tct = T0

    for c in range(NCHUNK):
        src3 = xT.rearrange("(ki p) t -> p ki t", p=128)[:, :, c * TC:(c + 1) * TC]
        dst3 = AX[:, 0:4096].rearrange("p (ki t) -> p ki t", ki=8)
        t_x0r = p.add("gpsimd", lambda e, o=dst3, i=src3: e.dma_start(out=o, in_=i),
                      waits=[t_lastscan] + ([t_zev] if c > 0 else []), inc="sg", n=16)
        t_hf = []
        for ii in range(8):
            t_hf.append(p.add("vector", lambda e, o=hfix_sl(ii), a=apb_sl(ii, c),
                              s=sm["cin"][:, ii:ii + 1], h=hloc_sl(ii, c):
                              e.scalar_tensor_tensor(o, a, s, h, OP.mult, OP.add),
                              waits=[t_cin] + ([t_wout_mm] if c > 0 else []), inc="sv"))
        t_hf_g = vgap()
        t_zs = []
        for ii in range(8):
            r = ii % 4
            tok = None
            for ki in range(8):
                tok = p.add("tensor", lambda e, ps=rot3[r], l=aw1_w(24576, ki, ii),
                            rr=hfix_sl(ki), s=(ki == 0), t=(ki == 7):
                            e.matmul(ps, l, rr, start=s, stop=t),
                            waits=([t_hf_g, t_w["out"], rot3_evac[r]] if ki == 0 else []),
                            inc=("st" if ki == 7 else None))
            tz = p.add("vector", lambda e, ps=rot3[r], o=z_sl(ii),
                       b=sm["b_out"][:, ii:ii + 1], x0=x0b_sl(ii):
                       e.scalar_tensor_tensor(o, ps, b, x0, OP.add, OP.add),
                       waits=[tok, t_x0r], inc="sv")
            rot3_evac[r] = tz
            t_zs.append(tz)
        t_wout_mm = p.tok("st")
        t_zev = t_zs[-1]
        t_slot_mm = [T0, T0]
        tok_sum = tok_sq = None
        for ii in range(8):
            k = ii % 2
            p.add("vector", lambda e, o=zb_slot(k), z=z_sl(ii): e.tensor_copy(o, z),
                  waits=[t_zs[ii], t_slot_mm[k]])
            p.add("vector", lambda e, o=zsq_slot(k), z=z_sl(ii): e.tensor_mul(o, z, z))
            tzb = tzq = vgap()
            tok_sum = p.add("tensor", lambda e, rr=zb_slot(k), s=(ii == 0), t=(ii == 7):
                            e.matmul(PA0[0:1, :], sm["onesc"][:, :], rr, start=s, stop=t),
                            waits=[tzb, t_ones] + ([pa0_free] if ii == 0 else []),
                            inc="st")
            tok_sq = p.add("tensor", lambda e, rr=zsq_slot(k), s=(ii == 0), t=(ii == 7):
                           e.matmul(PA1[0:1, :], sm["onesc"][:, :], rr, start=s, stop=t),
                           waits=[tzq] + ([pa1_free] if ii == 0 else []),
                           inc="st")
            t_slot_mm[k] = tok_sq
        trm = p.add("vector", lambda e: e.tensor_scalar_mul(rows(0), PA0[0:1, :], 1.0 / D),
                    waits=[tok_sum], inc="sv")
        p.add("vector", lambda e: e.tensor_scalar_mul(rows(1), PA1[0:1, :], 1.0 / D),
              waits=[tok_sq])
        spacer()
        p.add("vector", lambda e: e.tensor_mul(rows(2), rows(0), rows(0)))
        spacer()
        p.add("vector", lambda e: e.tensor_sub(rows(1), rows(1), rows(2)))
        spacer()
        spacer()
        tvar = p.add("vector", lambda e: e.tensor_copy(sm["dmy"][:, :], sm["tmpc"][:, :]),
                     inc="sv")
        tsq = p.add("scalar", lambda e: e.activation(rows(2), rows(1), AF.Sqrt, bias=sm["epsc"][0:1, 0:1]),
                    waits=[tvar], inc="sc")
        trc = p.add("vector", lambda e: e.reciprocal(rows(3), rows(2)), waits=[tsq],
                    inc="sv")
        spacer()
        spacer()
        p.add("vector", lambda e: e.tensor_mul(rows(2), rows(0), rows(3)))
        tmr = vgap()
        tbr = p.add("tensor", lambda e: e.matmul(PA1[:, :], sm["onesf"][:, :], rows(3),
                                                 start=True, stop=True),
                    waits=[tmr, tvar, t_ones], inc="st")
        tbm = p.add("tensor", lambda e: e.matmul(PA0[:, :], sm["onesf"][:, :], rows(2),
                                                 start=True, stop=True),
                    waits=[tmr, trm], inc="st")
        tx1last = None
        for ii in range(8):
            p.add("vector", lambda e, z=z_sl(ii): e.tensor_mul(x1tmp(), z, PA1[:, :]),
                  waits=([tbr, tbm, t_glog_mm, t_x1dma_prev] if ii == 0 else []))
            p.add("vector", lambda e, z=z_sl(ii): e.tensor_sub(z, x1tmp(), PA0[:, :]))
            p.add("vector", lambda e, o=x1c_sl(ii), z=z_sl(ii): e.tensor_copy(o, z))
            tx1last = None
        tx1last = vgap()
        pa0_free = tx1last
        pa1_free = tx1last
        t_x1dma_prev = p.add("gpsimd", lambda e, c=c: e.dma_start(
            out=agq_in[c].ap()[0:D, :].rearrange("(ki p) t -> p ki t", p=128),
            in_=AB[:, 4096:8192].rearrange("p (ki t) -> p ki t", ki=8)),
            waits=[tx1last], inc="sg", n=16)
        tok = None
        for ki in range(8):
            tok = p.add("tensor", lambda e, l=sm["gwtf"][:, ki * 8:(ki + 1) * 8],
                        rr=z_sl(ki), s=(ki == 0), t=(ki == 7):
                        e.matmul(PSP[0:8, :], l, rr, start=s, stop=t),
                        waits=([tx1last, t_gwtf, psp_free] if ki == 0 else []),
                        inc=("st" if ki == 7 else None))
        t_glog_mm = tok
        tgl = p.add("vector", lambda e: e.tensor_copy(glog_sl(), PSP[0:8, :]),
                    waits=[tok], inc="sv")
        psp_free = tgl
        tprev = pa2_free
        for j in range(4):
            ttr = p.add("tensor", lambda e, j=j: e.transpose(
                PA2[:, 0:8], glog_sl()[0:8, j * 128:(j + 1) * 128],
                sm["idn"][0:8, 0:8]),
                waits=[tgl, t_idn, tprev], inc="st")
            tprev = p.add("vector", lambda e, j=j: e.tensor_copy(
                sm["glt4"][:, j * 8:(j + 1) * 8], PA2[:, 0:8]),
                waits=[ttr], inc="sv")
        pa2_free = tprev
        for j in range(4):
            p.add("vector", lambda e, j=j: e.reduce_max(
                sm["m14"][:, j:j + 1], sm["glt4"][:, j * 8:(j + 1) * 8], axis=AX_X))
        spacer()
        spacer()
        for j in range(4):
            p.add("vector", lambda e, j=j: e.tensor_scalar(
                sm["mk4"][:, j * 8:(j + 1) * 8], sm["glt4"][:, j * 8:(j + 1) * 8],
                sm["m14"][:, j:j + 1], 1.0, OP.is_equal, OP.mult))
        spacer()
        for j in range(4):
            p.add("vector", lambda e, j=j: e.scalar_tensor_tensor(
                sm["ms4"][:, j * 8:(j + 1) * 8], sm["mk4"][:, j * 8:(j + 1) * 8],
                -1e30, sm["glt4"][:, j * 8:(j + 1) * 8], OP.mult, OP.add))
        spacer()
        for j in range(4):
            p.add("vector", lambda e, j=j: e.reduce_max(
                sm["m24"][:, j:j + 1], sm["ms4"][:, j * 8:(j + 1) * 8], axis=AX_X))
        spacer()
        spacer()
        p.add("vector", lambda e: e.tensor_sub(sm["dd4"], sm["m24"],
                                               sm["m14"]))
        spacer()
        spacer()
        tdd = p.add("vector", lambda e: e.tensor_copy(sm["dmy"][:, :], sm["tmpc"][:, :]),
                    inc="sv")
        tth = p.add("scalar", lambda e: e.activation(sm["th4"], sm["dd4"],
                                                     AF.Tanh, scale=0.5),
                    waits=[tdd], inc="sc")
        p.add("vector", lambda e: e.tensor_scalar(sm["g24"], sm["th4"],
                                                  0.5, 0.5, OP.mult, OP.add),
              waits=[tth])
        p.add("vector", lambda e: e.tensor_scalar(sm["g14"], sm["th4"],
                                                  -0.5, 0.5, OP.mult, OP.add))
        spacer()
        spacer()
        for j in range(4):
            p.add("vector", lambda e, j=j: e.tensor_scalar(
                sm["comb4"][:, j * 8:(j + 1) * 8], sm["glt4"][:, j * 8:(j + 1) * 8],
                sm["m14"][:, j:j + 1], sm["g14"][:, j:j + 1], OP.is_equal, OP.mult))
        spacer()
        for j in range(4):
            p.add("vector", lambda e, j=j: e.tensor_scalar(
                sm["c24"][:, j * 8:(j + 1) * 8], sm["ms4"][:, j * 8:(j + 1) * 8],
                sm["m24"][:, j:j + 1], sm["g24"][:, j:j + 1], OP.is_equal, OP.mult))
        spacer()
        p.add("vector", lambda e: e.tensor_add(sm["comb4"][:, :], sm["comb4"][:, :],
                                               sm["c24"][:, :]))
        spacer()
        spacer()
        tcb = p.add("vector", lambda e: e.tensor_copy(sm["cmtok0"][:, :], sm["comb4"][:, :]),
                    inc="sv")
        tct = vgap()
        tcdma = p.add("gpsimd", lambda e, c=c: e.dma_start(
            out=agq_in[c].ap()[D:D + 8, :].rearrange(
                "r t -> (r t)").rearrange("(p c) -> p c", p=128),
            in_=sm["cmtok0"][:, :]),
            waits=[tct], inc="sg", n=16)
        t_agq[c] = p.add("gpsimd", lambda e, c=c: e.collective_compute(
            "AllGather", OP.bypass, replica_groups=[list(range(CORES))],
            ins=[agq_in[c].ap().opt()], outs=[agq_out[c].ap().opt()]),
            waits=[tcdma, t_x1dma_prev], inc="scc")

    t_tensorA_done = p.tok("st")

    tw1a = p.add("gpsimd", lambda e: e.dma_start(
        out=AW1[:, 0:24576].rearrange("p (ki f) -> p ki f", ki=6),
        in_=w1T[0:768, :].rearrange("(ki p) f -> p ki f", p=128)),
        waits=[t_gates_done], inc="sg", n=16)
    tw1 = p.add("gpsimd", lambda e: e.dma_start(
        out=AW1[:, 24576:32768].rearrange("p (ki f) -> p ki f", ki=2),
        in_=w1T[768:1024, :].rearrange("(ki p) f -> p ki f", p=128)),
        waits=[t_tensorA_done, tw1a], inc="sg", n=16)
    tw2 = p.add("gpsimd", lambda e: e.dma_start(
        out=AW2.ap().rearrange("p (fk d) -> p fk d", fk=32),
        in_=w2T.rearrange("(fk p) d -> p fk d", p=128)),
        waits=[t_hf[-1]], inc="sg", n=16)

    h1_rot = [PA0, PA1, PA2, PSP]
    h1_evac = [pa0_free, pa1_free, pa2_free, psp_free]
    h1_mm = [T0, T0, T0, T0]
    pyb = [PY0, PY1]
    py_evac = [rot3_evac[1], rot3_evac[3]]
    py_mm_init = [rot3_evac[0], rot3_evac[2]]
    py_first = [True, True]
    t_xg = {}
    t_c8 = {}
    t_xw = {}
    wsel_done = {}
    t_wsel_read = {}
    t_cmtok_free = tw1
    h1_mm_last = {-1: T0, -2: T0}
    last_rs_dma = [("sg", 0), ("sg", 0)]
    t_out_count = 0
    hro = 0
    pyo = 0
    last_ln_dma = [("so", 0), ("so", 0)]
    last_ln_read = [T0, T0]
    t_rs_q = {}
    NCI = NQ * CORES

    def xbuf(ci):
        par = ci % 2
        return lambda ki: AX[:, par * 4096 + ki * TB:par * 4096 + (ki + 1) * TB]

    def issue_loads(ci):
        lc, r = ci // CORES, ci % CORES
        par = ci % 2
        xs = xbuf(ci)
        t_xg[ci] = p.add("sync", lambda e, r=r, lc=lc, par=par: e.dma_start(
            out=AX[:, par * 4096:par * 4096 + 4096].rearrange("p (ki t) -> p ki t", ki=8),
            in_=agq_out[lc].ap()[r * (D + 8):r * (D + 8) + D, :].rearrange(
                "(ki p) t -> p ki t", p=128)),
            waits=[t_agq[lc], t_zev, h1_mm_last.get(ci - 2, T0)] +
                  ([t_tensorA_done] if ci < 2 else []), inc="ss", n=16)
        if r == 0:
            src3 = agq_out[lc].ap().rearrange(
                "(rk row) t -> rk row t", row=D + 8)[:, D:D + 8, :].rearrange(
                "rk row t -> rk (row t)").rearrange(
                "rk (p c) -> p rk c", p=128)
            t_c8[ci] = p.add("sync", lambda e, s3=src3: e.dma_start(
                out=sm["cmquar"].ap().rearrange("p (rk c) -> p rk c", c=32),
                in_=s3),
                waits=[t_agq[lc], wsel_done.get(ci - 1, T0)] +
                      ([t_cmtok_free] if ci < 2 else []), inc="ss", n=16)
        else:
            t_c8[ci] = t_c8[ci - 1]

    def issue_wb_xw(ci):
        par = ci % 2
        r_ = ci % CORES
        for tt in range(4):
            p.add("vector", lambda e, tt=tt, r_=r_: e.tensor_mul(
                sm["mk4"][:, tt * 8:(tt + 1) * 8],
                sm["cmquar"][:, r_ * 32 + tt * 8:r_ * 32 + (tt + 1) * 8],
                sm["ohm"]),
                waits=([t_c8[ci], t_wsel_read.get(ci - 2, T0)] if tt == 0 else []))
        for tt in range(4):
            p.add("vector", lambda e, tt=tt, par=par: e.reduce_sum(
                sm["wsel"][:, par * 4 + tt:par * 4 + tt + 1],
                sm["mk4"][:, tt * 8:(tt + 1) * 8], axis=AX_X))
        wsel_done[ci] = vgap()
        t_xw[ci] = t_xg[ci]

    def do_ln2(lc, halves=(0, 1)):
        nonlocal t_out_count
        for tp_ in halves:
            tin = {}
            tz2 = {}
            tmv = {}
            trc2 = {}
            for u in range(2):
                tt = tp_ * 2 + u
                tx1t = p.add("scalar", lambda e, lc=lc, tt=tt, u=u: e.dma_start(
                    out=x1T_sl(u), in_=agq_in[lc].ap()[0:D, tt * 128:(tt + 1) * 128],
                    transpose=True),
                    waits=[t_agq[lc], last_ln_dma[u], last_ln_read[u]], inc="ss", n=16)
                tmo = p.add("scalar", lambda e, lc=lc, tt=tt, u=u: e.dma_start(
                    out=moe_sl(u), in_=rs_out[lc].ap()[tt * 128:(tt + 1) * 128, :]),
                    waits=[t_rs_q[lc], last_ln_dma[u], last_ln_read[u]], inc="ss", n=16)
                tin[u] = (tx1t, tmo)
            for u in range(2):
                tz2[u] = p.add("vector", lambda e, u=u: e.tensor_add(
                    z2_sl(u), x1T_sl(u), moe_sl(u)),
                    waits=list(tin[u]), inc="sv")
                last_ln_read[u] = tz2[u]
            for u in range(2):
                p.add("vector", lambda e, u=u: e.bn_stats(
                    sm["st12"][:, u * 12:u * 12 + 6], z2_sl(u)[:, 0:512]),
                    waits=[tz2[u]])
                p.add("vector", lambda e, u=u: e.bn_stats(
                    sm["st12"][:, u * 12 + 6:u * 12 + 12], z2_sl(u)[:, 512:1024]))
            for u in range(2):
                tmv[u] = p.add("vector", lambda e, u=u: e.bn_aggr(
                    sm["mv2"][:, u * 2:u * 2 + 2], sm["st12"][:, u * 12:u * 12 + 12]),
                    inc="sv")
            spacer()
            spacer()
            tmvg = p.add("vector", lambda e: e.tensor_copy(sm["dmy"][:, :], sm["tmpc"][:, :]),
                         inc="sv")
            tmv = {0: tmvg, 1: tmvg}
            for u in range(2):
                tsq2 = p.add("scalar", lambda e, u=u: e.activation(
                    sm["rstd2"][:, u:u + 1], sm["mv2"][:, u * 2 + 1:u * 2 + 2],
                    AF.Sqrt, bias=sm["epsc"][:, 0:1]), waits=[tmv[u]], inc="sc")
                trc2[u] = tsq2
            for u in range(2):
                trc2[u] = p.add("vector", lambda e, u=u: e.reciprocal(
                    sm["rstd2"][:, u:u + 1], sm["rstd2"][:, u:u + 1]),
                    waits=[trc2[u]], inc="sv")
            spacer()
            spacer()
            touts = {}
            for u in range(2):
                p.add("vector", lambda e, u=u: e.tensor_scalar(
                    z2_sl(u), z2_sl(u), sm["mv2"][:, u * 2:u * 2 + 1],
                    sm["rstd2"][:, u:u + 1], OP.subtract, OP.mult),
                    waits=[trc2[u]])
            tog = vgap()
            for u in range(2):
                touts[u] = tog
            for u in range(2):
                tt = tp_ * 2 + u
                last_ln_dma[u] = p.add("scalar", lambda e, lc=lc, tt=tt, u=u: e.dma_start(
                    out=out_d[lc * TC + tt * 128:(lc * TC + (tt + 1) * 128), :],
                    in_=z2_sl(u)), waits=[touts[u]], inc="so", n=16)
                t_out_count += 16

    issue_loads(0)
    issue_loads(1)
    issue_wb_xw(0)
    for ci in range(NCI):
        lc, r = ci // CORES, ci % CORES
        xs = xbuf(ci)
        for fi in range(FI):
            rr_ = hro % 4
            hro += 1
            tok = None
            for ki in range(8):
                tok = p.add("tensor", lambda e, ps=h1_rot[rr_], l=aw1_w1(ki, fi),
                            rhs=xs(ki), s=(ki == 0), t=(ki == 7):
                            e.matmul(ps[:, :], l, rhs, start=s, stop=t),
                            waits=([t_xw[ci], tw1, h1_evac[rr_]] if ki == 0 else []),
                            inc=("st" if ki == 7 else None))
            h1_mm[rr_] = tok
            if fi % 2 == 0:
                t_ev = p.add("vector", lambda e, ps=h1_rot[rr_], o=h1s_sl(fi, 0, TB):
                             e.tensor_scalar(o, ps[:, :], 0.0, None, OP.max),
                             waits=[tok], inc="sv")
            else:
                t_ev = p.add("scalar", lambda e, ps=h1_rot[rr_], o=h1s_sl(fi, 0, TB):
                             e.activation(o, ps[:, :], AF.Relu),
                             waits=[tok], inc="sc")
            h1_evac[rr_] = t_ev
        h1_mm_last[ci] = h1_mm[(hro - 1) % 4]
        h1s_done_v = vgap()
        h1s_done_c = p.add("scalar", lambda e: e.activation(
            sm["sdmy"][:, :], sm["b_in"][:, :], AF.Copy), inc="sc")
        if ci + 2 < NCI:
            issue_loads(ci + 2)
        if ci + 1 < NCI:
            issue_wb_xw(ci + 1)
        for tt in range(4):
            pb = pyo % 2
            pyo += 1
            tok = None
            for half in range(2):
                for fk in range(FI):
                    is_first = (half == 0 and fk == 0)
                    is_last = (half == 1 and fk == FI - 1)
                    tok = p.add("tensor", lambda e, ps=pyb[pb], h=half,
                                l=h1s_sl(fk, tt * 128, 128),
                                rhs=w2_rhs(fk, half), s=(fk == 0), t=(fk == FI - 1):
                                e.matmul(ps[:, h * 512:(h + 1) * 512], l, rhs,
                                         start=s, stop=t),
                                waits=(([h1s_done_v, h1s_done_c, tw2, py_evac[pb]] +
                                        ([py_mm_init[pb]] if py_first[pb] else []))
                                       if is_first else []),
                                inc=("st" if is_last else None))
                    if is_first:
                        py_first[pb] = False
            spar = tt % 2
            par = ci % 2
            p.add("vector", lambda e, ps=pyb[pb], o=rsst_sl(spar), tt=tt, par=par:
                  e.tensor_scalar(o, ps[:, :], sm["wsel"][:, par * 4 + tt:par * 4 + tt + 1],
                                  None, OP.mult),
                  waits=[tok, last_rs_dma[spar], wsel_done[ci]])
            tev = vgap()
            py_evac[pb] = tev
            t_wsel_read[ci] = tev
            last_rs_dma[spar] = p.add(
                "gpsimd", lambda e, q=lc, r=r, tt=tt, spar=spar:
                e.dma_start(out=rs_in[q].ap()[r * TC + tt * 128:
                                              r * TC + (tt + 1) * 128, :],
                            in_=rsst_sl(spar)),
                waits=[tev], inc="sg", n=16)
        if r == CORES - 1:
            t_rs_q[lc] = p.add("gpsimd", lambda e, q=lc: e.collective_compute(
                "ReduceScatter", OP.add, replica_groups=[list(range(CORES))],
                ins=[rs_in[q].ap().opt()], outs=[rs_out[q].ap().opt()]),
                waits=[last_rs_dma[0], last_rs_dma[1]], inc="scc")
        if r == 1 and lc > 0:
            do_ln2(lc - 1, (0,))
        if r == 3 and lc > 0:
            do_ln2(lc - 1, (1,))
    do_ln2(NQ - 1)

    p.add("vector", lambda e: e.tensor_copy(sm["dmy"][:, :], sm["tmpc"][:, :]),
          waits=[("so", t_out_count)])

    with nc.Block() as block:
        p.emit(block)
    es.close()
    return nc


_NC_CACHE = None


def kernel(L0_attn_x, L0_attn_W_in, L0_attn_b_in, L0_attn_W_gx, L0_attn_b_gx,
           L0_attn_W_ga, L0_attn_b_ga, L0_attn_a_param, L0_attn_W_out,
           L0_attn_b_out, L0_ffn_gate_w, L0_ffn_w1, L0_ffn_w2):
    global _NC_CACHE
    f = np.float32
    x = np.asarray(L0_attn_x, f)
    xT = np.ascontiguousarray(x[0].T)

    def colmajor8(v):
        return np.ascontiguousarray(np.asarray(v, f).reshape(8, 128).T)

    W_in = np.ascontiguousarray(np.asarray(L0_attn_W_in, f))
    W_gx = np.ascontiguousarray(np.asarray(L0_attn_W_gx, f))
    W_ga = np.ascontiguousarray(np.asarray(L0_attn_W_ga, f))
    W_out = np.ascontiguousarray(np.asarray(L0_attn_W_out, f))
    gw = np.asarray(L0_ffn_gate_w, f)
    w1 = np.asarray(L0_ffn_w1, f)
    w2 = np.asarray(L0_ffn_w2, f)
    bins = colmajor8(L0_attn_b_in)
    bgxs = colmajor8(L0_attn_b_gx)
    bgas = colmajor8(L0_attn_b_ga)
    bouts = colmajor8(L0_attn_b_out)
    apars = colmajor8(L0_attn_a_param)
    idn = np.eye(128, dtype=f)
    gwp = np.ascontiguousarray(gw.T.reshape(8, 128, 8).transpose(1, 0, 2).reshape(128, 64))

    if _NC_CACHE is None:
        _NC_CACHE = build_nc()
    nc = _NC_CACHE

    in_maps = []
    for i in range(CORES):
        selc = np.zeros((128, 8), f)
        if i > 0:
            selc[:, i - 1] = 1.0
        ohm = np.tile((np.arange(8) == i).astype(f), (128, 1))
        smalls = np.concatenate([bins, bgxs, bgas, bouts, apars, selc, gwp, ohm,
                                 np.zeros((128, 8), f)], axis=1).astype(f)
        in_maps.append({
            "xT": np.ascontiguousarray(xT[:, i * TOK:(i + 1) * TOK]),
            "w_in": W_in, "w_gx": W_gx, "w_ga": W_ga, "w_out": W_out,
            "smalls_h": np.ascontiguousarray(smalls),
            "w1T": np.ascontiguousarray(w1[i].T),
            "w2T": np.ascontiguousarray(w2[i].T),
            "idn_h": idn,
        })

    res = run_bass_kernel_spmd(nc, in_maps, core_ids=list(range(CORES)))
    out = np.concatenate([res.results[i]["out"] for i in range(CORES)], axis=0)
    return out.reshape(1, S, D).astype(np.float32)
